# version 01
# baseline (speedup 1.0000x reference)
import numpy as np
from contextlib import ExitStack
import concourse.bass as bass
import concourse.mybir as mybir
from concourse.bass_utils import run_bass_kernel_spmd

F32 = mybir.dt.float32
BF16 = mybir.dt.bfloat16
I32 = mybir.dt.int32
AF = mybir.ActivationFunctionType
ALU = mybir.AluOpType

NB, L, D = 2, 4096, 4096
NCORES = 8
CH = 512
T = 1024
NKT = D // 128
DIM, DFF, AD = 512, 2048, 4
NH, HD = 32, 128
EPS = 1e-5
f32 = np.float32


class Sem:
    def __init__(self, ctx, nc, name):
        self.h = ctx.enter_context(nc.semaphore(name))
        self.n = 0

    def inc(self, instr, k=1):
        instr.then_inc(self.h, k)
        self.n += k
        return self.n

    def wait(self, eng, n=None):
        eng.wait_ge(self.h, self.n if n is None else n)


class SV:

    def __init__(self, p):
        self._p = p

    def __getattr__(self, name):
        p = self._p
        f = getattr(p.ve, name)

        def g(*a, **k):
            ins = f(*a, **k)
            n = p.s_ve.inc(ins)
            p.ve.wait_ge(p.s_ve.h, n)
            return ins
        return g


class P:
    def __init__(self, nc, ctx):
        self.nc = nc
        self.ctx = ctx
        self.te, self.ve, self.se = nc.tensor, nc.vector, nc.scalar
        self.gp, self.sy = nc.gpsimd, nc.sync
        self.s_ld = Sem(ctx, nc, "s_ld")
        self.s_gd = Sem(ctx, nc, "s_gd")
        self.s_te = Sem(ctx, nc, "s_te")
        self.s_ve = Sem(ctx, nc, "s_ve")
        self.s_se = Sem(ctx, nc, "s_se")
        self.s_gp = Sem(ctx, nc, "s_gp")
        self.s_cc = Sem(ctx, nc, "s_cc")
        self.s_x = [Sem(ctx, nc, "s_x0"), Sem(ctx, nc, "s_x1")]
        self.s_sto = [Sem(ctx, nc, "s_sto0"), Sem(ctx, nc, "s_sto1")]
        self.s_wt = Sem(ctx, nc, "s_wt")
        self.s_const = Sem(ctx, nc, "s_const")
        self.s_z = Sem(ctx, nc, "s_z")
        self.s_zg = Sem(ctx, nc, "s_zg")
        self.s_ug = [Sem(ctx, nc, "s_ug0"), Sem(ctx, nc, "s_ug1")]
        self.s_kv = [Sem(ctx, nc, "s_kv0"), Sem(ctx, nc, "s_kv1")]
        self.s_q = [Sem(ctx, nc, "s_q0"), Sem(ctx, nc, "s_q1")]
        self.s_bar = Sem(ctx, nc, "s_bar")
        self.engs = [self.te, self.ve, self.se, self.gp, self.sy]

    def sbuf(self, name, shape, dt):
        return self.ctx.enter_context(self.nc.sbuf_tensor(name, shape, dt))

    def psum(self, name, shape, dt=F32):
        return self.ctx.enter_context(self.nc.psum_tensor(name, shape, dt))


def emit_stats(p, src_dram, w, double_ln, keep_cast=False):
    v, se, te, sy = p.ve, p.se, p.te, p.sy
    xin, xbs, sqs = w["xin2"], w["xbs2"], w["sqs2"]
    pb = w["pb"]
    p.s_se.wait(te)
    p.s_ve.wait(te)
    ld_at = [p.s_x[0].n, p.s_x[1].n]
    ve_at = [p.s_ve.n] * 2
    se_at = [p.s_se.n] * 2
    te_done = [p.s_te.n] * 2
    for kt in range(NKT):
        s = kt % 2
        p.s_ve.wait(sy, ve_at[s])
        ld_at[s] = p.s_x[s].inc(
            sy.dma_start(out=xin[s][:], in_=src_dram[kt * 128:(kt + 1) * 128, :]), 16)
        p.s_x[s].wait(v, ld_at[s])
        p.s_te.wait(v, te_done[s])
        xcast = w["mirror"][:, kt * T:(kt + 1) * T] if keep_cast else xbs[s][:]
        ve_at[s] = p.s_ve.inc(v.tensor_copy(xcast, xin[s][:]))
        p.s_ve.wait(se, ve_at[s])
        p.s_te.wait(se, te_done[s])
        se_at[s] = p.s_se.inc(se.activation(sqs[s][:], xcast, AF.Square))
        p.s_se.wait(te, se_at[s])
        st, sp = kt == 0, kt == NKT - 1
        te.matmul(pb[0][0:1, :], w["ones_bf"][:], xcast[:, 0:512],
                  start=st, stop=sp, skip_group_check=True)
        te.matmul(pb[1][0:1, :], w["ones_bf"][:], xcast[:, 512:1024],
                  start=st, stop=sp, skip_group_check=True)
        te.matmul(pb[2][0:1, :], w["ones_bf"][:], sqs[s][:, 0:512],
                  start=st, stop=sp, skip_group_check=True)
        ins = te.matmul(pb[3][0:1, :], w["ones_bf"][:], sqs[s][:, 512:1024],
                        start=st, stop=sp, skip_group_check=True)
        te_done[s] = p.s_te.inc(ins)
    mu, ex2, var, tmp = w["r0"], w["r1"], w["r2"], w["r3"]
    p.s_te.wait(v)
    inv_d = float(1.0 / D)
    p.s_ve.inc(v.tensor_scalar_mul(mu[0:1, 0:512], pb[0][0:1, :], inv_d))
    v.tensor_scalar_mul(mu[0:1, 512:1024], pb[1][0:1, :], inv_d)
    v.tensor_scalar_mul(ex2[0:1, 0:512], pb[2][0:1, :], inv_d)
    v.tensor_scalar_mul(ex2[0:1, 512:1024], pb[3][0:1, :], inv_d)
    v.tensor_mul(tmp[0:1, :], mu[0:1, :], mu[0:1, :])
    v.tensor_sub(var[0:1, :], ex2[0:1, :], tmp[0:1, :])
    v.tensor_scalar_add(tmp[0:1, :], var[0:1, :], float(EPS))
    p.s_ve.inc(v.reciprocal(tmp[0:1, :], tmp[0:1, :]))
    p.s_ve.wait(se)
    p.s_se.inc(se.activation(w["rstd"][0:1, :], tmp[0:1, :], AF.Sqrt))
    scale = w["rstd"]
    if double_ln:
        p.s_se.wait(v)
        v.tensor_mul(ex2[0:1, :], scale[0:1, :], scale[0:1, :])
        v.tensor_mul(ex2[0:1, :], ex2[0:1, :], var[0:1, :])
        v.tensor_scalar_add(ex2[0:1, :], ex2[0:1, :], float(EPS))
        p.s_ve.inc(v.reciprocal(ex2[0:1, :], ex2[0:1, :]))
        p.s_ve.wait(se)
        p.s_se.inc(se.activation(ex2[0:1, :], ex2[0:1, :], AF.Sqrt))
        p.s_se.wait(v)
        p.s_ve.inc(v.tensor_mul(w["sc_row"][0:1, :], scale[0:1, :], ex2[0:1, :]))
        scale = w["sc_row"]
    bcast_rows(p, [(mu, w["mu_b"]), (scale, w["sc_b"])], w)


def bcast_rows(p, pairs, w):
    v, te = p.ve, p.te
    bank = w["pb"][4]
    for row, dstb in pairs:
        for th in range(2):
            p.s_se.wait(te)
            p.s_ve.wait(te)
            p.s_te.inc(te.matmul(bank[:], w["ones_row"][:],
                                 row[0:1, th * 512:(th + 1) * 512],
                                 start=True, stop=True, skip_group_check=True))
            p.s_te.wait(v)
            p.s_ve.inc(v.tensor_copy(dstb[:, th * 512:(th + 1) * 512], bank[:]))


def emit_mirror(p, src_dram, w, gcols, bcols):
    v, sy = p.ve, p.sy
    xin, t1 = w["xin2"], w["t1"]
    mirror = w["mirror"]
    ld_at = [p.s_x[0].n, p.s_x[1].n]
    ve_at = [p.s_ve.n] * 2
    for kt in range(NKT):
        s = kt % 2
        p.s_ve.wait(sy, ve_at[s])
        ld_at[s] = p.s_x[s].inc(
            sy.dma_start(out=xin[s][:], in_=src_dram[kt * 128:(kt + 1) * 128, :]), 16)
        p.s_x[s].wait(v, ld_at[s])
        dst = mirror[:, kt * T:(kt + 1) * T]
        v.tensor_sub(t1[:], xin[s][:], w["mu_b"][:])
        if gcols is not None:
            v.tensor_mul(t1[:], t1[:], w["sc_b"][:])
            ve_at[s] = p.s_ve.inc(
                v.tensor_scalar(dst, t1[:], gcols[:, kt:kt + 1],
                                bcols[:, kt:kt + 1], op0=ALU.mult, op1=ALU.add))
        else:
            ve_at[s] = p.s_ve.inc(v.tensor_mul(dst, t1[:], w["sc_b"][:]))


_FFN_N = [0]


def emit_ffn(p, x_dram, out_dram, wf, w, last_sem_holder):
    v, se, te, gp, sy = p.ve, p.se, p.te, p.gp, p.sy
    nc = p.nc
    mm = te.matmul
    fctx = p.ctx
    sctx = ExitStack()
    _FFN_N[0] += 1
    sfx = str(_FFN_N[0])
    w["mirror"] = sctx.enter_context(nc.sbuf_tensor("ffn_mirror" + sfx, [128, NKT * T], BF16))
    w["ustage"] = sctx.enter_context(nc.sbuf_tensor("ffn_ustage" + sfx, [128, T], BF16))
    w["yg"] = sctx.enter_context(nc.sbuf_tensor("ffn_yg" + sfx, [128, T], F32))
    emit_stats(p, x_dram, w, double_ln=True)
    bcast_rows(p, [(w["rstd"], w["sc2_b"])], w)
    emit_mirror(p, x_dram, w, wf["gcols"], wf["bcols"])
    mirror = w["mirror"]
    u_dram = wf["u_dram"]
    ustage = w["ustage"]
    p.s_ve.wait(te)
    banks = [w["pb"][0], w["pb"][1], w["pb"][2], w["pb"][3]]
    ve_evac = [0, 0]
    gd_at = [p.s_ug[0].n, p.s_ug[1].n]
    te_at = [0, 0, 0, 0]
    for pi in range(64):
        kt, half = pi // 2, pi % 2
        slot = pi % 2
        for th in range(2):
            b = banks[2 * slot + th]
            p.s_ve.wait(te, ve_evac[slot])
            rhs = mirror[half * 64:half * 64 + 64,
                         kt * T + th * 512:kt * T + (th + 1) * 512]
            lhsA = wf["A_sb"][half * 64:half * 64 + 64, :]
            te_at[2 * slot + th] = p.s_te.inc(
                mm(b[0:64, :], lhsA, rhs, start=True, stop=True,
                   skip_group_check=True))
        p.s_te.wait(v, te_at[2 * slot + 1])
        p.s_ug[slot].wait(v, gd_at[slot])
        v.tensor_copy(ustage[slot * 64:slot * 64 + 64, 0:512], banks[2 * slot][0:64, :])
        ve_evac[slot] = p.s_ve.inc(
            v.tensor_copy(ustage[slot * 64:slot * 64 + 64, 512:1024],
                          banks[2 * slot + 1][0:64, :]))
        p.s_ve.wait(gp, ve_evac[slot])
        gd_at[slot] = p.s_ug[slot].inc(
            gp.dma_start(out=u_dram[pi * 64:(pi + 1) * 64, :],
                         in_=ustage[slot * 64:slot * 64 + 64, :]), 16)
    u2v = u_dram.rearrange("(pp bb) t -> bb pp t", bb=64)
    x2v = x_dram.rearrange("(q bb) t -> bb q t", bb=64)
    o2v = out_dram.rearrange("(q bb) t -> bb q t", bb=64)
    yg = w["yg"]
    xp2 = w["xin2"][0]
    p.s_ve.wait(te)
    sto_tot = (p.s_ug[0].n, p.s_ug[1].n)
    u2_gd = [p.s_gd.n]
    te2_at = [p.s_te.n]
    se2_at = [p.s_se.n] * 4
    ve2_at = [p.s_ve.n]
    st2_at = [p.s_sto[0].n]
    for pair in range(32):
        b0 = 2 * pair
        p.s_ug[0].wait(gp, sto_tot[0])
        p.s_ug[1].wait(gp, sto_tot[1])
        p.s_te.wait(gp, te2_at[0])
        p.s_gd.inc(gp.dma_start(out=ustage[0:64, :], in_=u2v[b0]), 16)
        u2_gd[0] = p.s_gd.inc(gp.dma_start(out=ustage[64:128, :], in_=u2v[b0 + 1]), 16)
        p.s_ve.wait(sy, ve2_at[0])
        p.s_sto[0].wait(sy, st2_at[0])
        p.s_x[0].inc(sy.dma_start(out=xp2[0:64, :], in_=x2v[b0]), 16)
        x_ld = p.s_x[0].inc(sy.dma_start(out=xp2[64:128, :], in_=x2v[b0 + 1]), 16)
        p.s_gd.wait(te, u2_gd[0])
        for sub in range(2):
            for th in range(2):
                b = banks[2 * sub + th]
                p.s_se.wait(te, se2_at[2 * sub + th])
                ins = mm(b[0:64, :], wf["B_sb"][sub * 64:sub * 64 + 64, :],
                         ustage[sub * 64:sub * 64 + 64, th * 512:(th + 1) * 512],
                         start=True, stop=True, skip_group_check=True)
            te2_at[0] = p.s_te.inc(ins)
            p.s_te.wait(se, te2_at[0])
            p.s_ve.wait(se, ve2_at[0])
            se.activation(yg[sub * 64:sub * 64 + 64, 0:512],
                          banks[2 * sub][0:64, :], AF.Gelu)
            se2_at[2 * sub] = se2_at[2 * sub + 1] = p.s_se.inc(
                se.activation(yg[sub * 64:sub * 64 + 64, 512:1024],
                              banks[2 * sub + 1][0:64, :], AF.Gelu))
        p.s_se.wait(v)
        p.s_x[0].wait(v, x_ld)
        t1 = w["t1"]
        v.tensor_sub(t1[:], xp2[:], w["mu_b"][:])
        v.tensor_mul(t1[:], t1[:], w["sc2_b"][:])
        v.tensor_add(t1[:], t1[:], xp2[:])
        ve2_at[0] = p.s_ve.inc(v.scalar_tensor_tensor(
            xp2[:], yg[:], wf["biasp_cols"][:, pair:pair + 1], t1[:],
            op0=ALU.add, op1=ALU.add))
        p.s_ve.wait(sy, ve2_at[0])
        p.s_sto[0].inc(sy.dma_start(out=o2v[b0], in_=xp2[0:64, :]), 16)
        st2_at[0] = p.s_sto[0].inc(sy.dma_start(out=o2v[b0 + 1], in_=xp2[64:128, :]), 16)
    last_sem_holder["sem"] = p.s_sto[0]
    last_sem_holder["val"] = p.s_sto[0].n
    barrier(p)
    sctx.close()


def sync_all(p, eng):
    for s in (p.s_ld, p.s_gd, p.s_te, p.s_ve, p.s_se, p.s_gp, p.s_cc,
              p.s_x[0], p.s_x[1], p.s_sto[0], p.s_sto[1], p.s_wt,
              p.s_const, p.s_z, p.s_zg, p.s_ug[0], p.s_ug[1],
              p.s_kv[0], p.s_kv[1], p.s_q[0], p.s_q[1]):
        s.wait(eng)


def barrier(p):
    for eng in p.engs:
        sync_all(p, eng)
        p.s_bar.inc(eng.nop())
    for eng in p.engs:
        p.s_bar.wait(eng)


def small_stats(p, w, src_sb, col0, gcols_pair, dst_sb, dcol0):
    v, se, te = p.ve, p.se, p.te
    pb = w["pb"]
    sync_all(p, te)
    sync_all(p, se)
    sqt = w["t1"]
    for ct in range(4):
        sl = src_sb[:, col0 + ct * T:col0 + (ct + 1) * T]
        p.s_ve.wait(se)
        p.s_te.wait(se)
        p.s_se.inc(se.activation(sqt[:], sl, AF.Square))
        p.s_se.wait(te)
        st, sp = ct == 0, ct == 3
        for th in range(2):
            te.matmul(pb[th][0:1, :], w["ones_f32"][:], sl[:, th * 512:(th + 1) * 512],
                      start=st, stop=sp, skip_group_check=True)
            ins = te.matmul(pb[2 + th][0:1, :], w["ones_f32"][:],
                            sqt[:, th * 512:(th + 1) * 512],
                            start=st, stop=sp, skip_group_check=True)
        p.s_te.inc(ins)
    mu, ex2, var, tmp = w["r0"], w["r1"], w["r2"], w["r3"]
    p.s_te.wait(v)
    inv_d = float(1.0 / DIM)
    v.tensor_scalar_mul(mu[0:1, 0:512], pb[0][0:1, :], inv_d)
    v.tensor_scalar_mul(mu[0:1, 512:1024], pb[1][0:1, :], inv_d)
    v.tensor_scalar_mul(ex2[0:1, 0:512], pb[2][0:1, :], inv_d)
    v.tensor_scalar_mul(ex2[0:1, 512:1024], pb[3][0:1, :], inv_d)
    v.tensor_mul(tmp[0:1, :], mu[0:1, :], mu[0:1, :])
    v.tensor_sub(var[0:1, :], ex2[0:1, :], tmp[0:1, :])
    v.tensor_scalar_add(tmp[0:1, :], var[0:1, :], float(EPS))
    p.s_ve.inc(v.reciprocal(tmp[0:1, :], tmp[0:1, :]))
    p.s_ve.wait(se)
    p.s_se.inc(se.activation(w["rstd"][0:1, :], tmp[0:1, :], AF.Sqrt))
    bcast_rows(p, [(mu, w["mu_b"]), (w["rstd"], w["sc_b"])], w)
    gc, bc = gcols_pair
    for ct in range(4):
        sl = src_sb[:, col0 + ct * T:col0 + (ct + 1) * T]
        dst = dst_sb[:, dcol0 + ct * T:dcol0 + (ct + 1) * T]
        v.tensor_sub(w["t1"][:], sl, w["mu_b"][:])
        v.tensor_mul(w["t1"][:], w["t1"][:], w["sc_b"][:])
        p.s_ve.inc(v.tensor_scalar(dst, w["t1"][:], gc[:, ct:ct + 1],
                                   bc[:, ct:ct + 1], op0=ALU.mult, op1=ALU.add))


def emit_s1(p, x_dram, out_dram, ws, w, dram, dbg):
    nc = p.nc
    v, se, te, gp, sy = p.ve, p.se, p.te, p.gp, p.sy
    mm = te.matmul

    octx = ExitStack()
    h_sb = octx.enter_context(nc.sbuf_tensor("h_sb", [128, 4 * T], F32))
    y2_sb = octx.enter_context(nc.sbuf_tensor("y2_sb", [128, 4 * T], BF16))
    wbar = octx.enter_context(nc.sbuf_tensor("wbar", [128, 4], F32))

    sctx = ExitStack()
    w["mirror"] = sctx.enter_context(nc.sbuf_tensor("s1_mirror", [128, NKT * T], BF16))
    win_sb = sctx.enter_context(nc.sbuf_tensor("win_sb", [128, NKT * DIM], BF16))
    emit_stats(p, x_dram, w, double_ln=False, keep_cast=True)
    mirror = w["mirror"]
    if True:
        for q in range(4):
            p.s_wt.inc(gp.dma_start(
                out=win_sb[:, q * 8 * DIM:(q + 1) * 8 * DIM],
                in_=dram["win_img"][:, q * 8 * DIM:(q + 1) * 8 * DIM]), 16)
        p.s_wt.wait(te)
        p.s_ve.wait(te)
        for ct in range(4):
            p.s_ve.wait(te)
            for kt in range(NKT):
                ins = mm(w["pb"][5][0:128, 0:1],
                         win_sb[:, kt * DIM + ct * 128:kt * DIM + (ct + 1) * 128],
                         w["ones_bf"][:], start=kt == 0, stop=kt == NKT - 1,
                         skip_group_check=True)
            p.s_te.inc(ins)
            p.s_te.wait(v)
            wb_at = p.s_ve.inc(v.tensor_copy(wbar[:, ct:ct + 1], w["pb"][5][0:128, 0:1]))
            v.wait_ge(p.s_ve.h, wb_at)
        p.s_ve.wait(v)
        p.s_ve.inc(v.tensor_mul(w["r1"][0:1, :], w["r0"][0:1, :], w["rstd"][0:1, :]))
        bcast_rows(p, [(w["r1"], w["mu_b"]), (w["rstd"], w["sc_b"])], w)

        embank = [w["pb"][0], w["pb"][1]]
        ve_emb = [p.s_ve.n] * 2
        for ct in range(4):
            for th in range(2):
                slot = (2 * ct + th) % 2
                bank = embank[slot]
                p.s_ve.wait(te, ve_emb[slot])
                for kt in range(NKT):
                    ins = mm(bank[:, :],
                             win_sb[:, kt * DIM + ct * 128:kt * DIM + (ct + 1) * 128],
                             mirror[:, kt * T + th * 512:kt * T + (th + 1) * 512],
                             start=kt == 0, stop=kt == NKT - 1, skip_group_check=True)
                p.s_te.inc(ins)
                p.s_te.wait(v)
                dst = h_sb[:, ct * T + th * 512:ct * T + (th + 1) * 512]
                mu_sl = w["mu_b"][:, th * 512:(th + 1) * 512]
                sc_sl = w["sc_b"][:, th * 512:(th + 1) * 512]
                v.scalar_tensor_tensor(w["t1"][:, 0:512], mu_sl, wbar[:, ct:ct + 1],
                                       bank[:, :], op0=ALU.mult, op1=ALU.subtract)
                v.scalar_tensor_tensor(w["t1"][:, 512:1024], w["t1"][:, 0:512], -1.0,
                                       sc_sl, op0=ALU.mult, op1=ALU.mult)
                ve_emb[slot] = p.s_ve.inc(
                    v.tensor_scalar(dst, w["t1"][:, 512:1024],
                                    ws["bin_cols"][:, ct:ct + 1], None, op0=ALU.add))
        if dbg is not None and "h_embed" in dbg:
            emit_dump(p, dbg["h_embed"], h_sb, 4)
    barrier(p)
    sctx.close()

    sctx = ExitStack()
    y_sb = sctx.enter_context(nc.sbuf_tensor("y_sb", [128, 4 * T], F32))
    u_sb = sctx.enter_context(nc.sbuf_tensor("u_sb", [128, 16 * 512], BF16))
    w1_sb = sctx.enter_context(nc.sbuf_tensor("w1_sb", [128, 4 * DFF], BF16))
    w2_sb = sctx.enter_context(nc.sbuf_tensor("w2_sb", [128, 16 * 512], BF16))
    vre = sctx.enter_context(nc.sbuf_tensor("vre", [128, T], F32))
    vim = sctx.enter_context(nc.sbuf_tensor("vim", [128, T], F32))
    cost = sctx.enter_context(nc.sbuf_tensor("cost", [128, 4 * T], F32))
    sint = sctx.enter_context(nc.sbuf_tensor("sint", [128, 4 * T], F32))
    xmre = sctx.enter_context(nc.sbuf_tensor("xmre", [128, T], F32))
    xmim = sctx.enter_context(nc.sbuf_tensor("xmim", [128, T], F32))
    rtile4 = sctx.enter_context(nc.sbuf_tensor("rtile4", [128, 4 * 512], F32))
    sc1 = sctx.enter_context(nc.sbuf_tensor("sc1", [128, 72], F32))
    zsb = sctx.enter_context(nc.sbuf_tensor("zsb", [128, 16], F32))
    zg_sb = sctx.enter_context(nc.sbuf_tensor("zg_sb", [16, 512], F32))
    zt_sb = sctx.enter_context(nc.sbuf_tensor("zt_sb", [128, 64], F32))
    if True:
        for lay in range(AD):
            sync_all(p, gp)
            for q in range(4):
                p.s_wt.inc(gp.dma_start(
                    out=w1_sb[:, q * DFF:(q + 1) * DFF],
                    in_=dram["w1_img"][lay, :, q * DFF:(q + 1) * DFF]), 16)
                p.s_wt.inc(gp.dma_start(
                    out=w2_sb[:, q * 2048:(q + 1) * 2048],
                    in_=dram["w2_img"][lay, :, q * 2048:(q + 1) * 2048]), 16)
            small_stats(p, w, h_sb, 0,
                        (ws["ln1_w"][lay], ws["ln1_b"][lay]), y_sb, 0)
            sync_all(p, se)
            for ct in range(4):
                p.s_se.inc(se.activation(y_sb[:, ct * T:(ct + 1) * T],
                                         y_sb[:, ct * T:(ct + 1) * T], AF.Silu))
            if dbg is not None and f"y{lay}" in dbg:
                emit_dump(p, dbg[f"y{lay}"], y_sb, 4)
            emit_sconv(p, w, ws, lay, y_sb, h_sb, vre, vim, cost, sint, xmre,
                       xmim, rtile4, sc1, zsb, zg_sb, zt_sb, dram, dbg)
            if dbg is not None and f"h_mid{lay}" in dbg:
                emit_dump(p, dbg[f"h_mid{lay}"], h_sb, 4)
            small_stats(p, w, h_sb, 0,
                        (ws["ln2_w"][lay], ws["ln2_b"][lay]), y2_sb, 0)
            for th in range(2):
                sync_all(p, te)
                for ft in range(16):
                    bank = w["pb"][ft % 2]
                    for ct in range(4):
                        ins = mm(bank[:, :],
                                 w1_sb[:, ct * DFF + ft * 128:ct * DFF + (ft + 1) * 128],
                                 y2_sb[:, ct * T + th * 512:ct * T + (th + 1) * 512],
                                 start=ct == 0, stop=ct == 3, skip_group_check=True)
                    p.s_te.inc(ins)
                    p.s_te.wait(se)
                    p.s_se.inc(se.activation(
                        u_sb[:, ft * 512:(ft + 1) * 512],
                        bank[:, :], AF.Silu, bias=ws["b1_cols"][lay][:, ft:ft + 1]))
                    if ft >= 1:
                        p.s_se.wait(te)
                sync_all(p, te)
                for ct in range(4):
                    bank = w["pb"][2 + ct % 2]
                    for ft in range(16):
                        ins = mm(bank[:, :],
                                 w2_sb[:, ft * 512 + ct * 128:ft * 512 + (ct + 1) * 128],
                                 u_sb[:, ft * 512:(ft + 1) * 512],
                                 start=ft == 0, stop=ft == 15, skip_group_check=True)
                    p.s_te.inc(ins)
                    p.s_te.wait(v)
                    dst = h_sb[:, ct * T + th * 512:ct * T + (th + 1) * 512]
                    p.s_ve.inc(v.scalar_tensor_tensor(
                        dst, bank[:, :], ws["b2_cols"][lay][:, ct:ct + 1], dst,
                        op0=ALU.add, op1=ALU.add))
                    p.s_ve.wait(te)
            if dbg is not None and f"h{lay}" in dbg:
                emit_dump(p, dbg[f"h{lay}"], h_sb, 4)

    small_stats(p, w, h_sb, 0, (ws["lnl_w"], ws["lnl_b"]), y2_sb, 0)
    barrier(p)
    sctx.close()

    sctx = ExitStack()
    wout_sb = sctx.enter_context(nc.sbuf_tensor("wout_sb", [128, 4 * D], BF16))
    if True:
        sync_all(p, gp)
        for q in range(4):
            p.s_wt.inc(gp.dma_start(
                out=wout_sb[:, q * 4096:(q + 1) * 4096],
                in_=dram["wout_img"][:, q * 4096:(q + 1) * 4096]), 16)
        p.s_wt.wait(te)
        xin = w["xin2"]
        obank = [w["pb"][0], w["pb"][1]]
        ld_at = [p.s_x[0].n, p.s_x[1].n]
        st_at = [p.s_sto[0].n, p.s_sto[1].n]
        ve_at = [p.s_ve.n] * 2
        last_stt = [p.s_ve.n]
        for ot in range(NKT):
            s = ot % 2
            p.s_ve.wait(sy, ve_at[s])
            p.s_sto[s].wait(sy, st_at[s])
            ld_at[s] = p.s_x[s].inc(
                sy.dma_start(out=xin[s][:], in_=x_dram[ot * 128:(ot + 1) * 128, :]), 16)
            for th in range(2):
                bank = obank[th]
                p.s_ve.wait(te, last_stt[0])
                for ct in range(4):
                    ins = mm(bank[:, :],
                             wout_sb[:, ct * 4096 + ot * 128:ct * 4096 + (ot + 1) * 128],
                             y2_sb[:, ct * T + th * 512:ct * T + (th + 1) * 512],
                             start=ct == 0, stop=ct == 3, skip_group_check=True)
                p.s_te.inc(ins)
            p.s_te.wait(v)
            p.s_x[s].wait(v, ld_at[s])
            v.scalar_tensor_tensor(xin[s][:, 0:512], obank[0][:, :],
                                   ws["bout_cols"][:, ot:ot + 1], xin[s][:, 0:512],
                                   op0=ALU.add, op1=ALU.add)
            ve_at[s] = p.s_ve.inc(v.scalar_tensor_tensor(
                xin[s][:, 512:1024], obank[1][:, :], ws["bout_cols"][:, ot:ot + 1],
                xin[s][:, 512:1024], op0=ALU.add, op1=ALU.add))
            last_stt[0] = ve_at[s]
            p.s_ve.wait(sy, ve_at[s])
            st_at[s] = p.s_sto[s].inc(
                sy.dma_start(out=out_dram[ot * 128:(ot + 1) * 128, :],
                             in_=xin[s][:]), 16)
    barrier(p)
    sctx.close()
    octx.close()


def _cmul_sq(v, c, s, ta, tb):
    v.tensor_mul(ta, c, c)
    v.tensor_mul(tb, s, s)
    v.tensor_mul(s, c, s)
    v.tensor_scalar_mul(s, s, 2.0)
    v.tensor_sub(c, ta, tb)


def emit_sconv(p, w, ws, lay, y_sb, h_sb, vre, vim, cost, sint, xmre, xmim,
               rtile4, sc1, zsb, zg_sb, zt_sb, dram, dbg):
    nc = p.nc
    v, se, te, gp, sy = p.ve, p.se, p.te, p.gp, p.sy
    sv = SV(p)
    X = mybir.AxisListType.X
    C = lambda i: sc1[:, i:i + 1]
    MAG, RR, CTH, STH = 12, 13, 14, 15
    EC, ES, TA, TB, NEG = 16, 17, 18, 19, 20
    CG0, SG0, CG1, SG1 = 21, 22, 23, 24
    APC, APS = 25, 33
    VEC, VES = 41, 50
    T8 = 59
    phz_re, phz_im = ws["phz_re"][lay], ws["phz_im"][lay]
    lc_re, lc_im = ws["lc_re"][lay], ws["lc_im"][lay]
    scl = ws["scl"][lay]

    for ct in range(4):
        sync_all(p, v)
        sync_all(p, se)
        sv.tensor_mul(C(TA), phz_re[:, ct:ct + 1], phz_re[:, ct:ct + 1])
        sv.tensor_mul(C(TB), phz_im[:, ct:ct + 1], phz_im[:, ct:ct + 1])
        sv.tensor_add(C(MAG), C(TA), C(TB))
        sv.tensor_scalar_add(C(MAG), C(MAG), 1e-30)
        p.s_ve.wait(se)
        sq_at = p.s_se.inc(se.activation(C(MAG), C(MAG), AF.Sqrt))
        p.s_se.wait(se, sq_at)
        p.s_se.inc(se.activation(C(RR), C(MAG), AF.Tanh))
        p.s_se.wait(v)
        sv.reciprocal(C(TA), C(MAG))
        sv.tensor_mul(C(CTH), phz_re[:, ct:ct + 1], C(TA))
        sv.tensor_mul(C(STH), phz_im[:, ct:ct + 1], C(TA))
        sv.tensor_mul(C(TA), C(CTH), lc_re[:, ct:ct + 1])
        sv.tensor_mul(C(TB), C(STH), lc_im[:, ct:ct + 1])
        sv.tensor_sub(C(4 + ct), C(TA), C(TB))
        sv.tensor_mul(C(TA), C(STH), lc_re[:, ct:ct + 1])
        sv.tensor_mul(C(TB), C(CTH), lc_im[:, ct:ct + 1])
        sv.tensor_add(C(8 + ct), C(TA), C(TB))
        sv.tensor_scalar_mul(C(8 + ct), C(8 + ct), -1.0)
        sv.tensor_copy(C(ct), C(RR))
        sv.tensor_copy(C(EC), C(CTH))
        sv.tensor_copy(C(ES), C(STH))
        for _ in range(9):
            sv.tensor_mul(C(ct), C(ct), C(ct))
            _cmul_sq(sv, C(EC), C(ES), C(TA), C(TB))
        sv.memset(C(APC), 1.0)
        sv.memset(C(APS), 0.0)
        for k in range(7):
            sv.tensor_mul(C(TA), C(APC + k), C(EC))
            sv.tensor_mul(C(TB), C(APS + k), C(ES))
            sv.tensor_sub(C(APC + k + 1), C(TA), C(TB))
            sv.tensor_mul(C(TA), C(APS + k), C(EC))
            sv.tensor_mul(C(TB), C(APC + k), C(ES))
            sv.tensor_add(C(APS + k + 1), C(TA), C(TB))
        for (cg, sg, ohb) in ((CG0, SG0, w["oh0_b"]), (CG1, SG1, w["oh1_b"])):
            sv.tensor_mul(sc1[:, T8:T8 + 8], sc1[:, APC:APC + 8], ohb[:])
            sv.tensor_reduce(C(cg), sc1[:, T8:T8 + 8], X, ALU.add)
            sv.tensor_mul(sc1[:, T8:T8 + 8], sc1[:, APS:APS + 8], ohb[:])
            sv.tensor_reduce(C(sg), sc1[:, T8:T8 + 8], X, ALU.add)
        sv.memset(xmre[:, 0:1], 1.0)
        sv.memset(xmim[:, 0:1], 0.0)
        sv.tensor_copy(C(EC), C(CTH))
        sv.tensor_copy(C(ES), C(STH))
        n = 1
        while n < 512:
            sv.tensor_scalar_mul(C(NEG), C(ES), -1.0)
            sv.tensor_scalar(vre[:, 0:n], xmre[:, 0:n], C(EC), None, op0=ALU.mult)
            sv.scalar_tensor_tensor(xmre[:, n:2 * n], xmim[:, 0:n], C(NEG),
                                    vre[:, 0:n], op0=ALU.mult, op1=ALU.add)
            sv.tensor_scalar(vre[:, 0:n], xmim[:, 0:n], C(EC), None, op0=ALU.mult)
            sv.scalar_tensor_tensor(xmim[:, n:2 * n], xmre[:, 0:n], C(ES),
                                    vre[:, 0:n], op0=ALU.mult, op1=ALU.add)
            _cmul_sq(sv, C(EC), C(ES), C(TA), C(TB))
            n *= 2
        for (sl, cg, sg) in ((0, CG0, SG0), (1, CG1, SG1)):
            cd = cost[:, ct * T + sl * 512:ct * T + (sl + 1) * 512]
            sd = sint[:, ct * T + sl * 512:ct * T + (sl + 1) * 512]
            sv.tensor_scalar_mul(C(NEG), C(sg), -1.0)
            sv.tensor_scalar(vre[:, 0:512], xmre[:, 0:512], C(cg), None, op0=ALU.mult)
            sv.scalar_tensor_tensor(cd, xmim[:, 0:512], C(NEG), vre[:, 0:512],
                                   op0=ALU.mult, op1=ALU.add)
            sv.tensor_scalar(vre[:, 0:512], xmim[:, 0:512], C(cg), None, op0=ALU.mult)
            sv.scalar_tensor_tensor(sd, xmre[:, 0:512], C(sg), vre[:, 0:512],
                                   op0=ALU.mult, op1=ALU.add)
        sv.memset(rtile4[:, ct * 512:(ct + 1) * 512], 1.0)
        sv.tensor_scalar(rtile4[:, ct * 512:(ct + 1) * 512],
                         rtile4[:, ct * 512:(ct + 1) * 512], C(RR), None, op0=ALU.mult)
        ysl = y_sb[:, ct * T:(ct + 1) * T]
        sv.tensor_mul(xmre[:], ysl, cost[:, ct * T:(ct + 1) * T])
        sv.tensor_mul(xmim[:], ysl, sint[:, ct * T:(ct + 1) * T])
        for sl in range(2):
            for comp, (vv, xm) in enumerate(((vre, xmre), (vim, xmim))):
                sv.tensor_tensor_scan(
                    vv[:, sl * 512:(sl + 1) * 512],
                    rtile4[:, ct * 512:(ct + 1) * 512],
                    xm[:, sl * 512:(sl + 1) * 512], 0.0,
                    op0=ALU.mult, op1=ALU.add)
                sv.tensor_copy(
                    zsb[:, ct * 4 + sl * 2 + comp:ct * 4 + sl * 2 + comp + 1],
                    vv[:, sl * 512 + 511:sl * 512 + 512])
        if dbg is not None and "scdbg" in dbg and lay == 0 and ct == 0:
            dd = dbg["scdbg"]
            p.s_ve.wait(sy)
            for bi, (tile, wdt) in enumerate((
                    (cost[:, 0:T], T), (sint[:, 0:T], T), (xmre[:], T),
                    (xmim[:], T), (vre[:], T), (vim[:], T),
                    (rtile4[:, 0:512], 512), (sc1[:, 0:72], 72))):
                p.s_z.inc(sy.dma_start(out=dd[bi * 128:bi * 128 + 128, 0:wdt],
                                       in_=tile), 16)
            p.s_z.wait(v)
    sync_all(p, te)
    p.s_te.inc(te.transpose(w["pbt"][0:16, 0:128], zsb[:], w["ident"][:]))
    p.s_te.wait(v)
    sv.tensor_copy(zg_sb[0:16, 0:128], w["pbt"][0:16, 0:128])
    p.s_ve.wait(sy)
    for ct in range(4):
        p.s_z.inc(sy.dma_start(out=dram["z_src"][0:4, ct * 128:(ct + 1) * 128],
                               in_=zg_sb[ct * 4:(ct + 1) * 4, 0:128]), 16)
    p.s_z.wait(gp)
    p.s_cc.inc(gp.collective_compute(
        "AllGather", ALU.bypass,
        replica_groups=[[0, 1, 2, 3], [4, 5, 6, 7]],
        ins=[dram["z_src"][:, :].opt()],
        outs=[dram["z_gth"][:, :].opt()]), 1)
    p.s_cc.wait(gp)
    p.s_zg.inc(gp.dma_start(out=zg_sb[:], in_=dram["z_gth"][:, :]), 16)
    p.s_zg.wait(te)
    for ct in range(4):
        p.s_ve.wait(te)
        p.s_te.inc(te.transpose(w["pbt"][0:128, 0:16],
                                zg_sb[0:16, ct * 128:(ct + 1) * 128],
                                w["ident"][0:16, 0:16]))
        p.s_te.wait(v)
        sv.tensor_copy(zt_sb[:, ct * 16:(ct + 1) * 16], w["pbt"][0:128, 0:16])
    for ct in range(4):
        sync_all(p, v)
        sv.tensor_copy(C(VEC), C(4 + ct))
        sv.tensor_copy(C(VES), C(8 + ct))
        for k in range(8):
            rk = k if k < 4 else 7 - k
            slk = 0 if k < 4 else 1
            for comp, base in ((0, VEC), (1, VES)):
                zcol = zt_sb[:, ct * 16 + rk * 4 + slk * 2 + comp:
                             ct * 16 + rk * 4 + slk * 2 + comp + 1]
                sv.scalar_tensor_tensor(C(base + k + 1), C(base + k), C(ct),
                                        zcol, op0=ALU.mult, op1=ALU.add)
        for (sl, ohb, ic, isv) in ((0, w["oh0_b"], TA, TB), (1, w["oh1_b"], EC, ES)):
            sv.tensor_mul(sc1[:, T8:T8 + 8], sc1[:, VEC:VEC + 8], ohb[:])
            sv.tensor_reduce(C(ic), sc1[:, T8:T8 + 8], X, ALU.add)
            sv.tensor_mul(sc1[:, T8:T8 + 8], sc1[:, VES:VES + 8], ohb[:])
            sv.tensor_reduce(C(isv), sc1[:, T8:T8 + 8], X, ALU.add)
        ysl = y_sb[:, ct * T:(ct + 1) * T]
        sv.tensor_mul(xmre[:], ysl, cost[:, ct * T:(ct + 1) * T])
        sv.tensor_mul(xmim[:], ysl, sint[:, ct * T:(ct + 1) * T])
        for sl, (icol, iscol) in ((0, (TA, TB)), (1, (EC, ES))):
            sv.tensor_tensor_scan(vre[:, sl * 512:(sl + 1) * 512],
                                 rtile4[:, ct * 512:(ct + 1) * 512],
                                 xmre[:, sl * 512:(sl + 1) * 512], C(icol),
                                 op0=ALU.mult, op1=ALU.add)
            sv.tensor_tensor_scan(vim[:, sl * 512:(sl + 1) * 512],
                                 rtile4[:, ct * 512:(ct + 1) * 512],
                                 xmim[:, sl * 512:(sl + 1) * 512], C(iscol),
                                 op0=ALU.mult, op1=ALU.add)
        if dbg is not None and "scdbg2" in dbg and lay == 0 and ct == 0:
            dd = dbg["scdbg2"]
            p.s_ve.wait(sy)
            for bi, (tile, wdt) in enumerate((
                    (zt_sb[:, 0:64], 64), (sc1[:, 0:72], 72), (vre[:], T),
                    (vim[:], T), (xmre[:], T), (xmim[:], T),
                    (rtile4[:, 0:512], 512), (zg_sb[0:16, 0:512], 512))):
                p.s_z.inc(sy.dma_start(out=dd[bi * 128:bi * 128 + 128, 0:wdt]
                                       if bi != 7 else dd[bi * 128:bi * 128 + 16, 0:wdt],
                                       in_=tile), 16)
            p.s_z.wait(v)
        sv.tensor_mul(xmre[:], cost[:, ct * T:(ct + 1) * T], vre[:])
        sv.tensor_mul(xmim[:], sint[:, ct * T:(ct + 1) * T], vim[:])
        sv.tensor_add(xmre[:], xmre[:], xmim[:])
        if dbg is not None and f"conv{lay}" in dbg:
            sv.tensor_copy(xmim[:], xmre[:])
            p.s_ve.wait(sy)
            p.s_z.inc(sy.dma_start(
                out=dbg[f"conv{lay}"][ct * 128:(ct + 1) * 128, :], in_=xmim[:]), 16)
        hs = h_sb[:, ct * T:(ct + 1) * T]
        sv.scalar_tensor_tensor(hs, xmre[:], scl[:, ct:ct + 1], hs,
                                op0=ALU.mult, op1=ALU.add)


NSB = 16
SBW = 256
INV_SQRT_HD = float(1.0 / np.sqrt(HD))
EXP_SHIFT = -12.0


def s3_positions():
    pos = []
    bias_idx = 0
    for sl in (0, 1):
        n_off = 4 if sl == 0 else 7
        for ch in range(n_off):
            rank = ch if ch < 4 else 7 - ch
            slot = 0 if ch < 4 else 1
            pos.append((sl, "g", rank, slot, bias_idx))
            bias_idx += 1
        pos.append((sl, "d", 0, sl, bias_idx))
        bias_idx += 1
    return pos


S3_POS = s3_positions()
S3_NBIAS = 13


def emit_qkvw_proj(p, w, wimg_dram, bias_cols, dst_writer, wbuf, mirror):
    te, v, gp = p.te, p.ve, p.gp
    mm = te.matmul
    wslot = [0, 0]
    guards = {"ve": [p.s_ve.n] * 2, "se": [p.s_se.n] * 2}
    for sbi in range(NSB):
        s = 0
        p.s_te.wait(gp)
        wslot[s] = p.s_wt.inc(gp.dma_start(
            out=wbuf[s][:], in_=wimg_dram[sbi]), 16)
        p.s_wt.wait(te, wslot[s])
        for ft in range(2):
            for th in range(2):
                bank = w["pb"][2 * ft + th]
                dst_writer("guard", bank, None, None, None)
                for kt in range(NKT):
                    ins = mm(bank[:, :],
                             wbuf[s][:, kt * SBW + ft * 128:kt * SBW + (ft + 1) * 128],
                             mirror[:, kt * T + th * 512:kt * T + (th + 1) * 512],
                             start=kt == 0, stop=kt == NKT - 1,
                             skip_group_check=True)
                p.s_te.inc(ins)
                dst_writer("use", bank, sbi, ft, th)


def emit_s3(p, x_dram, out_dram, wa, w, dram, dbg):
    nc = p.nc
    v, se, te, gp, sy = p.ve, p.se, p.te, p.gp, p.sy
    sv = SV(p)
    mm = te.matmul

    octx = ExitStack()
    av_sb = octx.enter_context(nc.sbuf_tensor("av_sb", [128, NH * T], BF16))

    sctx = ExitStack()
    w["mirror"] = sctx.enter_context(nc.sbuf_tensor("s3_mirror", [128, NKT * T], BF16))
    wbuf = [sctx.enter_context(nc.sbuf_tensor("s3_wb0", [128, NKT * SBW], BF16))]
    kvstage = sctx.enter_context(nc.sbuf_tensor("kvstage", [128, T], BF16))
    bvb = sctx.enter_context(nc.sbuf_tensor("bvb", [128, SBW], F32))
    bvrow = sctx.enter_context(nc.sbuf_tensor("bvrow", [1, SBW], F32))
    emit_stats(p, x_dram, w, double_ln=False)
    emit_mirror(p, x_dram, w, None, None)
    mirror = w["mirror"]

    q_sto = [0]

    def q_writer(mode, bank, sbi, ft, th):
        if mode == "guard":
            p.s_ve.wait(te)
            return
        p.s_te.wait(v)
        p.s_sto[1].wait(v, q_sto[0])
        fcol = sbi * SBW + ft * 128
        p.s_ve.inc(v.tensor_scalar(
            kvstage[:, th * 512:(th + 1) * 512], bank[:, :],
            wa["bq_cols"][:, fcol // 128:fcol // 128 + 1], None, op0=ALU.add))
        if th == 1:
            p.s_ve.wait(sy)
            q_sto[0] = p.s_sto[1].inc(sy.dma_start(
                out=dram["q_dram"][fcol:fcol + 128, :], in_=kvstage[:]), 16)
    emit_qkvw_proj(p, w, dram["wq_img"], None, q_writer, wbuf, mirror)

    k_sto = [0]

    def k_writer(mode, bank, sbi, ft, th):
        if mode == "guard":
            p.s_ve.wait(te)
            return
        p.s_te.wait(v)
        p.s_sto[0].wait(v, k_sto[0])
        fcol = sbi * SBW + ft * 128
        p.s_ve.inc(v.tensor_scalar(
            kvstage[:, th * 512:(th + 1) * 512], bank[:, :],
            wa["bk_cols"][:, fcol // 128:fcol // 128 + 1], None, op0=ALU.add))
        if th == 1:
            p.s_ve.wait(sy)
            k_sto[0] = p.s_sto[0].inc(sy.dma_start(
                out=dram["kv_src"][fcol:fcol + 128, :], in_=kvstage[:]), 16)
    emit_qkvw_proj(p, w, dram["wk_img"], None, k_writer, wbuf, mirror)

    vreg = dram["kv_src"].rearrange("(r c) t -> r c t", c=4)
    v_sto = [0]
    for sbi in range(NSB):
        s = 0
        p.s_te.wait(gp)
        wld = p.s_wt.inc(gp.dma_start(out=wbuf[s][:], in_=dram["wv_img"][sbi]), 16)
        p.s_te.wait(sy)
        bvld = p.s_x[0].inc(sy.dma_start(
            out=bvrow[:], in_=dram["bv_row"][0:1, sbi * SBW:(sbi + 1) * SBW]), 16)
        p.s_x[0].wait(te, bvld)
        p.s_ve.wait(te)
        p.s_te.inc(te.matmul(w["pb"][5][:, 0:SBW], w["ones_row"][:],
                             bvrow[0:1, :], start=True, stop=True,
                             skip_group_check=True))
        p.s_te.wait(v)
        p.s_ve.inc(v.tensor_copy(bvb[:], w["pb"][5][:, 0:SBW]))
        p.s_wt.wait(te, wld)
        for tt in range(8):
            bank = w["pb"][tt % 2]
            p.s_ve.wait(te)
            for kt in range(NKT):
                ins = mm(bank[:, 0:SBW],
                         mirror[:, kt * T + tt * 128:kt * T + (tt + 1) * 128],
                         wbuf[s][:, kt * SBW:(kt + 1) * SBW],
                         start=kt == 0, stop=kt == NKT - 1, skip_group_check=True)
            p.s_te.inc(ins)
            p.s_te.wait(v)
            p.s_sto[1].wait(v, v_sto[0])
            p.s_ve.inc(v.tensor_add(kvstage[:, 0:SBW], bank[:, 0:SBW], bvb[:]))
            p.s_ve.wait(sy)
            f0 = sbi * SBW
            v_sto[0] = p.s_sto[1].inc(sy.dma_start(
                out=vreg[1024 + tt * 128:1024 + (tt + 1) * 128, f0 // 1024,
                         f0 % 1024:f0 % 1024 + SBW],
                in_=kvstage[0:128, 0:SBW]), 16)
    barrier(p)
    sctx.close()

    p.s_cc.inc(gp.collective_compute(
        "AllGather", ALU.bypass,
        replica_groups=[[0, 1, 2, 3], [4, 5, 6, 7]],
        ins=[dram["kv_src"][:, :].opt()],
        outs=[dram["kv_gth"].rearrange("r f t -> (r f) t").opt()]), 1)
    p.s_cc.wait(gp)
    barrier(p)

    sctx = ExitStack()
    qbuf = [sctx.enter_context(nc.sbuf_tensor(f"qb{i}", [128, 512], BF16))
            for i in range(2)]
    kst = [sctx.enter_context(nc.sbuf_tensor(f"kst{i}", [128, 512], BF16))
           for i in range(2)]
    vst = [sctx.enter_context(nc.sbuf_tensor(f"vst{i}", [128, 512], BF16))
           for i in range(2)]
    pt_sb = [sctx.enter_context(nc.sbuf_tensor(f"pt{i}", [128, 512], BF16))
             for i in range(2)]
    masks = sctx.enter_context(nc.sbuf_tensor("masks", [128, 4 * 512], BF16))
    diag_i2 = sctx.enter_context(nc.sbuf_tensor("diag_i2", [128, 512], I32))
    recipr = sctx.enter_context(nc.sbuf_tensor("recipr", [1, 512], F32))
    recipb = sctx.enter_context(nc.sbuf_tensor("recipb", [128, 512], F32))
    p.s_gp.inc(gp.iota(diag_i2[:], pattern=[[-1, 512]], base=0,
                       channel_multiplier=1))
    p.s_gp.wait(v)
    for kt in range(4):
        sv.tensor_single_scalar(masks[:, kt * 512:(kt + 1) * 512], diag_i2[:],
                                float(-kt * 128), ALU.is_le)
    kv_at = [p.s_kv[0].n, p.s_kv[1].n]
    q_at = [p.s_q[0].n, p.s_q[1].n]
    te_kv = [p.s_te.n] * 2
    te_q = [p.s_te.n] * 2
    te_pt = [p.s_te.n] * 2
    se_pt = [p.s_se.n] * 2
    npos = len(S3_POS)
    for h in range(NH):
        for pi, (sl, kind, rank, slot, bcol) in enumerate(S3_POS):
            s = pi % 2
            seg = (h * 2 + sl) % 2
            if pi == 0 or S3_POS[pi - 1][0] != sl:
                p.s_te.wait(sy, te_q[seg])
                q_at[seg] = p.s_q[seg].inc(sy.dma_start(
                    out=qbuf[seg][:],
                    in_=dram["q_dram"][h * 128:(h + 1) * 128,
                                       sl * 512:(sl + 1) * 512]), 16)
            p.s_te.wait(sy, te_kv[s])
            if kind == "g":
                ksrc = dram["kv_gth"][rank, h * 128:(h + 1) * 128,
                                      slot * 512:(slot + 1) * 512]
                vsrc = dram["kv_gth"].rearrange("r (q c) t -> r q c t", c=4)[
                    rank, 1024 + slot * 512:1024 + (slot + 1) * 512,
                    h // 8, (h % 8) * 128:(h % 8) * 128 + 128]
            else:
                ksrc = dram["kv_src"][h * 128:(h + 1) * 128,
                                      slot * 512:(slot + 1) * 512]
                vsrc = dram["kv_src"].rearrange("(q c) t -> q c t", c=4)[
                    1024 + slot * 512:1024 + (slot + 1) * 512,
                    h // 8, (h % 8) * 128:(h % 8) * 128 + 128]
            p.s_kv[s].inc(sy.dma_start(out=kst[s][:], in_=ksrc), 16)
            kv_at[s] = p.s_kv[s].inc(sy.dma_start(
                out=vst[s][:].rearrange("p (kt c) -> p kt c", kt=4),
                in_=vsrc.rearrange("(kt p) c -> p kt c", p=128)), 16)
            first = pi == 0 or S3_POS[pi - 1][0] != sl
            last = pi == npos - 1 or S3_POS[pi + 1][0] != sl
            p.s_kv[s].wait(te, kv_at[s])
            for kt in range(4):
                scb = w["pb"][4 + (kt % 2)]
                p.s_se.wait(te, se_pt[kt % 2])
                p.s_q[seg].wait(te, q_at[seg])
                p.s_te.inc(mm(scb[:, :], kst[s][:, kt * 128:(kt + 1) * 128],
                              qbuf[seg][:], start=True, stop=True,
                              skip_group_check=True))
                p.s_te.wait(se)
                p.s_ve.wait(se)
                se_pt[kt % 2] = p.s_se.inc(se.activation(
                    pt_sb[kt % 2][:], scb[:, :], AF.Exp,
                    bias=wa["kvb_cols"][:, bcol:bcol + 1],
                    scale=np.float32(INV_SQRT_HD)))
                if kind == "d":
                    p.s_se.wait(v)
                    p.s_ve.inc(v.tensor_mul(pt_sb[kt % 2][:], pt_sb[kt % 2][:],
                                            masks[:, kt * 512:(kt + 1) * 512]))
                p.s_se.wait(te)
                p.s_ve.wait(te)
                st = first and kt == 0
                sp = last and kt == 3
                mm(w["pb"][2 + sl][:, :], vst[s][:, kt * 128:(kt + 1) * 128],
                   pt_sb[kt % 2][:], start=st, stop=sp, skip_group_check=True)
                ins = mm(w["pb"][0 + sl][0:1, :], w["ones_bf"][:], pt_sb[kt % 2][:],
                         start=st, stop=sp, skip_group_check=True)
                te_pt[kt % 2] = p.s_te.inc(ins)
            te_kv[s] = p.s_te.n
            te_q[seg] = p.s_te.n
            if last:
                p.s_te.wait(v)
                sv.tensor_copy(recipr[0:1, :], w["pb"][0 + sl][0:1, :])
                sv.reciprocal(recipr[0:1, :], recipr[0:1, :])
                p.s_ve.wait(te)
                p.s_te.inc(te.matmul(w["pb"][6][:, :], w["ones_row"][:],
                                     recipr[0:1, :], start=True, stop=True,
                                     skip_group_check=True))
                p.s_te.wait(v)
                sv.tensor_copy(recipb[:], w["pb"][6][:, :])
                p.s_ve.inc(v.tensor_mul(
                    av_sb[:, h * T + sl * 512:h * T + (sl + 1) * 512],
                    w["pb"][2 + sl][:, :], recipb[:]))
                p.s_ve.wait(te)
    if dbg is not None and "xav" in dbg:
        emit_dump(p, dbg["xav"], av_sb, NKT)
    barrier(p)
    sctx.close()

    sctx = ExitStack()
    wobuf = [sctx.enter_context(nc.sbuf_tensor(f"s3_wo{i}", [128, NKT * SBW], BF16))
             for i in range(2)]
    xin = w["xin2"]
    wslot = [0, 0]
    ld_at = [p.s_x[0].n, p.s_x[1].n]
    st_at = [p.s_sto[0].n, p.s_sto[1].n]
    ve_at = [p.s_ve.n] * 2
    last_stt = [p.s_ve.n]
    for sbi in range(NSB):
        s2 = sbi % 2
        p.s_te.wait(gp)
        wslot[s2] = p.s_wt.inc(gp.dma_start(out=wobuf[s2][:],
                                            in_=dram["wo_img"][sbi]), 16)
        p.s_wt.wait(te, wslot[s2])
        for ft in range(2):
            ot = sbi * 2 + ft
            s = ot % 2
            p.s_ve.wait(sy, ve_at[s])
            p.s_sto[s].wait(sy, st_at[s])
            ld_at[s] = p.s_x[s].inc(sy.dma_start(
                out=xin[s][:], in_=x_dram[ot * 128:(ot + 1) * 128, :]), 16)
            for th in range(2):
                bank = w["pb"][th]
                p.s_ve.wait(te, last_stt[0])
                for kt in range(NKT):
                    ins = mm(bank[:, :],
                             wobuf[s2][:, kt * SBW + ft * 128:kt * SBW + (ft + 1) * 128],
                             av_sb[:, kt * T + th * 512:kt * T + (th + 1) * 512],
                             start=kt == 0, stop=kt == NKT - 1,
                             skip_group_check=True)
                p.s_te.inc(ins)
            p.s_te.wait(v)
            p.s_x[s].wait(v, ld_at[s])
            v.scalar_tensor_tensor(xin[s][:, 0:512], w["pb"][0][:, :],
                                   wa["bo_cols"][:, ot:ot + 1], xin[s][:, 0:512],
                                   op0=ALU.add, op1=ALU.add)
            ve_at[s] = p.s_ve.inc(v.scalar_tensor_tensor(
                xin[s][:, 512:1024], w["pb"][1][:, :], wa["bo_cols"][:, ot:ot + 1],
                xin[s][:, 512:1024], op0=ALU.add, op1=ALU.add))
            last_stt[0] = ve_at[s]
            p.s_ve.wait(sy, ve_at[s])
            st_at[s] = p.s_sto[s].inc(sy.dma_start(
                out=out_dram[ot * 128:(ot + 1) * 128, :], in_=xin[s][:]), 16)
    barrier(p)
    sctx.close()
    octx.close()


def emit_dump(p, dst_dram, src_sb, nct):
    barrier(p)
    for ct in range(nct):
        eng = p.gp if src_sb.dtype != F32 else p.sy
    for ct in range(nct):
        if src_sb.dtype != F32:
            p.s_gd.inc(p.gp.dma_start(out=dst_dram[ct * 128:(ct + 1) * 128, :],
                                      in_=src_sb[:, ct * T:(ct + 1) * T]), 16)
        else:
            p.s_z.inc(p.sy.dma_start(out=dst_dram[ct * 128:(ct + 1) * 128, :],
                                     in_=src_sb[:, ct * T:(ct + 1) * T]), 16)
    barrier(p)


def _s1_cols_layout():
    lay = {}
    off = 0

    def add(name, n):
        nonlocal off
        lay[name] = (off, n)
        off += n
    add("bin", 4)
    add("lnl_w", 4)
    add("lnl_b", 4)
    add("bout", 32)
    for l in range(AD):
        for nm, n in (("ln1_w", 4), ("ln1_b", 4), ("ln2_w", 4), ("ln2_b", 4),
                      ("phz_re", 4), ("phz_im", 4), ("lc_re", 4), ("lc_im", 4),
                      ("scl", 4), ("b2", 4), ("b1", 16)):
            add(f"{nm}{l}", n)
    return lay, off


S1_COLS, S1C_N = _s1_cols_layout()


def build_program(stages=("s1", "s2", "s3", "s4"), dbg_keys=()):
    nc = bass.Bass("TRN2", target_bir_lowering=False, debug=False,
                   detect_race_conditions=False)
    dram = {}

    def din(name, shape, dt=F32):
        dram[name] = nc.dram_tensor(name, shape, dt, kind="ExternalInput").ap()
        return dram[name]

    x_in = din("x_in", [D, T])
    out_fm = nc.dram_tensor("out_fm", [D, T], F32, kind="ExternalOutput").ap()
    x_res = nc.dram_tensor("x_res", [D, T], F32).ap()
    u_bounce = nc.dram_tensor("u_bounce", [D, T], BF16).ap()

    din("ffn_A", [64, 64])
    din("ffn_B", [64, 64])
    din("ffn_g_cols", [128, NKT])
    din("ffn_b_cols", [128, NKT])
    din("ffn_biasp_cols", [128, NKT])

    if "s3" in stages:
        for nm in ("wq_img", "wk_img", "wv_img", "wo_img"):
            din(nm, [NSB, 128, NKT * SBW])
        din("bq_cols", [128, NKT])
        din("bk_cols", [128, NKT])
        din("bo_cols", [128, NKT])
        din("bv_row", [1, D])
        din("kvb_cols", [128, S3_NBIAS])
        dram["kv_src"] = nc.dram_tensor("kv_src", [2 * D, T], BF16).ap()
        dram["kv_gth"] = nc.dram_tensor("kv_gth", [4, 2 * D, T], BF16).ap()
        dram["q_dram"] = nc.dram_tensor("q_dram", [D, T], BF16).ap()

    if "s1" in stages:
        din("win_img", [128, NKT * DIM])
        din("wout_img", [128, 4 * D])
        din("w1_img", [AD, 128, 4 * DFF])
        din("w2_img", [AD, 128, 16 * 512])
        din("s1_cols", [128, S1C_N])
        din("oh0", [1, 8])
        din("oh1", [1, 8])
        dram["z_src"] = nc.dram_tensor("z_src", [4, DIM], F32).ap()
        dram["z_gth"] = nc.dram_tensor("z_gth", [16, DIM], F32).ap()

    dbg = {}
    for k in dbg_keys:
        if k in ("scdbg", "scdbg2"):
            dbg[k] = nc.dram_tensor("dbg_" + k, [8 * 128, T], F32,
                                    kind="ExternalOutput").ap()
            continue
        nct = 4 if not k.startswith("x") else NKT
        dbg[k] = nc.dram_tensor("dbg_" + k, [nct * 128, T], F32,
                                kind="ExternalOutput").ap()

    with ExitStack() as ctx:
        p = P(nc, ctx)
        with nc.Block():
            w = {
                "xin2": [p.sbuf("xin0", [128, T], F32), p.sbuf("xin1", [128, T], F32)],
                "xbs2": [p.sbuf("xbs0", [128, T], BF16), p.sbuf("xbs1", [128, T], BF16)],
                "sqs2": [p.sbuf("sqs0", [128, T], BF16), p.sbuf("sqs1", [128, T], BF16)],
                "t1": p.sbuf("t1", [128, T], F32),
                "ones_bf": p.sbuf("ones_bf", [128, 1], BF16),
                "ones_row": p.sbuf("ones_row", [1, 128], F32),
                "r0": p.sbuf("r0", [1, T], F32),
                "r1": p.sbuf("r1", [1, T], F32),
                "r2": p.sbuf("r2", [1, T], F32),
                "r3": p.sbuf("r3", [1, T], F32),
                "rstd": p.sbuf("rstd", [1, T], F32),
                "sc_row": p.sbuf("sc_row", [1, T], F32),
                "mu_b": p.sbuf("mu_b", [128, T], F32),
                "sc_b": p.sbuf("sc_b", [128, T], F32),
                "sc2_b": p.sbuf("sc2_b", [128, T], F32),
                "pb": [p.psum(f"pb{i}", [128, 512]) for i in range(7)],
            }
            w["ones_f32"] = p.sbuf("ones_f32", [128, 1], F32)
            w["pbt"] = w["pb"][5]
            p.s_ve.inc(p.ve.memset(w["ones_bf"][:], 1.0))
            p.s_ve.inc(p.ve.memset(w["ones_f32"][:], 1.0))
            p.s_ve.inc(p.ve.memset(w["ones_row"][:], 1.0))

            ws = None
            if "s1" in stages:
                diag_i = p.sbuf("diag_i", [128, 128], I32)
                w["ident"] = p.sbuf("ident", [128, 128], F32)
                p.s_gp.inc(p.gp.iota(diag_i[:], pattern=[[-1, 128]], base=0,
                                     channel_multiplier=1))
                oh0r = p.sbuf("oh0r", [1, 8], F32)
                oh1r = p.sbuf("oh1r", [1, 8], F32)
                w["oh0_b"] = p.sbuf("oh0_b", [128, 8], F32)
                w["oh1_b"] = p.sbuf("oh1_b", [128, 8], F32)
                p.s_const.inc(p.gp.dma_start(out=oh0r[:], in_=dram["oh0"]), 16)
                p.s_const.inc(p.gp.dma_start(out=oh1r[:], in_=dram["oh1"]), 16)
                s1c = p.sbuf("s1c", [128, S1C_N], F32)
                p.s_const.inc(p.gp.dma_start(out=s1c[:], in_=dram["s1_cols"]), 16)

                def wsl(name):
                    o, n = S1_COLS[name]
                    return s1c[:, o:o + n]
                ws = {
                    "bin_cols": wsl("bin"), "lnl_w": wsl("lnl_w"),
                    "lnl_b": wsl("lnl_b"), "bout_cols": wsl("bout"),
                }
                for nm in ("ln1_w", "ln1_b", "ln2_w", "ln2_b", "phz_re",
                           "phz_im", "lc_re", "lc_im", "scl", "b2", "b1"):
                    key = nm + "_cols" if nm in ("b1", "b2") else nm
                    ws[key] = [wsl(f"{nm}{l}") for l in range(AD)]

            wf = {"u_dram": u_bounce}
            for nm in ("A_sb", "B_sb"):
                t = p.sbuf(nm, [128, 64], BF16)
                p.s_const.inc(p.gp.dma_start(out=t[0:64, :], in_=dram["ffn_" + nm[0]]), 16)
                p.s_const.inc(p.gp.dma_start(out=t[64:128, :], in_=dram["ffn_" + nm[0]]), 16)
                wf[nm] = t
            for nm in ("gcols", "bcols", "biasp_cols"):
                t = p.sbuf(nm, [128, NKT], F32)
                key = {"gcols": "ffn_g_cols", "bcols": "ffn_b_cols",
                       "biasp_cols": "ffn_biasp_cols"}[nm]
                p.s_const.inc(p.gp.dma_start(out=t[:], in_=dram[key]), 16)
                wf[nm] = t
            barrier(p)

            wa = None
            if "s3" in stages:
                wa = {}
                for nm in ("bq_cols", "bk_cols", "bo_cols", "kvb_cols"):
                    t = p.sbuf("wa_" + nm, [128, dram[nm].shape[1]], F32)
                    p.s_const.inc(p.gp.dma_start(out=t[:], in_=dram[nm]), 16)
                    wa[nm] = t
            if "s1" in stages:
                p.s_ve.inc(p.ve.tensor_single_scalar(w["ident"][:], diag_i[:],
                                                     0.0, ALU.is_equal))
                for ohr, ohb in ((oh0r, w["oh0_b"]), (oh1r, w["oh1_b"])):
                    p.s_ve.wait(p.te)
                    p.s_te.inc(p.te.matmul(w["pb"][4][:, 0:8], w["ones_row"][:],
                                           ohr[:], start=True, stop=True,
                                           skip_group_check=True))
                    p.s_te.wait(p.ve)
                    p.s_ve.inc(p.ve.tensor_copy(ohb[:], w["pb"][4][:, 0:8]))

            last = {}
            cur_x = x_in
            for si, st in enumerate(stages):
                is_last = si == len(stages) - 1
                dst = out_fm if is_last else x_res
                if st in ("s2", "s4"):
                    emit_ffn(p, cur_x, dst, wf, w, last)
                elif st == "s1":
                    emit_s1(p, cur_x, dst, ws, w, dram, dbg if dbg else None)
                elif st == "s3":
                    emit_s3(p, cur_x, dst, wa, w, dram, dbg if dbg else None)
                else:
                    raise NotImplementedError(st)
                dk = f"xafter{si}"
                if dbg and dk in dbg:
                    barrier(p)
                    for q8 in range(8):
                        p.s_z.inc(p.sy.dma_start(
                            out=dbg[dk][q8 * 512:(q8 + 1) * 512, :],
                            in_=dst[q8 * 512:(q8 + 1) * 512, :]), 16)
                    barrier(p)
                cur_x = x_res

            sync_all(p, p.sy)
    return nc


def tok_ids(core):
    j = core % 4
    return np.r_[j * CH:(j + 1) * CH, (7 - j) * CH:(8 - j) * CH]


def host_inputs(inputs):
    x = inputs["L0_attn_a_x"]
    A = np.ascontiguousarray(np.asarray(inputs["L0_ffn_A"], f32))
    Bm = np.ascontiguousarray(np.asarray(inputs["L0_ffn_B"], f32))
    bias = np.asarray(inputs["L0_ffn_bias"], f32)
    g = np.asarray(inputs["L0_ffn_ln_gamma"], f32)
    be = np.asarray(inputs["L0_ffn_ln_beta"], f32)

    def cols(vec):
        return np.ascontiguousarray(vec.reshape(NKT, 128).T)

    biasp = np.ascontiguousarray(bias.reshape(64, 64).T).reshape(D)
    shared = {
        "ffn_A": A, "ffn_B": Bm,
        "ffn_g_cols": cols(g), "ffn_b_cols": cols(be),
        "ffn_biasp_cols": cols(biasp),
    }

    def colv(vec, n):
        return np.asarray(vec, f32).reshape(n, 128).T

    aa = {k[len("L0_attn_a_"):]: np.asarray(v, f32) for k, v in inputs.items()
          if k.startswith("L0_attn_a_") and k != "L0_attn_a_x"}
    shared["win_img"] = np.ascontiguousarray(
        aa["Win"].T.reshape(NKT, 128, DIM).transpose(1, 0, 2).reshape(128, NKT * DIM))
    shared["wout_img"] = np.ascontiguousarray(
        aa["Wout"].T.reshape(4, 128, D).transpose(1, 0, 2).reshape(128, 4 * D))
    shared["w1_img"] = np.ascontiguousarray(np.stack([
        aa["W1"][l].T.reshape(4, 128, DFF).transpose(1, 0, 2).reshape(128, 4 * DFF)
        for l in range(AD)]))
    shared["w2_img"] = np.ascontiguousarray(np.stack([
        aa["W2"][l].T.reshape(16, 128, 512).transpose(1, 0, 2).reshape(128, 16 * 512)
        for l in range(AD)]))
    ab = {k[len("L0_attn_b_"):]: np.asarray(v, f32) for k, v in inputs.items()
          if k.startswith("L0_attn_b_")}

    def wimg(m):
        return np.ascontiguousarray(
            m.T.reshape(NKT, 128, NSB, SBW).transpose(2, 1, 0, 3)
            .reshape(NSB, 128, NKT * SBW))
    shared["wq_img"] = wimg(ab["Wq"])
    shared["wk_img"] = wimg(ab["Wk"])
    shared["wv_img"] = wimg(ab["Wv"])
    shared["wo_img"] = wimg(ab["Wo"])
    shared["bq_cols"] = cols(ab["bq"])
    shared["bk_cols"] = cols(ab["bk"])
    shared["bo_cols"] = cols(ab["bo"])
    shared["bv_row"] = np.ascontiguousarray(ab["bv"].reshape(1, D))

    s1c = np.zeros((128, S1C_N), f32)

    def put(name, vec, n):
        o, k = S1_COLS[name]
        assert k == n
        s1c[:, o:o + n] = colv(vec, n)
    put("bin", aa["bin"], 4)
    put("lnl_w", aa["lnl_w"], 4)
    put("lnl_b", aa["lnl_b"], 4)
    put("bout", aa["bout"], 32)
    for l in range(AD):
        for nm, n in (("ln1_w", 4), ("ln1_b", 4), ("ln2_w", 4), ("ln2_b", 4),
                      ("phz_re", 4), ("phz_im", 4), ("lc_re", 4), ("lc_im", 4),
                      ("sc_lin", 4), ("b2", 4), ("b1", 16)):
            key = "scl" if nm == "sc_lin" else nm
            put(f"{key}{l}", aa[nm][l], n)
    shared["s1_cols"] = s1c

    maps = []
    for c in range(NCORES):
        b = c // 4
        j = c % 4
        xs = np.ascontiguousarray(np.asarray(x[b], dtype=f32)[tok_ids(c), :].T)
        m = dict(shared)
        m["x_in"] = xs
        oh0 = np.zeros((1, 8), f32)
        oh0[0, j] = 1.0
        oh1 = np.zeros((1, 8), f32)
        oh1[0, 7 - j] = 1.0
        m["oh0"] = oh0
        m["oh1"] = oh1
        kvb = np.full((128, S3_NBIAS), -1e30, f32)
        bi = 0
        for sl in (0, 1):
            n_off = 4 if sl == 0 else 7
            gq = j if sl == 0 else 7 - j
            for ch in range(n_off):
                kvb[:, bi] = -12.0 if ch < gq else -1e30
                bi += 1
            kvb[:, bi] = -12.0
            bi += 1
        m["kvb_cols"] = kvb
        maps.append(m)
    return maps


def kernel(**inputs):
    maps = host_inputs(inputs)
    nc = build_program()
    res = run_bass_kernel_spmd(nc, maps, core_ids=list(range(NCORES)))
    out = np.empty((NB, L, D), np.float32)
    for c in range(NCORES):
        out[c // 4, tok_ids(c), :] = res.results[c]["out_fm"].T
    return out


# version 02
# speedup vs baseline: 1.1571x; 1.1571x over previous
import numpy as np
from contextlib import ExitStack
import concourse.bass as bass
import concourse.mybir as mybir
from concourse.bass_utils import run_bass_kernel_spmd

F32 = mybir.dt.float32
BF16 = mybir.dt.bfloat16
I32 = mybir.dt.int32
AF = mybir.ActivationFunctionType
ALU = mybir.AluOpType

NB, L, D = 2, 4096, 4096
NCORES = 8
CH = 512
T = 1024
NKT = D // 128
DIM, DFF, AD = 512, 2048, 4
NH, HD = 32, 128
EPS = 1e-5
f32 = np.float32


class Sem:
    def __init__(self, ctx, nc, name):
        self.h = ctx.enter_context(nc.semaphore(name))
        self.n = 0

    def inc(self, instr, k=1):
        instr.then_inc(self.h, k)
        self.n += k
        return self.n

    def wait(self, eng, n=None):
        eng.wait_ge(self.h, self.n if n is None else n)


class SV:

    def __init__(self, p):
        self._p = p

    def __getattr__(self, name):
        p = self._p
        f = getattr(p.ve, name)

        def g(*a, **k):
            ins = f(*a, **k)
            n = p.s_ve.inc(ins)
            p.ve.wait_ge(p.s_ve.h, n)
            return ins
        return g


class P:
    def __init__(self, nc, ctx):
        self.nc = nc
        self.ctx = ctx
        self.te, self.ve, self.se = nc.tensor, nc.vector, nc.scalar
        self.gp, self.sy = nc.gpsimd, nc.sync
        self.s_ld = Sem(ctx, nc, "s_ld")
        self.s_gd = Sem(ctx, nc, "s_gd")
        self.s_te = Sem(ctx, nc, "s_te")
        self.s_ve = Sem(ctx, nc, "s_ve")
        self.s_se = Sem(ctx, nc, "s_se")
        self.s_gp = Sem(ctx, nc, "s_gp")
        self.s_cc = Sem(ctx, nc, "s_cc")
        self.s_x = [Sem(ctx, nc, "s_x0"), Sem(ctx, nc, "s_x1")]
        self.s_sto = [Sem(ctx, nc, "s_sto0"), Sem(ctx, nc, "s_sto1")]
        self.s_wt = Sem(ctx, nc, "s_wt")
        self.s_const = Sem(ctx, nc, "s_const")
        self.s_z = Sem(ctx, nc, "s_z")
        self.s_zg = Sem(ctx, nc, "s_zg")
        self.s_ug = [Sem(ctx, nc, "s_ug0"), Sem(ctx, nc, "s_ug1")]
        self.s_kv = [Sem(ctx, nc, "s_kv0"), Sem(ctx, nc, "s_kv1")]
        self.s_q = [Sem(ctx, nc, "s_q0"), Sem(ctx, nc, "s_q1")]
        self.s_w2 = Sem(ctx, nc, "s_w2")
        self.s_bar = Sem(ctx, nc, "s_bar")
        self.engs = [self.te, self.ve, self.se, self.gp, self.sy]

    def sbuf(self, name, shape, dt):
        return self.ctx.enter_context(self.nc.sbuf_tensor(name, shape, dt))

    def psum(self, name, shape, dt=F32):
        return self.ctx.enter_context(self.nc.psum_tensor(name, shape, dt))


def emit_stats(p, src_dram, w, double_ln, keep_cast=False):
    v, se, te, sy = p.ve, p.se, p.te, p.sy
    xin, xbs, sqs = w["xin2"], w["xbs2"], w["sqs2"]
    pb = w["pb"]
    p.s_se.wait(te)
    p.s_ve.wait(te)
    ld_at = [p.s_x[0].n, p.s_x[1].n]
    ve_at = [p.s_ve.n] * 2
    se_at = [p.s_se.n] * 2
    te_done = [p.s_te.n] * 2
    for kt in range(NKT):
        s = kt % 2
        p.s_ve.wait(sy, ve_at[s])
        ld_at[s] = p.s_x[s].inc(
            sy.dma_start(out=xin[s][:], in_=src_dram[kt * 128:(kt + 1) * 128, :]), 16)
        p.s_x[s].wait(v, ld_at[s])
        p.s_te.wait(v, te_done[s])
        xcast = w["mirror"][:, kt * T:(kt + 1) * T] if keep_cast else xbs[s][:]
        ve_at[s] = p.s_ve.inc(v.tensor_copy(xcast, xin[s][:]))
        p.s_ve.wait(se, ve_at[s])
        p.s_te.wait(se, te_done[s])
        se_at[s] = p.s_se.inc(se.activation(sqs[s][:], xcast, AF.Square))
        p.s_se.wait(te, se_at[s])
        st, sp = kt == 0, kt == NKT - 1
        te.matmul(pb[0][0:1, :], w["ones_bf"][:], xcast[:, 0:512],
                  start=st, stop=sp, skip_group_check=True)
        te.matmul(pb[1][0:1, :], w["ones_bf"][:], xcast[:, 512:1024],
                  start=st, stop=sp, skip_group_check=True)
        te.matmul(pb[2][0:1, :], w["ones_bf"][:], sqs[s][:, 0:512],
                  start=st, stop=sp, skip_group_check=True)
        ins = te.matmul(pb[3][0:1, :], w["ones_bf"][:], sqs[s][:, 512:1024],
                        start=st, stop=sp, skip_group_check=True)
        te_done[s] = p.s_te.inc(ins)
    mu, ex2, var, tmp = w["r0"], w["r1"], w["r2"], w["r3"]
    p.s_te.wait(v)
    inv_d = float(1.0 / D)
    p.s_ve.inc(v.tensor_scalar_mul(mu[0:1, 0:512], pb[0][0:1, :], inv_d))
    v.tensor_scalar_mul(mu[0:1, 512:1024], pb[1][0:1, :], inv_d)
    v.tensor_scalar_mul(ex2[0:1, 0:512], pb[2][0:1, :], inv_d)
    v.tensor_scalar_mul(ex2[0:1, 512:1024], pb[3][0:1, :], inv_d)
    v.tensor_mul(tmp[0:1, :], mu[0:1, :], mu[0:1, :])
    v.tensor_sub(var[0:1, :], ex2[0:1, :], tmp[0:1, :])
    v.tensor_scalar_add(tmp[0:1, :], var[0:1, :], float(EPS))
    p.s_ve.inc(v.reciprocal(tmp[0:1, :], tmp[0:1, :]))
    p.s_ve.wait(se)
    p.s_se.inc(se.activation(w["rstd"][0:1, :], tmp[0:1, :], AF.Sqrt))
    scale = w["rstd"]
    if double_ln:
        p.s_se.wait(v)
        v.tensor_mul(ex2[0:1, :], scale[0:1, :], scale[0:1, :])
        v.tensor_mul(ex2[0:1, :], ex2[0:1, :], var[0:1, :])
        v.tensor_scalar_add(ex2[0:1, :], ex2[0:1, :], float(EPS))
        p.s_ve.inc(v.reciprocal(ex2[0:1, :], ex2[0:1, :]))
        p.s_ve.wait(se)
        p.s_se.inc(se.activation(ex2[0:1, :], ex2[0:1, :], AF.Sqrt))
        p.s_se.wait(v)
        p.s_ve.inc(v.tensor_mul(w["sc_row"][0:1, :], scale[0:1, :], ex2[0:1, :]))
        scale = w["sc_row"]
    bcast_rows(p, [(mu, w["mu_b"]), (scale, w["sc_b"])], w)


def bcast_rows(p, pairs, w):
    v, te = p.ve, p.te
    bank = w["pb"][4]
    for row, dstb in pairs:
        for th in range(2):
            p.s_se.wait(te)
            p.s_ve.wait(te)
            p.s_te.inc(te.matmul(bank[:], w["ones_row"][:],
                                 row[0:1, th * 512:(th + 1) * 512],
                                 start=True, stop=True, skip_group_check=True))
            p.s_te.wait(v)
            p.s_ve.inc(v.tensor_copy(dstb[:, th * 512:(th + 1) * 512], bank[:]))


def emit_mirror(p, src_dram, w, gcols, bcols):
    v, sy = p.ve, p.sy
    xin, t1 = w["xin2"], w["t1"]
    mirror = w["mirror"]
    ld_at = [p.s_x[0].n, p.s_x[1].n]
    ve_at = [p.s_ve.n] * 2
    for kt in range(NKT):
        s = kt % 2
        p.s_ve.wait(sy, ve_at[s])
        ld_at[s] = p.s_x[s].inc(
            sy.dma_start(out=xin[s][:], in_=src_dram[kt * 128:(kt + 1) * 128, :]), 16)
        p.s_x[s].wait(v, ld_at[s])
        dst = mirror[:, kt * T:(kt + 1) * T]
        v.tensor_sub(t1[:], xin[s][:], w["mu_b"][:])
        if gcols is not None:
            v.tensor_mul(t1[:], t1[:], w["sc_b"][:])
            ve_at[s] = p.s_ve.inc(
                v.tensor_scalar(dst, t1[:], gcols[:, kt:kt + 1],
                                bcols[:, kt:kt + 1], op0=ALU.mult, op1=ALU.add))
        else:
            ve_at[s] = p.s_ve.inc(v.tensor_mul(dst, t1[:], w["sc_b"][:]))


_FFN_N = [0]


def emit_ffn(p, x_dram, out_dram, wf, w, last_sem_holder):
    v, se, te, gp, sy = p.ve, p.se, p.te, p.gp, p.sy
    nc = p.nc
    mm = te.matmul
    fctx = p.ctx
    sctx = ExitStack()
    _FFN_N[0] += 1
    sfx = str(_FFN_N[0])
    w["mirror"] = sctx.enter_context(nc.sbuf_tensor("ffn_mirror" + sfx, [128, NKT * T], BF16))
    w["ustage"] = sctx.enter_context(nc.sbuf_tensor("ffn_ustage" + sfx, [128, T], BF16))
    w["yg"] = sctx.enter_context(nc.sbuf_tensor("ffn_yg" + sfx, [128, T], F32))
    emit_stats(p, x_dram, w, double_ln=True)
    bcast_rows(p, [(w["rstd"], w["sc2_b"])], w)
    emit_mirror(p, x_dram, w, wf["gcols"], wf["bcols"])
    mirror = w["mirror"]
    u_dram = wf["u_dram"]
    ustage = w["ustage"]
    p.s_ve.wait(te)
    banks = [w["pb"][0], w["pb"][1], w["pb"][2], w["pb"][3]]
    ve_evac = [0, 0]
    gd_at = [p.s_ug[0].n, p.s_ug[1].n]
    te_at = [0, 0, 0, 0]
    for pi in range(64):
        kt, half = pi // 2, pi % 2
        slot = pi % 2
        for th in range(2):
            b = banks[2 * slot + th]
            p.s_ve.wait(te, ve_evac[slot])
            rhs = mirror[half * 64:half * 64 + 64,
                         kt * T + th * 512:kt * T + (th + 1) * 512]
            lhsA = wf["A_sb"][half * 64:half * 64 + 64, :]
            te_at[2 * slot + th] = p.s_te.inc(
                mm(b[0:64, :], lhsA, rhs, start=True, stop=True,
                   skip_group_check=True))
        p.s_te.wait(v, te_at[2 * slot + 1])
        p.s_ug[slot].wait(v, gd_at[slot])
        v.tensor_copy(ustage[slot * 64:slot * 64 + 64, 0:512], banks[2 * slot][0:64, :])
        ve_evac[slot] = p.s_ve.inc(
            v.tensor_copy(ustage[slot * 64:slot * 64 + 64, 512:1024],
                          banks[2 * slot + 1][0:64, :]))
        p.s_ve.wait(gp, ve_evac[slot])
        gd_at[slot] = p.s_ug[slot].inc(
            gp.dma_start(out=u_dram[pi * 64:(pi + 1) * 64, :],
                         in_=ustage[slot * 64:slot * 64 + 64, :]), 16)
    u2v = u_dram.rearrange("(pp bb) t -> bb pp t", bb=64)
    x2v = x_dram.rearrange("(q bb) t -> bb q t", bb=64)
    o2v = out_dram.rearrange("(q bb) t -> bb q t", bb=64)
    yg = w["yg"]
    xp2 = w["xin2"][0]
    p.s_ve.wait(te)
    sto_tot = (p.s_ug[0].n, p.s_ug[1].n)
    u2_gd = [p.s_gd.n]
    te2_at = [p.s_te.n]
    se2_at = [p.s_se.n] * 4
    ve2_at = [p.s_ve.n]
    st2_at = [p.s_sto[0].n]
    for pair in range(32):
        b0 = 2 * pair
        p.s_ug[0].wait(gp, sto_tot[0])
        p.s_ug[1].wait(gp, sto_tot[1])
        p.s_te.wait(gp, te2_at[0])
        p.s_gd.inc(gp.dma_start(out=ustage[0:64, :], in_=u2v[b0]), 16)
        u2_gd[0] = p.s_gd.inc(gp.dma_start(out=ustage[64:128, :], in_=u2v[b0 + 1]), 16)
        p.s_ve.wait(sy, ve2_at[0])
        p.s_sto[0].wait(sy, st2_at[0])
        p.s_x[0].inc(sy.dma_start(out=xp2[0:64, :], in_=x2v[b0]), 16)
        x_ld = p.s_x[0].inc(sy.dma_start(out=xp2[64:128, :], in_=x2v[b0 + 1]), 16)
        p.s_gd.wait(te, u2_gd[0])
        for sub in range(2):
            for th in range(2):
                b = banks[2 * sub + th]
                p.s_se.wait(te, se2_at[2 * sub + th])
                ins = mm(b[0:64, :], wf["B_sb"][sub * 64:sub * 64 + 64, :],
                         ustage[sub * 64:sub * 64 + 64, th * 512:(th + 1) * 512],
                         start=True, stop=True, skip_group_check=True)
            te2_at[0] = p.s_te.inc(ins)
            p.s_te.wait(se, te2_at[0])
            p.s_ve.wait(se, ve2_at[0])
            se.activation(yg[sub * 64:sub * 64 + 64, 0:512],
                          banks[2 * sub][0:64, :], AF.Gelu)
            se2_at[2 * sub] = se2_at[2 * sub + 1] = p.s_se.inc(
                se.activation(yg[sub * 64:sub * 64 + 64, 512:1024],
                              banks[2 * sub + 1][0:64, :], AF.Gelu))
        p.s_se.wait(v)
        p.s_x[0].wait(v, x_ld)
        t1 = w["t1"]
        v.tensor_sub(t1[:], xp2[:], w["mu_b"][:])
        v.tensor_mul(t1[:], t1[:], w["sc2_b"][:])
        v.tensor_add(t1[:], t1[:], xp2[:])
        ve2_at[0] = p.s_ve.inc(v.scalar_tensor_tensor(
            xp2[:], yg[:], wf["biasp_cols"][:, pair:pair + 1], t1[:],
            op0=ALU.add, op1=ALU.add))
        p.s_ve.wait(sy, ve2_at[0])
        p.s_sto[0].inc(sy.dma_start(out=o2v[b0], in_=xp2[0:64, :]), 16)
        st2_at[0] = p.s_sto[0].inc(sy.dma_start(out=o2v[b0 + 1], in_=xp2[64:128, :]), 16)
    last_sem_holder["sem"] = p.s_sto[0]
    last_sem_holder["val"] = p.s_sto[0].n
    barrier(p)
    sctx.close()


def sync_all(p, eng):
    for s in (p.s_ld, p.s_gd, p.s_te, p.s_ve, p.s_se, p.s_gp, p.s_cc,
              p.s_x[0], p.s_x[1], p.s_sto[0], p.s_sto[1], p.s_wt,
              p.s_const, p.s_z, p.s_zg, p.s_ug[0], p.s_ug[1],
              p.s_kv[0], p.s_kv[1], p.s_q[0], p.s_q[1], p.s_w2):
        s.wait(eng)


def barrier(p):
    for eng in p.engs:
        sync_all(p, eng)
        p.s_bar.inc(eng.nop())
    for eng in p.engs:
        p.s_bar.wait(eng)


def small_stats(p, w, src_sb, col0, gcols_pair, dst_sb, dcol0):
    v, se, te = p.ve, p.se, p.te
    pb = w["pb"]
    sync_all(p, te)
    sync_all(p, se)
    sqt = w["t1"]
    for ct in range(4):
        sl = src_sb[:, col0 + ct * T:col0 + (ct + 1) * T]
        p.s_ve.wait(se)
        p.s_te.wait(se)
        p.s_se.inc(se.activation(sqt[:], sl, AF.Square))
        p.s_se.wait(te)
        st, sp = ct == 0, ct == 3
        for th in range(2):
            te.matmul(pb[th][0:1, :], w["ones_f32"][:], sl[:, th * 512:(th + 1) * 512],
                      start=st, stop=sp, skip_group_check=True)
            ins = te.matmul(pb[2 + th][0:1, :], w["ones_f32"][:],
                            sqt[:, th * 512:(th + 1) * 512],
                            start=st, stop=sp, skip_group_check=True)
        p.s_te.inc(ins)
    mu, ex2, var, tmp = w["r0"], w["r1"], w["r2"], w["r3"]
    p.s_te.wait(v)
    inv_d = float(1.0 / DIM)
    v.tensor_scalar_mul(mu[0:1, 0:512], pb[0][0:1, :], inv_d)
    v.tensor_scalar_mul(mu[0:1, 512:1024], pb[1][0:1, :], inv_d)
    v.tensor_scalar_mul(ex2[0:1, 0:512], pb[2][0:1, :], inv_d)
    v.tensor_scalar_mul(ex2[0:1, 512:1024], pb[3][0:1, :], inv_d)
    v.tensor_mul(tmp[0:1, :], mu[0:1, :], mu[0:1, :])
    v.tensor_sub(var[0:1, :], ex2[0:1, :], tmp[0:1, :])
    v.tensor_scalar_add(tmp[0:1, :], var[0:1, :], float(EPS))
    p.s_ve.inc(v.reciprocal(tmp[0:1, :], tmp[0:1, :]))
    p.s_ve.wait(se)
    p.s_se.inc(se.activation(w["rstd"][0:1, :], tmp[0:1, :], AF.Sqrt))
    bcast_rows(p, [(mu, w["mu_b"]), (w["rstd"], w["sc_b"])], w)
    gc, bc = gcols_pair
    for ct in range(4):
        sl = src_sb[:, col0 + ct * T:col0 + (ct + 1) * T]
        dst = dst_sb[:, dcol0 + ct * T:dcol0 + (ct + 1) * T]
        v.tensor_sub(w["t1"][:], sl, w["mu_b"][:])
        v.tensor_mul(w["t1"][:], w["t1"][:], w["sc_b"][:])
        p.s_ve.inc(v.tensor_scalar(dst, w["t1"][:], gc[:, ct:ct + 1],
                                   bc[:, ct:ct + 1], op0=ALU.mult, op1=ALU.add))


def emit_s1(p, x_dram, out_dram, ws, w, dram, dbg):
    nc = p.nc
    v, se, te, gp, sy = p.ve, p.se, p.te, p.gp, p.sy
    mm = te.matmul

    octx = ExitStack()
    h_sb = octx.enter_context(nc.sbuf_tensor("h_sb", [128, 4 * T], F32))
    y2_sb = octx.enter_context(nc.sbuf_tensor("y2_sb", [128, 4 * T], BF16))
    wbar = octx.enter_context(nc.sbuf_tensor("wbar", [128, 4], F32))

    sctx = ExitStack()
    w["mirror"] = sctx.enter_context(nc.sbuf_tensor("s1_mirror", [128, NKT * T], BF16))
    win_sb = sctx.enter_context(nc.sbuf_tensor("win_sb", [128, NKT * DIM], BF16))
    emit_stats(p, x_dram, w, double_ln=False, keep_cast=True)
    mirror = w["mirror"]
    if True:
        for q in range(4):
            p.s_wt.inc(gp.dma_start(
                out=win_sb[:, q * 8 * DIM:(q + 1) * 8 * DIM],
                in_=dram["win_img"][:, q * 8 * DIM:(q + 1) * 8 * DIM]), 16)
        p.s_wt.wait(te)
        p.s_ve.wait(te)
        for ct in range(4):
            p.s_ve.wait(te)
            for kt in range(NKT):
                ins = mm(w["pb"][5][0:128, 0:1],
                         win_sb[:, kt * DIM + ct * 128:kt * DIM + (ct + 1) * 128],
                         w["ones_bf"][:], start=kt == 0, stop=kt == NKT - 1,
                         skip_group_check=True)
            p.s_te.inc(ins)
            p.s_te.wait(v)
            wb_at = p.s_ve.inc(v.tensor_copy(wbar[:, ct:ct + 1], w["pb"][5][0:128, 0:1]))
            v.wait_ge(p.s_ve.h, wb_at)
        p.s_ve.wait(v)
        p.s_ve.inc(v.tensor_mul(w["r1"][0:1, :], w["r0"][0:1, :], w["rstd"][0:1, :]))
        bcast_rows(p, [(w["r1"], w["mu_b"]), (w["rstd"], w["sc_b"])], w)

        embank = [w["pb"][0], w["pb"][1]]
        ve_emb = [p.s_ve.n] * 2
        for ct in range(4):
            for th in range(2):
                slot = (2 * ct + th) % 2
                bank = embank[slot]
                p.s_ve.wait(te, ve_emb[slot])
                for kt in range(NKT):
                    ins = mm(bank[:, :],
                             win_sb[:, kt * DIM + ct * 128:kt * DIM + (ct + 1) * 128],
                             mirror[:, kt * T + th * 512:kt * T + (th + 1) * 512],
                             start=kt == 0, stop=kt == NKT - 1, skip_group_check=True)
                p.s_te.inc(ins)
                p.s_te.wait(v)
                dst = h_sb[:, ct * T + th * 512:ct * T + (th + 1) * 512]
                mu_sl = w["mu_b"][:, th * 512:(th + 1) * 512]
                sc_sl = w["sc_b"][:, th * 512:(th + 1) * 512]
                v.scalar_tensor_tensor(w["t1"][:, 0:512], mu_sl, wbar[:, ct:ct + 1],
                                       bank[:, :], op0=ALU.mult, op1=ALU.subtract)
                v.scalar_tensor_tensor(w["t1"][:, 512:1024], w["t1"][:, 0:512], -1.0,
                                       sc_sl, op0=ALU.mult, op1=ALU.mult)
                ve_emb[slot] = p.s_ve.inc(
                    v.tensor_scalar(dst, w["t1"][:, 512:1024],
                                    ws["bin_cols"][:, ct:ct + 1], None, op0=ALU.add))
        if dbg is not None and "h_embed" in dbg:
            emit_dump(p, dbg["h_embed"], h_sb, 4)
    barrier(p)
    sctx.close()

    sctx = ExitStack()
    y_sb = sctx.enter_context(nc.sbuf_tensor("y_sb", [128, 4 * T], F32))
    u_sb = sctx.enter_context(nc.sbuf_tensor("u_sb", [128, 16 * 512], BF16))
    w1_sb = sctx.enter_context(nc.sbuf_tensor("w1_sb", [128, 4 * DFF], BF16))
    w2_sb = sctx.enter_context(nc.sbuf_tensor("w2_sb", [128, 16 * 512], BF16))
    vre = sctx.enter_context(nc.sbuf_tensor("vre", [128, T], F32))
    vim = sctx.enter_context(nc.sbuf_tensor("vim", [128, T], F32))
    cost = sctx.enter_context(nc.sbuf_tensor("cost", [128, 4 * T], F32))
    sint = sctx.enter_context(nc.sbuf_tensor("sint", [128, 4 * T], F32))
    xmre = sctx.enter_context(nc.sbuf_tensor("xmre", [128, T], F32))
    xmim = sctx.enter_context(nc.sbuf_tensor("xmim", [128, T], F32))
    rtile4 = sctx.enter_context(nc.sbuf_tensor("rtile4", [128, 4 * 512], F32))
    sc1 = sctx.enter_context(nc.sbuf_tensor("sc1", [128, 72], F32))
    zsb = sctx.enter_context(nc.sbuf_tensor("zsb", [128, 16], F32))
    zg_sb = sctx.enter_context(nc.sbuf_tensor("zg_sb", [16, 512], F32))
    zt_sb = sctx.enter_context(nc.sbuf_tensor("zt_sb", [128, 64], F32))
    if True:
        for lay in range(AD):
            sync_all(p, gp)
            for q in range(4):
                p.s_wt.inc(gp.dma_start(
                    out=w1_sb[:, q * DFF:(q + 1) * DFF],
                    in_=dram["w1_img"][lay, :, q * DFF:(q + 1) * DFF]), 16)
                p.s_wt.inc(gp.dma_start(
                    out=w2_sb[:, q * 2048:(q + 1) * 2048],
                    in_=dram["w2_img"][lay, :, q * 2048:(q + 1) * 2048]), 16)
            small_stats(p, w, h_sb, 0,
                        (ws["ln1_w"][lay], ws["ln1_b"][lay]), y_sb, 0)
            sync_all(p, se)
            for ct in range(4):
                p.s_se.inc(se.activation(y_sb[:, ct * T:(ct + 1) * T],
                                         y_sb[:, ct * T:(ct + 1) * T], AF.Silu))
            if dbg is not None and f"y{lay}" in dbg:
                emit_dump(p, dbg[f"y{lay}"], y_sb, 4)
            emit_sconv(p, w, ws, lay, y_sb, h_sb, vre, vim, cost, sint, xmre,
                       xmim, rtile4, sc1, zsb, zg_sb, zt_sb, dram, dbg)
            if dbg is not None and f"h_mid{lay}" in dbg:
                emit_dump(p, dbg[f"h_mid{lay}"], h_sb, 4)
            small_stats(p, w, h_sb, 0,
                        (ws["ln2_w"][lay], ws["ln2_b"][lay]), y2_sb, 0)
            for th in range(2):
                sync_all(p, te)
                for ft in range(16):
                    bank = w["pb"][ft % 2]
                    for ct in range(4):
                        ins = mm(bank[:, :],
                                 w1_sb[:, ct * DFF + ft * 128:ct * DFF + (ft + 1) * 128],
                                 y2_sb[:, ct * T + th * 512:ct * T + (th + 1) * 512],
                                 start=ct == 0, stop=ct == 3, skip_group_check=True)
                    p.s_te.inc(ins)
                    p.s_te.wait(se)
                    p.s_se.inc(se.activation(
                        u_sb[:, ft * 512:(ft + 1) * 512],
                        bank[:, :], AF.Silu, bias=ws["b1_cols"][lay][:, ft:ft + 1]))
                    if ft >= 1:
                        p.s_se.wait(te)
                sync_all(p, te)
                for ct in range(4):
                    bank = w["pb"][2 + ct % 2]
                    for ft in range(16):
                        ins = mm(bank[:, :],
                                 w2_sb[:, ft * 512 + ct * 128:ft * 512 + (ct + 1) * 128],
                                 u_sb[:, ft * 512:(ft + 1) * 512],
                                 start=ft == 0, stop=ft == 15, skip_group_check=True)
                    p.s_te.inc(ins)
                    p.s_te.wait(v)
                    dst = h_sb[:, ct * T + th * 512:ct * T + (th + 1) * 512]
                    p.s_ve.inc(v.scalar_tensor_tensor(
                        dst, bank[:, :], ws["b2_cols"][lay][:, ct:ct + 1], dst,
                        op0=ALU.add, op1=ALU.add))
                    p.s_ve.wait(te)
            if dbg is not None and f"h{lay}" in dbg:
                emit_dump(p, dbg[f"h{lay}"], h_sb, 4)

    small_stats(p, w, h_sb, 0, (ws["lnl_w"], ws["lnl_b"]), y2_sb, 0)
    barrier(p)
    sctx.close()

    sctx = ExitStack()
    wout_sb = sctx.enter_context(nc.sbuf_tensor("wout_sb", [128, 4 * D], BF16))
    if True:
        sync_all(p, gp)
        for q in range(4):
            p.s_wt.inc(gp.dma_start(
                out=wout_sb[:, q * 4096:(q + 1) * 4096],
                in_=dram["wout_img"][:, q * 4096:(q + 1) * 4096]), 16)
        p.s_wt.wait(te)
        xin = w["xin2"]
        obank = [w["pb"][0], w["pb"][1]]
        ld_at = [p.s_x[0].n, p.s_x[1].n]
        st_at = [p.s_sto[0].n, p.s_sto[1].n]
        ve_at = [p.s_ve.n] * 2
        last_stt = [p.s_ve.n]
        for ot in range(NKT):
            s = ot % 2
            p.s_ve.wait(sy, ve_at[s])
            p.s_sto[s].wait(sy, st_at[s])
            ld_at[s] = p.s_x[s].inc(
                sy.dma_start(out=xin[s][:], in_=x_dram[ot * 128:(ot + 1) * 128, :]), 16)
            for th in range(2):
                bank = obank[th]
                p.s_ve.wait(te, last_stt[0])
                for ct in range(4):
                    ins = mm(bank[:, :],
                             wout_sb[:, ct * 4096 + ot * 128:ct * 4096 + (ot + 1) * 128],
                             y2_sb[:, ct * T + th * 512:ct * T + (th + 1) * 512],
                             start=ct == 0, stop=ct == 3, skip_group_check=True)
                p.s_te.inc(ins)
            p.s_te.wait(v)
            p.s_x[s].wait(v, ld_at[s])
            v.scalar_tensor_tensor(xin[s][:, 0:512], obank[0][:, :],
                                   ws["bout_cols"][:, ot:ot + 1], xin[s][:, 0:512],
                                   op0=ALU.add, op1=ALU.add)
            ve_at[s] = p.s_ve.inc(v.scalar_tensor_tensor(
                xin[s][:, 512:1024], obank[1][:, :], ws["bout_cols"][:, ot:ot + 1],
                xin[s][:, 512:1024], op0=ALU.add, op1=ALU.add))
            last_stt[0] = ve_at[s]
            p.s_ve.wait(sy, ve_at[s])
            st_at[s] = p.s_sto[s].inc(
                sy.dma_start(out=out_dram[ot * 128:(ot + 1) * 128, :],
                             in_=xin[s][:]), 16)
    barrier(p)
    sctx.close()
    octx.close()


def _cmul_sq(v, c, s, ta, tb):
    v.tensor_mul(ta, c, c)
    v.tensor_mul(tb, s, s)
    v.tensor_mul(s, c, s)
    v.tensor_scalar_mul(s, s, 2.0)
    v.tensor_sub(c, ta, tb)


def emit_sconv(p, w, ws, lay, y_sb, h_sb, vre, vim, cost, sint, xmre, xmim,
               rtile4, sc1, zsb, zg_sb, zt_sb, dram, dbg):
    nc = p.nc
    v, se, te, gp, sy = p.ve, p.se, p.te, p.gp, p.sy
    sv = SV(p)
    X = mybir.AxisListType.X
    C = lambda i: sc1[:, i:i + 1]
    MAG, RR, CTH, STH = 12, 13, 14, 15
    EC, ES, TA, TB, NEG = 16, 17, 18, 19, 20
    CG0, SG0, CG1, SG1 = 21, 22, 23, 24
    APC, APS = 25, 33
    VEC, VES = 41, 50
    T8 = 59
    phz_re, phz_im = ws["phz_re"][lay], ws["phz_im"][lay]
    lc_re, lc_im = ws["lc_re"][lay], ws["lc_im"][lay]
    scl = ws["scl"][lay]

    for ct in range(4):
        sync_all(p, v)
        sync_all(p, se)
        sv.tensor_mul(C(TA), phz_re[:, ct:ct + 1], phz_re[:, ct:ct + 1])
        sv.tensor_mul(C(TB), phz_im[:, ct:ct + 1], phz_im[:, ct:ct + 1])
        sv.tensor_add(C(MAG), C(TA), C(TB))
        sv.tensor_scalar_add(C(MAG), C(MAG), 1e-30)
        p.s_ve.wait(se)
        sq_at = p.s_se.inc(se.activation(C(MAG), C(MAG), AF.Sqrt))
        p.s_se.wait(se, sq_at)
        p.s_se.inc(se.activation(C(RR), C(MAG), AF.Tanh))
        p.s_se.wait(v)
        sv.reciprocal(C(TA), C(MAG))
        sv.tensor_mul(C(CTH), phz_re[:, ct:ct + 1], C(TA))
        sv.tensor_mul(C(STH), phz_im[:, ct:ct + 1], C(TA))
        sv.tensor_mul(C(TA), C(CTH), lc_re[:, ct:ct + 1])
        sv.tensor_mul(C(TB), C(STH), lc_im[:, ct:ct + 1])
        sv.tensor_sub(C(4 + ct), C(TA), C(TB))
        sv.tensor_mul(C(TA), C(STH), lc_re[:, ct:ct + 1])
        sv.tensor_mul(C(TB), C(CTH), lc_im[:, ct:ct + 1])
        sv.tensor_add(C(8 + ct), C(TA), C(TB))
        sv.tensor_scalar_mul(C(8 + ct), C(8 + ct), -1.0)
        sv.tensor_copy(C(ct), C(RR))
        sv.tensor_copy(C(EC), C(CTH))
        sv.tensor_copy(C(ES), C(STH))
        for _ in range(9):
            sv.tensor_mul(C(ct), C(ct), C(ct))
            _cmul_sq(sv, C(EC), C(ES), C(TA), C(TB))
        sv.memset(C(APC), 1.0)
        sv.memset(C(APS), 0.0)
        for k in range(7):
            sv.tensor_mul(C(TA), C(APC + k), C(EC))
            sv.tensor_mul(C(TB), C(APS + k), C(ES))
            sv.tensor_sub(C(APC + k + 1), C(TA), C(TB))
            sv.tensor_mul(C(TA), C(APS + k), C(EC))
            sv.tensor_mul(C(TB), C(APC + k), C(ES))
            sv.tensor_add(C(APS + k + 1), C(TA), C(TB))
        for (cg, sg, ohb) in ((CG0, SG0, w["oh0_b"]), (CG1, SG1, w["oh1_b"])):
            sv.tensor_mul(sc1[:, T8:T8 + 8], sc1[:, APC:APC + 8], ohb[:])
            sv.tensor_reduce(C(cg), sc1[:, T8:T8 + 8], X, ALU.add)
            sv.tensor_mul(sc1[:, T8:T8 + 8], sc1[:, APS:APS + 8], ohb[:])
            sv.tensor_reduce(C(sg), sc1[:, T8:T8 + 8], X, ALU.add)
        sv.memset(xmre[:, 0:1], 1.0)
        sv.memset(xmim[:, 0:1], 0.0)
        sv.tensor_copy(C(EC), C(CTH))
        sv.tensor_copy(C(ES), C(STH))
        n = 1
        while n < 512:
            sv.tensor_scalar_mul(C(NEG), C(ES), -1.0)
            sv.tensor_scalar(vre[:, 0:n], xmre[:, 0:n], C(EC), None, op0=ALU.mult)
            sv.scalar_tensor_tensor(xmre[:, n:2 * n], xmim[:, 0:n], C(NEG),
                                    vre[:, 0:n], op0=ALU.mult, op1=ALU.add)
            sv.tensor_scalar(vre[:, 0:n], xmim[:, 0:n], C(EC), None, op0=ALU.mult)
            sv.scalar_tensor_tensor(xmim[:, n:2 * n], xmre[:, 0:n], C(ES),
                                    vre[:, 0:n], op0=ALU.mult, op1=ALU.add)
            _cmul_sq(sv, C(EC), C(ES), C(TA), C(TB))
            n *= 2
        for (sl, cg, sg) in ((0, CG0, SG0), (1, CG1, SG1)):
            cd = cost[:, ct * T + sl * 512:ct * T + (sl + 1) * 512]
            sd = sint[:, ct * T + sl * 512:ct * T + (sl + 1) * 512]
            sv.tensor_scalar_mul(C(NEG), C(sg), -1.0)
            sv.tensor_scalar(vre[:, 0:512], xmre[:, 0:512], C(cg), None, op0=ALU.mult)
            sv.scalar_tensor_tensor(cd, xmim[:, 0:512], C(NEG), vre[:, 0:512],
                                   op0=ALU.mult, op1=ALU.add)
            sv.tensor_scalar(vre[:, 0:512], xmim[:, 0:512], C(cg), None, op0=ALU.mult)
            sv.scalar_tensor_tensor(sd, xmre[:, 0:512], C(sg), vre[:, 0:512],
                                   op0=ALU.mult, op1=ALU.add)
        sv.memset(rtile4[:, ct * 512:(ct + 1) * 512], 1.0)
        sv.tensor_scalar(rtile4[:, ct * 512:(ct + 1) * 512],
                         rtile4[:, ct * 512:(ct + 1) * 512], C(RR), None, op0=ALU.mult)
        ysl = y_sb[:, ct * T:(ct + 1) * T]
        sv.tensor_mul(xmre[:], ysl, cost[:, ct * T:(ct + 1) * T])
        sv.tensor_mul(xmim[:], ysl, sint[:, ct * T:(ct + 1) * T])
        for sl in range(2):
            for comp, (vv, xm) in enumerate(((vre, xmre), (vim, xmim))):
                sv.tensor_tensor_scan(
                    vv[:, sl * 512:(sl + 1) * 512],
                    rtile4[:, ct * 512:(ct + 1) * 512],
                    xm[:, sl * 512:(sl + 1) * 512], 0.0,
                    op0=ALU.mult, op1=ALU.add)
                sv.tensor_copy(
                    zsb[:, ct * 4 + sl * 2 + comp:ct * 4 + sl * 2 + comp + 1],
                    vv[:, sl * 512 + 511:sl * 512 + 512])
        if dbg is not None and "scdbg" in dbg and lay == 0 and ct == 0:
            dd = dbg["scdbg"]
            p.s_ve.wait(sy)
            for bi, (tile, wdt) in enumerate((
                    (cost[:, 0:T], T), (sint[:, 0:T], T), (xmre[:], T),
                    (xmim[:], T), (vre[:], T), (vim[:], T),
                    (rtile4[:, 0:512], 512), (sc1[:, 0:72], 72))):
                p.s_z.inc(sy.dma_start(out=dd[bi * 128:bi * 128 + 128, 0:wdt],
                                       in_=tile), 16)
            p.s_z.wait(v)
    sync_all(p, te)
    p.s_te.inc(te.transpose(w["pbt"][0:16, 0:128], zsb[:], w["ident"][:]))
    p.s_te.wait(v)
    sv.tensor_copy(zg_sb[0:16, 0:128], w["pbt"][0:16, 0:128])
    p.s_ve.wait(sy)
    for ct in range(4):
        p.s_z.inc(sy.dma_start(out=dram["z_src"][0:4, ct * 128:(ct + 1) * 128],
                               in_=zg_sb[ct * 4:(ct + 1) * 4, 0:128]), 16)
    p.s_z.wait(gp)
    p.s_cc.inc(gp.collective_compute(
        "AllGather", ALU.bypass,
        replica_groups=[[0, 1, 2, 3], [4, 5, 6, 7]],
        ins=[dram["z_src"][:, :].opt()],
        outs=[dram["z_gth"][:, :].opt()]), 1)
    p.s_cc.wait(gp)
    p.s_zg.inc(gp.dma_start(out=zg_sb[:], in_=dram["z_gth"][:, :]), 16)
    p.s_zg.wait(te)
    for ct in range(4):
        p.s_ve.wait(te)
        p.s_te.inc(te.transpose(w["pbt"][0:128, 0:16],
                                zg_sb[0:16, ct * 128:(ct + 1) * 128],
                                w["ident"][0:16, 0:16]))
        p.s_te.wait(v)
        sv.tensor_copy(zt_sb[:, ct * 16:(ct + 1) * 16], w["pbt"][0:128, 0:16])
    for ct in range(4):
        sync_all(p, v)
        sv.tensor_copy(C(VEC), C(4 + ct))
        sv.tensor_copy(C(VES), C(8 + ct))
        for k in range(8):
            rk = k if k < 4 else 7 - k
            slk = 0 if k < 4 else 1
            for comp, base in ((0, VEC), (1, VES)):
                zcol = zt_sb[:, ct * 16 + rk * 4 + slk * 2 + comp:
                             ct * 16 + rk * 4 + slk * 2 + comp + 1]
                sv.scalar_tensor_tensor(C(base + k + 1), C(base + k), C(ct),
                                        zcol, op0=ALU.mult, op1=ALU.add)
        for (sl, ohb, ic, isv) in ((0, w["oh0_b"], TA, TB), (1, w["oh1_b"], EC, ES)):
            sv.tensor_mul(sc1[:, T8:T8 + 8], sc1[:, VEC:VEC + 8], ohb[:])
            sv.tensor_reduce(C(ic), sc1[:, T8:T8 + 8], X, ALU.add)
            sv.tensor_mul(sc1[:, T8:T8 + 8], sc1[:, VES:VES + 8], ohb[:])
            sv.tensor_reduce(C(isv), sc1[:, T8:T8 + 8], X, ALU.add)
        ysl = y_sb[:, ct * T:(ct + 1) * T]
        sv.tensor_mul(xmre[:], ysl, cost[:, ct * T:(ct + 1) * T])
        sv.tensor_mul(xmim[:], ysl, sint[:, ct * T:(ct + 1) * T])
        for sl, (icol, iscol) in ((0, (TA, TB)), (1, (EC, ES))):
            sv.tensor_tensor_scan(vre[:, sl * 512:(sl + 1) * 512],
                                 rtile4[:, ct * 512:(ct + 1) * 512],
                                 xmre[:, sl * 512:(sl + 1) * 512], C(icol),
                                 op0=ALU.mult, op1=ALU.add)
            sv.tensor_tensor_scan(vim[:, sl * 512:(sl + 1) * 512],
                                 rtile4[:, ct * 512:(ct + 1) * 512],
                                 xmim[:, sl * 512:(sl + 1) * 512], C(iscol),
                                 op0=ALU.mult, op1=ALU.add)
        if dbg is not None and "scdbg2" in dbg and lay == 0 and ct == 0:
            dd = dbg["scdbg2"]
            p.s_ve.wait(sy)
            for bi, (tile, wdt) in enumerate((
                    (zt_sb[:, 0:64], 64), (sc1[:, 0:72], 72), (vre[:], T),
                    (vim[:], T), (xmre[:], T), (xmim[:], T),
                    (rtile4[:, 0:512], 512), (zg_sb[0:16, 0:512], 512))):
                p.s_z.inc(sy.dma_start(out=dd[bi * 128:bi * 128 + 128, 0:wdt]
                                       if bi != 7 else dd[bi * 128:bi * 128 + 16, 0:wdt],
                                       in_=tile), 16)
            p.s_z.wait(v)
        sv.tensor_mul(xmre[:], cost[:, ct * T:(ct + 1) * T], vre[:])
        sv.tensor_mul(xmim[:], sint[:, ct * T:(ct + 1) * T], vim[:])
        sv.tensor_add(xmre[:], xmre[:], xmim[:])
        if dbg is not None and f"conv{lay}" in dbg:
            sv.tensor_copy(xmim[:], xmre[:])
            p.s_ve.wait(sy)
            p.s_z.inc(sy.dma_start(
                out=dbg[f"conv{lay}"][ct * 128:(ct + 1) * 128, :], in_=xmim[:]), 16)
        hs = h_sb[:, ct * T:(ct + 1) * T]
        sv.scalar_tensor_tensor(hs, xmre[:], scl[:, ct:ct + 1], hs,
                                op0=ALU.mult, op1=ALU.add)


NSB = 16
SBW = 256
INV_SQRT_HD = float(1.0 / np.sqrt(HD))
EXP_SHIFT = -12.0


def s3_positions():
    pos = []
    bias_idx = 0
    for sl in (0, 1):
        n_off = 4 if sl == 0 else 7
        for ch in range(n_off):
            rank = ch if ch < 4 else 7 - ch
            slot = 0 if ch < 4 else 1
            pos.append((sl, "g", rank, slot, bias_idx))
            bias_idx += 1
        pos.append((sl, "d", 0, sl, bias_idx))
        bias_idx += 1
    return pos


S3_POS = s3_positions()
S3_NBIAS = 13


def emit_qkvw_proj(p, w, wimg_dram, bias_cols, dst_writer, wbuf, mirror):
    te, v, gp = p.te, p.ve, p.gp
    mm = te.matmul
    wslot = [p.s_wt.n, p.s_w2.n]
    wsems = [p.s_wt, p.s_w2]
    te_sb = [p.s_te.n] * 2
    for sbi in range(NSB):
        s = sbi % 2
        p.s_te.wait(gp, te_sb[s])
        wslot[s] = wsems[s].inc(gp.dma_start(
            out=wbuf[s][:], in_=wimg_dram[sbi]), 16)
        wsems[s].wait(te, wslot[s])
        for ft in range(2):
            for th in range(2):
                bank = w["pb"][2 * ft + th]
                dst_writer("guard", bank, None, None, None)
                for kt in range(NKT):
                    ins = mm(bank[:, :],
                             wbuf[s][:, kt * SBW + ft * 128:kt * SBW + (ft + 1) * 128],
                             mirror[:, kt * T + th * 512:kt * T + (th + 1) * 512],
                             start=kt == 0, stop=kt == NKT - 1,
                             skip_group_check=True)
                p.s_te.inc(ins)
                te_sb[s] = p.s_te.n
                dst_writer("use", bank, sbi, ft, th)


def emit_s3(p, x_dram, out_dram, wa, w, dram, dbg):
    nc = p.nc
    v, se, te, gp, sy = p.ve, p.se, p.te, p.gp, p.sy
    sv = SV(p)
    mm = te.matmul

    octx = ExitStack()

    sctx = ExitStack()
    w["mirror"] = sctx.enter_context(nc.sbuf_tensor("s3_mirror", [128, NKT * T], BF16))
    wbuf = [sctx.enter_context(nc.sbuf_tensor(f"s3_wb{i}", [128, NKT * SBW], BF16))
            for i in range(2)]
    kvstage = sctx.enter_context(nc.sbuf_tensor("kvstage", [128, T], BF16))
    bvb = sctx.enter_context(nc.sbuf_tensor("bvb", [128, SBW], F32))
    bvrow = sctx.enter_context(nc.sbuf_tensor("bvrow", [1, SBW], F32))
    emit_stats(p, x_dram, w, double_ln=False)
    emit_mirror(p, x_dram, w, None, None)
    mirror = w["mirror"]

    q_sto = [0]

    def q_writer(mode, bank, sbi, ft, th):
        if mode == "guard":
            p.s_ve.wait(te)
            return
        p.s_te.wait(v)
        p.s_sto[1].wait(v, q_sto[0])
        fcol = sbi * SBW + ft * 128
        p.s_ve.inc(v.tensor_scalar(
            kvstage[:, th * 512:(th + 1) * 512], bank[:, :],
            wa["bq_cols"][:, fcol // 128:fcol // 128 + 1], None, op0=ALU.add))
        if th == 1:
            p.s_ve.wait(sy)
            q_sto[0] = p.s_sto[1].inc(sy.dma_start(
                out=dram["q_dram"][fcol:fcol + 128, :], in_=kvstage[:]), 16)
    emit_qkvw_proj(p, w, dram["wq_img"], None, q_writer, wbuf, mirror)

    k_sto = [0]

    def k_writer(mode, bank, sbi, ft, th):
        if mode == "guard":
            p.s_ve.wait(te)
            return
        p.s_te.wait(v)
        p.s_sto[0].wait(v, k_sto[0])
        fcol = sbi * SBW + ft * 128
        p.s_ve.inc(v.tensor_scalar(
            kvstage[:, th * 512:(th + 1) * 512], bank[:, :],
            wa["bk_cols"][:, fcol // 128:fcol // 128 + 1], None, op0=ALU.add))
        if th == 1:
            p.s_ve.wait(sy)
            k_sto[0] = p.s_sto[0].inc(sy.dma_start(
                out=dram["kv_src"][fcol:fcol + 128, :], in_=kvstage[:]), 16)
    emit_qkvw_proj(p, w, dram["wk_img"], None, k_writer, wbuf, mirror)

    vreg = dram["kv_src"].rearrange("(r c) t -> r c t", c=4)
    v_sto = [0]
    te_sbv = [p.s_te.n] * 2
    for sbi in range(NSB):
        s = sbi % 2
        p.s_te.wait(gp, te_sbv[s])
        wsem = p.s_wt if s == 0 else p.s_w2
        wld = wsem.inc(gp.dma_start(out=wbuf[s][:], in_=dram["wv_img"][sbi]), 16)
        p.s_te.wait(sy)
        bvld = p.s_x[0].inc(sy.dma_start(
            out=bvrow[:], in_=dram["bv_row"][0:1, sbi * SBW:(sbi + 1) * SBW]), 16)
        p.s_x[0].wait(te, bvld)
        p.s_ve.wait(te)
        p.s_te.inc(te.matmul(w["pb"][5][:, 0:SBW], w["ones_row"][:],
                             bvrow[0:1, :], start=True, stop=True,
                             skip_group_check=True))
        p.s_te.wait(v)
        p.s_ve.inc(v.tensor_copy(bvb[:], w["pb"][5][:, 0:SBW]))
        wsem.wait(te, wld)
        for tt in range(8):
            bank = w["pb"][tt % 2]
            p.s_ve.wait(te)
            for kt in range(NKT):
                ins = mm(bank[:, 0:SBW],
                         mirror[:, kt * T + tt * 128:kt * T + (tt + 1) * 128],
                         wbuf[s][:, kt * SBW:(kt + 1) * SBW],
                         start=kt == 0, stop=kt == NKT - 1, skip_group_check=True)
            p.s_te.inc(ins)
            te_sbv[s] = p.s_te.n
            p.s_te.wait(v)
            p.s_sto[1].wait(v, v_sto[0])
            p.s_ve.inc(v.tensor_add(kvstage[:, 0:SBW], bank[:, 0:SBW], bvb[:]))
            p.s_ve.wait(sy)
            f0 = sbi * SBW
            v_sto[0] = p.s_sto[1].inc(sy.dma_start(
                out=vreg[1024 + tt * 128:1024 + (tt + 1) * 128, f0 // 1024,
                         f0 % 1024:f0 % 1024 + SBW],
                in_=kvstage[0:128, 0:SBW]), 16)
    barrier(p)
    sctx.close()
    av_sb = octx.enter_context(nc.sbuf_tensor("av_sb", [128, NH * T], BF16))

    p.s_cc.inc(gp.collective_compute(
        "AllGather", ALU.bypass,
        replica_groups=[[0, 1, 2, 3], [4, 5, 6, 7]],
        ins=[dram["kv_src"][:, :].opt()],
        outs=[dram["kv_gth"].rearrange("r f t -> (r f) t").opt()]), 1)
    p.s_cc.wait(gp)
    barrier(p)

    sctx = ExitStack()
    qbuf = [sctx.enter_context(nc.sbuf_tensor(f"qb{i}", [128, 512], BF16))
            for i in range(2)]
    kst = [sctx.enter_context(nc.sbuf_tensor(f"kst{i}", [128, 512], BF16))
           for i in range(2)]
    vst = [sctx.enter_context(nc.sbuf_tensor(f"vst{i}", [128, 512], BF16))
           for i in range(2)]
    pt_sb = [sctx.enter_context(nc.sbuf_tensor(f"pt{i}", [128, 512], BF16))
             for i in range(4)]
    masks = sctx.enter_context(nc.sbuf_tensor("masks", [128, 4 * 512], BF16))
    diag_i2 = sctx.enter_context(nc.sbuf_tensor("diag_i2", [128, 512], I32))
    recipr = sctx.enter_context(nc.sbuf_tensor("recipr", [1, 512], F32))
    recipb = sctx.enter_context(nc.sbuf_tensor("recipb", [128, 512], F32))
    p.s_gp.inc(gp.iota(diag_i2[:], pattern=[[-1, 512]], base=0,
                       channel_multiplier=1))
    p.s_gp.wait(v)
    for kt in range(4):
        sv.tensor_single_scalar(masks[:, kt * 512:(kt + 1) * 512], diag_i2[:],
                                float(-kt * 128), ALU.is_le)
    kv_at = [p.s_kv[0].n, p.s_kv[1].n]
    q_at = [p.s_q[0].n, p.s_q[1].n]
    te_kv = [p.s_te.n] * 2
    te_q = [p.s_te.n] * 2
    te_pt = [p.s_te.n] * 4
    se_pt = [p.s_se.n] * 4
    ve_pt = [p.s_ve.n] * 4
    sc_at = [0] * 4
    npos = len(S3_POS)
    AVB, DENB, SCB0, BCB = 0, 1, 2, 6
    for h in range(NH):
        for pi, (sl, kind, rank, slot, bcol) in enumerate(S3_POS):
            s = pi % 2
            seg = (h * 2 + sl) % 2
            if pi == 0 or S3_POS[pi - 1][0] != sl:
                p.s_te.wait(sy, te_q[seg])
                q_at[seg] = p.s_q[seg].inc(sy.dma_start(
                    out=qbuf[seg][:],
                    in_=dram["q_dram"][h * 128:(h + 1) * 128,
                                       sl * 512:(sl + 1) * 512]), 16)
            p.s_te.wait(sy, te_kv[s])
            if kind == "g":
                ksrc = dram["kv_gth"][rank, h * 128:(h + 1) * 128,
                                      slot * 512:(slot + 1) * 512]
                vsrc = dram["kv_gth"].rearrange("r (q c) t -> r q c t", c=4)[
                    rank, 1024 + slot * 512:1024 + (slot + 1) * 512,
                    h // 8, (h % 8) * 128:(h % 8) * 128 + 128]
            else:
                ksrc = dram["kv_src"][h * 128:(h + 1) * 128,
                                      slot * 512:(slot + 1) * 512]
                vsrc = dram["kv_src"].rearrange("(q c) t -> q c t", c=4)[
                    1024 + slot * 512:1024 + (slot + 1) * 512,
                    h // 8, (h % 8) * 128:(h % 8) * 128 + 128]
            p.s_kv[s].inc(sy.dma_start(out=kst[s][:], in_=ksrc), 16)
            kv_at[s] = p.s_kv[s].inc(sy.dma_start(
                out=vst[s][:].rearrange("p (kt c) -> p kt c", kt=4),
                in_=vsrc.rearrange("(kt p) c -> p kt c", p=128)), 16)
            first = pi == 0 or S3_POS[pi - 1][0] != sl
            last = pi == npos - 1 or S3_POS[pi + 1][0] != sl
            p.s_kv[s].wait(te, kv_at[s])
            p.s_q[seg].wait(te, q_at[seg])
            for kt in range(4):
                scb = w["pb"][SCB0 + kt]
                p.s_se.wait(te, se_pt[kt])
                sc_at[kt] = p.s_te.inc(
                    mm(scb[:, :], kst[s][:, kt * 128:(kt + 1) * 128],
                       qbuf[seg][:], start=True, stop=True,
                       skip_group_check=True))
            for kt in range(4):
                p.s_te.wait(se, max(sc_at[kt], te_pt[kt]))
                se_pt[kt] = p.s_se.inc(se.activation(
                    pt_sb[kt][:], w["pb"][SCB0 + kt][:, :], AF.Exp,
                    bias=wa["kvb_cols"][:, bcol:bcol + 1],
                    scale=np.float32(INV_SQRT_HD)))
                if kind == "d":
                    p.s_se.wait(v, se_pt[kt])
                    ve_pt[kt] = p.s_ve.inc(v.tensor_mul(
                        pt_sb[kt][:], pt_sb[kt][:],
                        masks[:, kt * 512:(kt + 1) * 512]))
            for kt in range(4):
                p.s_se.wait(te, se_pt[kt])
                if kind == "d":
                    p.s_ve.wait(te, ve_pt[kt])
                st = first and kt == 0
                sp = last and kt == 3
                mm(w["pb"][AVB][:, :], vst[s][:, kt * 128:(kt + 1) * 128],
                   pt_sb[kt][:], start=st, stop=sp, skip_group_check=True)
                ins = mm(w["pb"][DENB][0:1, :], w["ones_bf"][:], pt_sb[kt][:],
                         start=st, stop=sp, skip_group_check=True)
                te_pt[kt] = p.s_te.inc(ins)
            te_kv[s] = p.s_te.n
            te_q[seg] = p.s_te.n
            if last:
                p.s_te.wait(v)
                sv.tensor_copy(recipr[0:1, :], w["pb"][DENB][0:1, :])
                sv.reciprocal(recipr[0:1, :], recipr[0:1, :])
                p.s_ve.wait(te)
                p.s_te.inc(te.matmul(w["pb"][BCB][:, :], w["ones_row"][:],
                                     recipr[0:1, :], start=True, stop=True,
                                     skip_group_check=True))
                p.s_te.wait(v)
                sv.tensor_copy(recipb[:], w["pb"][BCB][:, :])
                p.s_ve.inc(v.tensor_mul(
                    av_sb[:, h * T + sl * 512:h * T + (sl + 1) * 512],
                    w["pb"][AVB][:, :], recipb[:]))
                p.s_ve.wait(te)
    if dbg is not None and "xav" in dbg:
        emit_dump(p, dbg["xav"], av_sb, NKT)
    barrier(p)
    sctx.close()

    sctx = ExitStack()
    wobuf = [sctx.enter_context(nc.sbuf_tensor(f"s3_wo{i}", [128, NKT * SBW], BF16))
             for i in range(2)]
    xin = w["xin2"]
    wslot = [0, 0]
    ld_at = [p.s_x[0].n, p.s_x[1].n]
    st_at = [p.s_sto[0].n, p.s_sto[1].n]
    ve_at = [p.s_ve.n] * 2
    last_stt = [p.s_ve.n]
    te_sbo = [p.s_te.n] * 2
    for sbi in range(NSB):
        s2 = sbi % 2
        p.s_te.wait(gp, te_sbo[s2])
        wsem = p.s_wt if s2 == 0 else p.s_w2
        wslot[s2] = wsem.inc(gp.dma_start(out=wobuf[s2][:],
                                          in_=dram["wo_img"][sbi]), 16)
        wsem.wait(te, wslot[s2])
        for ft in range(2):
            ot = sbi * 2 + ft
            s = ot % 2
            p.s_ve.wait(sy, ve_at[s])
            p.s_sto[s].wait(sy, st_at[s])
            ld_at[s] = p.s_x[s].inc(sy.dma_start(
                out=xin[s][:], in_=x_dram[ot * 128:(ot + 1) * 128, :]), 16)
            for th in range(2):
                bank = w["pb"][th]
                p.s_ve.wait(te, last_stt[0])
                for kt in range(NKT):
                    ins = mm(bank[:, :],
                             wobuf[s2][:, kt * SBW + ft * 128:kt * SBW + (ft + 1) * 128],
                             av_sb[:, kt * T + th * 512:kt * T + (th + 1) * 512],
                             start=kt == 0, stop=kt == NKT - 1,
                             skip_group_check=True)
                p.s_te.inc(ins)
            te_sbo[s2] = p.s_te.n
            p.s_te.wait(v)
            p.s_x[s].wait(v, ld_at[s])
            v.scalar_tensor_tensor(xin[s][:, 0:512], w["pb"][0][:, :],
                                   wa["bo_cols"][:, ot:ot + 1], xin[s][:, 0:512],
                                   op0=ALU.add, op1=ALU.add)
            ve_at[s] = p.s_ve.inc(v.scalar_tensor_tensor(
                xin[s][:, 512:1024], w["pb"][1][:, :], wa["bo_cols"][:, ot:ot + 1],
                xin[s][:, 512:1024], op0=ALU.add, op1=ALU.add))
            last_stt[0] = ve_at[s]
            p.s_ve.wait(sy, ve_at[s])
            st_at[s] = p.s_sto[s].inc(sy.dma_start(
                out=out_dram[ot * 128:(ot + 1) * 128, :], in_=xin[s][:]), 16)
    barrier(p)
    sctx.close()
    octx.close()


def emit_dump(p, dst_dram, src_sb, nct):
    barrier(p)
    for ct in range(nct):
        eng = p.gp if src_sb.dtype != F32 else p.sy
    for ct in range(nct):
        if src_sb.dtype != F32:
            p.s_gd.inc(p.gp.dma_start(out=dst_dram[ct * 128:(ct + 1) * 128, :],
                                      in_=src_sb[:, ct * T:(ct + 1) * T]), 16)
        else:
            p.s_z.inc(p.sy.dma_start(out=dst_dram[ct * 128:(ct + 1) * 128, :],
                                     in_=src_sb[:, ct * T:(ct + 1) * T]), 16)
    barrier(p)


def _s1_cols_layout():
    lay = {}
    off = 0

    def add(name, n):
        nonlocal off
        lay[name] = (off, n)
        off += n
    add("bin", 4)
    add("lnl_w", 4)
    add("lnl_b", 4)
    add("bout", 32)
    for l in range(AD):
        for nm, n in (("ln1_w", 4), ("ln1_b", 4), ("ln2_w", 4), ("ln2_b", 4),
                      ("phz_re", 4), ("phz_im", 4), ("lc_re", 4), ("lc_im", 4),
                      ("scl", 4), ("b2", 4), ("b1", 16)):
            add(f"{nm}{l}", n)
    return lay, off


S1_COLS, S1C_N = _s1_cols_layout()


def build_program(stages=("s1", "s2", "s3", "s4"), dbg_keys=()):
    nc = bass.Bass("TRN2", target_bir_lowering=False, debug=False,
                   detect_race_conditions=False)
    dram = {}

    def din(name, shape, dt=F32):
        dram[name] = nc.dram_tensor(name, shape, dt, kind="ExternalInput").ap()
        return dram[name]

    x_in = din("x_in", [D, T])
    out_fm = nc.dram_tensor("out_fm", [D, T], F32, kind="ExternalOutput").ap()
    x_res = nc.dram_tensor("x_res", [D, T], F32).ap()
    u_bounce = nc.dram_tensor("u_bounce", [D, T], BF16).ap()

    din("ffn_A", [64, 64])
    din("ffn_B", [64, 64])
    din("ffn_g_cols", [128, NKT])
    din("ffn_b_cols", [128, NKT])
    din("ffn_biasp_cols", [128, NKT])

    if "s3" in stages:
        for nm in ("wq_img", "wk_img", "wv_img", "wo_img"):
            din(nm, [NSB, 128, NKT * SBW])
        din("bq_cols", [128, NKT])
        din("bk_cols", [128, NKT])
        din("bo_cols", [128, NKT])
        din("bv_row", [1, D])
        din("kvb_cols", [128, S3_NBIAS])
        dram["kv_src"] = nc.dram_tensor("kv_src", [2 * D, T], BF16).ap()
        dram["kv_gth"] = nc.dram_tensor("kv_gth", [4, 2 * D, T], BF16).ap()
        dram["q_dram"] = nc.dram_tensor("q_dram", [D, T], BF16).ap()

    if "s1" in stages:
        din("win_img", [128, NKT * DIM])
        din("wout_img", [128, 4 * D])
        din("w1_img", [AD, 128, 4 * DFF])
        din("w2_img", [AD, 128, 16 * 512])
        din("s1_cols", [128, S1C_N])
        din("oh0", [1, 8])
        din("oh1", [1, 8])
        dram["z_src"] = nc.dram_tensor("z_src", [4, DIM], F32).ap()
        dram["z_gth"] = nc.dram_tensor("z_gth", [16, DIM], F32).ap()

    dbg = {}
    for k in dbg_keys:
        if k in ("scdbg", "scdbg2"):
            dbg[k] = nc.dram_tensor("dbg_" + k, [8 * 128, T], F32,
                                    kind="ExternalOutput").ap()
            continue
        nct = 4 if not k.startswith("x") else NKT
        dbg[k] = nc.dram_tensor("dbg_" + k, [nct * 128, T], F32,
                                kind="ExternalOutput").ap()

    with ExitStack() as ctx:
        p = P(nc, ctx)
        with nc.Block():
            w = {
                "xin2": [p.sbuf("xin0", [128, T], F32), p.sbuf("xin1", [128, T], F32)],
                "xbs2": [p.sbuf("xbs0", [128, T], BF16), p.sbuf("xbs1", [128, T], BF16)],
                "sqs2": [p.sbuf("sqs0", [128, T], BF16), p.sbuf("sqs1", [128, T], BF16)],
                "t1": p.sbuf("t1", [128, T], F32),
                "ones_bf": p.sbuf("ones_bf", [128, 1], BF16),
                "ones_row": p.sbuf("ones_row", [1, 128], F32),
                "r0": p.sbuf("r0", [1, T], F32),
                "r1": p.sbuf("r1", [1, T], F32),
                "r2": p.sbuf("r2", [1, T], F32),
                "r3": p.sbuf("r3", [1, T], F32),
                "rstd": p.sbuf("rstd", [1, T], F32),
                "sc_row": p.sbuf("sc_row", [1, T], F32),
                "mu_b": p.sbuf("mu_b", [128, T], F32),
                "sc_b": p.sbuf("sc_b", [128, T], F32),
                "sc2_b": p.sbuf("sc2_b", [128, T], F32),
                "pb": [p.psum(f"pb{i}", [128, 512]) for i in range(7)],
            }
            w["ones_f32"] = p.sbuf("ones_f32", [128, 1], F32)
            w["pbt"] = w["pb"][5]
            p.s_ve.inc(p.ve.memset(w["ones_bf"][:], 1.0))
            p.s_ve.inc(p.ve.memset(w["ones_f32"][:], 1.0))
            p.s_ve.inc(p.ve.memset(w["ones_row"][:], 1.0))

            ws = None
            if "s1" in stages:
                diag_i = p.sbuf("diag_i", [128, 128], I32)
                w["ident"] = p.sbuf("ident", [128, 128], F32)
                p.s_gp.inc(p.gp.iota(diag_i[:], pattern=[[-1, 128]], base=0,
                                     channel_multiplier=1))
                oh0r = p.sbuf("oh0r", [1, 8], F32)
                oh1r = p.sbuf("oh1r", [1, 8], F32)
                w["oh0_b"] = p.sbuf("oh0_b", [128, 8], F32)
                w["oh1_b"] = p.sbuf("oh1_b", [128, 8], F32)
                p.s_const.inc(p.gp.dma_start(out=oh0r[:], in_=dram["oh0"]), 16)
                p.s_const.inc(p.gp.dma_start(out=oh1r[:], in_=dram["oh1"]), 16)
                s1c = p.sbuf("s1c", [128, S1C_N], F32)
                p.s_const.inc(p.gp.dma_start(out=s1c[:], in_=dram["s1_cols"]), 16)

                def wsl(name):
                    o, n = S1_COLS[name]
                    return s1c[:, o:o + n]
                ws = {
                    "bin_cols": wsl("bin"), "lnl_w": wsl("lnl_w"),
                    "lnl_b": wsl("lnl_b"), "bout_cols": wsl("bout"),
                }
                for nm in ("ln1_w", "ln1_b", "ln2_w", "ln2_b", "phz_re",
                           "phz_im", "lc_re", "lc_im", "scl", "b2", "b1"):
                    key = nm + "_cols" if nm in ("b1", "b2") else nm
                    ws[key] = [wsl(f"{nm}{l}") for l in range(AD)]

            wf = {"u_dram": u_bounce}
            for nm in ("A_sb", "B_sb"):
                t = p.sbuf(nm, [128, 64], BF16)
                p.s_const.inc(p.gp.dma_start(out=t[0:64, :], in_=dram["ffn_" + nm[0]]), 16)
                p.s_const.inc(p.gp.dma_start(out=t[64:128, :], in_=dram["ffn_" + nm[0]]), 16)
                wf[nm] = t
            for nm in ("gcols", "bcols", "biasp_cols"):
                t = p.sbuf(nm, [128, NKT], F32)
                key = {"gcols": "ffn_g_cols", "bcols": "ffn_b_cols",
                       "biasp_cols": "ffn_biasp_cols"}[nm]
                p.s_const.inc(p.gp.dma_start(out=t[:], in_=dram[key]), 16)
                wf[nm] = t
            barrier(p)

            wa = None
            if "s3" in stages:
                wa = {}
                for nm in ("bq_cols", "bk_cols", "bo_cols", "kvb_cols"):
                    t = p.sbuf("wa_" + nm, [128, dram[nm].shape[1]], F32)
                    p.s_const.inc(p.gp.dma_start(out=t[:], in_=dram[nm]), 16)
                    wa[nm] = t
            if "s1" in stages:
                p.s_ve.inc(p.ve.tensor_single_scalar(w["ident"][:], diag_i[:],
                                                     0.0, ALU.is_equal))
                for ohr, ohb in ((oh0r, w["oh0_b"]), (oh1r, w["oh1_b"])):
                    p.s_ve.wait(p.te)
                    p.s_te.inc(p.te.matmul(w["pb"][4][:, 0:8], w["ones_row"][:],
                                           ohr[:], start=True, stop=True,
                                           skip_group_check=True))
                    p.s_te.wait(p.ve)
                    p.s_ve.inc(p.ve.tensor_copy(ohb[:], w["pb"][4][:, 0:8]))

            last = {}
            cur_x = x_in
            for si, st in enumerate(stages):
                is_last = si == len(stages) - 1
                dst = out_fm if is_last else x_res
                if st in ("s2", "s4"):
                    emit_ffn(p, cur_x, dst, wf, w, last)
                elif st == "s1":
                    emit_s1(p, cur_x, dst, ws, w, dram, dbg if dbg else None)
                elif st == "s3":
                    emit_s3(p, cur_x, dst, wa, w, dram, dbg if dbg else None)
                else:
                    raise NotImplementedError(st)
                dk = f"xafter{si}"
                if dbg and dk in dbg:
                    barrier(p)
                    for q8 in range(8):
                        p.s_z.inc(p.sy.dma_start(
                            out=dbg[dk][q8 * 512:(q8 + 1) * 512, :],
                            in_=dst[q8 * 512:(q8 + 1) * 512, :]), 16)
                    barrier(p)
                cur_x = x_res

            sync_all(p, p.sy)
    return nc


def tok_ids(core):
    j = core % 4
    return np.r_[j * CH:(j + 1) * CH, (7 - j) * CH:(8 - j) * CH]


def host_inputs(inputs):
    x = inputs["L0_attn_a_x"]
    A = np.ascontiguousarray(np.asarray(inputs["L0_ffn_A"], f32))
    Bm = np.ascontiguousarray(np.asarray(inputs["L0_ffn_B"], f32))
    bias = np.asarray(inputs["L0_ffn_bias"], f32)
    g = np.asarray(inputs["L0_ffn_ln_gamma"], f32)
    be = np.asarray(inputs["L0_ffn_ln_beta"], f32)

    def cols(vec):
        return np.ascontiguousarray(vec.reshape(NKT, 128).T)

    biasp = np.ascontiguousarray(bias.reshape(64, 64).T).reshape(D)
    shared = {
        "ffn_A": A, "ffn_B": Bm,
        "ffn_g_cols": cols(g), "ffn_b_cols": cols(be),
        "ffn_biasp_cols": cols(biasp),
    }

    def colv(vec, n):
        return np.asarray(vec, f32).reshape(n, 128).T

    aa = {k[len("L0_attn_a_"):]: np.asarray(v, f32) for k, v in inputs.items()
          if k.startswith("L0_attn_a_") and k != "L0_attn_a_x"}
    shared["win_img"] = np.ascontiguousarray(
        aa["Win"].T.reshape(NKT, 128, DIM).transpose(1, 0, 2).reshape(128, NKT * DIM))
    shared["wout_img"] = np.ascontiguousarray(
        aa["Wout"].T.reshape(4, 128, D).transpose(1, 0, 2).reshape(128, 4 * D))
    shared["w1_img"] = np.ascontiguousarray(np.stack([
        aa["W1"][l].T.reshape(4, 128, DFF).transpose(1, 0, 2).reshape(128, 4 * DFF)
        for l in range(AD)]))
    shared["w2_img"] = np.ascontiguousarray(np.stack([
        aa["W2"][l].T.reshape(16, 128, 512).transpose(1, 0, 2).reshape(128, 16 * 512)
        for l in range(AD)]))
    ab = {k[len("L0_attn_b_"):]: np.asarray(v, f32) for k, v in inputs.items()
          if k.startswith("L0_attn_b_")}

    def wimg(m):
        return np.ascontiguousarray(
            m.T.reshape(NKT, 128, NSB, SBW).transpose(2, 1, 0, 3)
            .reshape(NSB, 128, NKT * SBW))
    shared["wq_img"] = wimg(ab["Wq"])
    shared["wk_img"] = wimg(ab["Wk"])
    shared["wv_img"] = wimg(ab["Wv"])
    shared["wo_img"] = wimg(ab["Wo"])
    shared["bq_cols"] = cols(ab["bq"])
    shared["bk_cols"] = cols(ab["bk"])
    shared["bo_cols"] = cols(ab["bo"])
    shared["bv_row"] = np.ascontiguousarray(ab["bv"].reshape(1, D))

    s1c = np.zeros((128, S1C_N), f32)

    def put(name, vec, n):
        o, k = S1_COLS[name]
        assert k == n
        s1c[:, o:o + n] = colv(vec, n)
    put("bin", aa["bin"], 4)
    put("lnl_w", aa["lnl_w"], 4)
    put("lnl_b", aa["lnl_b"], 4)
    put("bout", aa["bout"], 32)
    for l in range(AD):
        for nm, n in (("ln1_w", 4), ("ln1_b", 4), ("ln2_w", 4), ("ln2_b", 4),
                      ("phz_re", 4), ("phz_im", 4), ("lc_re", 4), ("lc_im", 4),
                      ("sc_lin", 4), ("b2", 4), ("b1", 16)):
            key = "scl" if nm == "sc_lin" else nm
            put(f"{key}{l}", aa[nm][l], n)
    shared["s1_cols"] = s1c

    maps = []
    for c in range(NCORES):
        b = c // 4
        j = c % 4
        xs = np.ascontiguousarray(np.asarray(x[b], dtype=f32)[tok_ids(c), :].T)
        m = dict(shared)
        m["x_in"] = xs
        oh0 = np.zeros((1, 8), f32)
        oh0[0, j] = 1.0
        oh1 = np.zeros((1, 8), f32)
        oh1[0, 7 - j] = 1.0
        m["oh0"] = oh0
        m["oh1"] = oh1
        kvb = np.full((128, S3_NBIAS), -1e30, f32)
        bi = 0
        for sl in (0, 1):
            n_off = 4 if sl == 0 else 7
            gq = j if sl == 0 else 7 - j
            for ch in range(n_off):
                kvb[:, bi] = -12.0 if ch < gq else -1e30
                bi += 1
            kvb[:, bi] = -12.0
            bi += 1
        m["kvb_cols"] = kvb
        maps.append(m)
    return maps


def kernel(**inputs):
    maps = host_inputs(inputs)
    nc = build_program()
    res = run_bass_kernel_spmd(nc, maps, core_ids=list(range(NCORES)))
    out = np.empty((NB, L, D), np.float32)
    for c in range(NCORES):
        out[c // 4, tok_ids(c), :] = res.results[c]["out_fm"].T
    return out


# version 03 (final)
# speedup vs baseline: 1.3304x; 1.1497x over previous
import numpy as np
from contextlib import ExitStack
import concourse.bass as bass
import concourse.mybir as mybir
from concourse.bass_utils import run_bass_kernel_spmd

F32 = mybir.dt.float32
BF16 = mybir.dt.bfloat16
I32 = mybir.dt.int32
AF = mybir.ActivationFunctionType
ALU = mybir.AluOpType

NB, L, D = 2, 4096, 4096
NCORES = 8
CH = 512
T = 1024
NKT = D // 128
DIM, DFF, AD = 512, 2048, 4
NH, HD = 32, 128
EPS = 1e-5
f32 = np.float32


class Sem:
    def __init__(self, ctx, nc, name):
        self.h = ctx.enter_context(nc.semaphore(name))
        self.n = 0

    def inc(self, instr, k=1):
        instr.then_inc(self.h, k)
        self.n += k
        return self.n

    def wait(self, eng, n=None):
        eng.wait_ge(self.h, self.n if n is None else n)


class SV:

    def __init__(self, p):
        self._p = p

    def __getattr__(self, name):
        p = self._p
        f = getattr(p.ve, name)

        def g(*a, **k):
            ins = f(*a, **k)
            n = p.s_ve.inc(ins)
            p.ve.wait_ge(p.s_ve.h, n)
            return ins
        return g


class P:
    def __init__(self, nc, ctx):
        self.nc = nc
        self.ctx = ctx
        self.te, self.ve, self.se = nc.tensor, nc.vector, nc.scalar
        self.gp, self.sy = nc.gpsimd, nc.sync
        self.s_ld = Sem(ctx, nc, "s_ld")
        self.s_gd = Sem(ctx, nc, "s_gd")
        self.s_te = Sem(ctx, nc, "s_te")
        self.s_ve = Sem(ctx, nc, "s_ve")
        self.s_se = Sem(ctx, nc, "s_se")
        self.s_gp = Sem(ctx, nc, "s_gp")
        self.s_cc = Sem(ctx, nc, "s_cc")
        self.s_x = [Sem(ctx, nc, "s_x0"), Sem(ctx, nc, "s_x1")]
        self.s_sto = [Sem(ctx, nc, "s_sto0"), Sem(ctx, nc, "s_sto1")]
        self.s_wt = Sem(ctx, nc, "s_wt")
        self.s_const = Sem(ctx, nc, "s_const")
        self.s_z = Sem(ctx, nc, "s_z")
        self.s_zg = Sem(ctx, nc, "s_zg")
        self.s_ug = [Sem(ctx, nc, "s_ug0"), Sem(ctx, nc, "s_ug1")]
        self.s_kv = [Sem(ctx, nc, "s_kv0"), Sem(ctx, nc, "s_kv1")]
        self.s_q = [Sem(ctx, nc, "s_q0"), Sem(ctx, nc, "s_q1")]
        self.s_w2 = Sem(ctx, nc, "s_w2")
        self.s_bar = Sem(ctx, nc, "s_bar")
        self.engs = [self.te, self.ve, self.se, self.gp, self.sy]

    def sbuf(self, name, shape, dt):
        return self.ctx.enter_context(self.nc.sbuf_tensor(name, shape, dt))

    def psum(self, name, shape, dt=F32):
        return self.ctx.enter_context(self.nc.psum_tensor(name, shape, dt))


def emit_stats(p, src_dram, w, double_ln, keep_cast=False):
    v, se, te, sy = p.ve, p.se, p.te, p.sy
    xin, xbs, sqs = w["xin2"], w["xbs2"], w["sqs2"]
    pb = w["pb"]
    p.s_se.wait(te)
    p.s_ve.wait(te)
    ld_at = [p.s_x[0].n, p.s_x[1].n]
    ve_at = [p.s_ve.n] * 2
    se_at = [p.s_se.n] * 2
    te_done = [p.s_te.n] * 2
    for kt in range(NKT):
        s = kt % 2
        p.s_ve.wait(sy, ve_at[s])
        ld_at[s] = p.s_x[s].inc(
            sy.dma_start(out=xin[s][:], in_=src_dram[kt * 128:(kt + 1) * 128, :]), 16)
        p.s_x[s].wait(v, ld_at[s])
        p.s_te.wait(v, te_done[s])
        xcast = w["mirror"][:, kt * T:(kt + 1) * T] if keep_cast else xbs[s][:]
        ve_at[s] = p.s_ve.inc(v.tensor_copy(xcast, xin[s][:]))
        p.s_ve.wait(se, ve_at[s])
        p.s_te.wait(se, te_done[s])
        se_at[s] = p.s_se.inc(se.activation(sqs[s][:], xcast, AF.Square))
        p.s_se.wait(te, se_at[s])
        st, sp = kt == 0, kt == NKT - 1
        te.matmul(pb[0][0:1, :], w["ones_bf"][:], xcast[:, 0:512],
                  start=st, stop=sp, skip_group_check=True)
        te.matmul(pb[1][0:1, :], w["ones_bf"][:], xcast[:, 512:1024],
                  start=st, stop=sp, skip_group_check=True)
        te.matmul(pb[2][0:1, :], w["ones_bf"][:], sqs[s][:, 0:512],
                  start=st, stop=sp, skip_group_check=True)
        ins = te.matmul(pb[3][0:1, :], w["ones_bf"][:], sqs[s][:, 512:1024],
                        start=st, stop=sp, skip_group_check=True)
        te_done[s] = p.s_te.inc(ins)
    mu, ex2, var, tmp = w["r0"], w["r1"], w["r2"], w["r3"]
    p.s_te.wait(v)
    inv_d = float(1.0 / D)
    p.s_ve.inc(v.tensor_scalar_mul(mu[0:1, 0:512], pb[0][0:1, :], inv_d))
    v.tensor_scalar_mul(mu[0:1, 512:1024], pb[1][0:1, :], inv_d)
    v.tensor_scalar_mul(ex2[0:1, 0:512], pb[2][0:1, :], inv_d)
    v.tensor_scalar_mul(ex2[0:1, 512:1024], pb[3][0:1, :], inv_d)
    v.tensor_mul(tmp[0:1, :], mu[0:1, :], mu[0:1, :])
    v.tensor_sub(var[0:1, :], ex2[0:1, :], tmp[0:1, :])
    v.tensor_scalar_add(tmp[0:1, :], var[0:1, :], float(EPS))
    p.s_ve.inc(v.reciprocal(tmp[0:1, :], tmp[0:1, :]))
    p.s_ve.wait(se)
    p.s_se.inc(se.activation(w["rstd"][0:1, :], tmp[0:1, :], AF.Sqrt))
    scale = w["rstd"]
    if double_ln:
        p.s_se.wait(v)
        v.tensor_mul(ex2[0:1, :], scale[0:1, :], scale[0:1, :])
        v.tensor_mul(ex2[0:1, :], ex2[0:1, :], var[0:1, :])
        v.tensor_scalar_add(ex2[0:1, :], ex2[0:1, :], float(EPS))
        p.s_ve.inc(v.reciprocal(ex2[0:1, :], ex2[0:1, :]))
        p.s_ve.wait(se)
        p.s_se.inc(se.activation(ex2[0:1, :], ex2[0:1, :], AF.Sqrt))
        p.s_se.wait(v)
        p.s_ve.inc(v.tensor_mul(w["sc_row"][0:1, :], scale[0:1, :], ex2[0:1, :]))
        scale = w["sc_row"]
    bcast_rows(p, [(mu, w["mu_b"]), (scale, w["sc_b"])], w)


def bcast_rows(p, pairs, w):
    v, te = p.ve, p.te
    bank = w["pb"][4]
    for row, dstb in pairs:
        for th in range(2):
            p.s_se.wait(te)
            p.s_ve.wait(te)
            p.s_te.inc(te.matmul(bank[:], w["ones_row"][:],
                                 row[0:1, th * 512:(th + 1) * 512],
                                 start=True, stop=True, skip_group_check=True))
            p.s_te.wait(v)
            p.s_ve.inc(v.tensor_copy(dstb[:, th * 512:(th + 1) * 512], bank[:]))


def emit_mirror(p, src_dram, w, gcols, bcols):
    v, sy = p.ve, p.sy
    xin, t1 = w["xin2"], w["t1"]
    mirror = w["mirror"]
    ld_at = [p.s_x[0].n, p.s_x[1].n]
    ve_at = [p.s_ve.n] * 2
    for kt in range(NKT):
        s = kt % 2
        p.s_ve.wait(sy, ve_at[s])
        ld_at[s] = p.s_x[s].inc(
            sy.dma_start(out=xin[s][:], in_=src_dram[kt * 128:(kt + 1) * 128, :]), 16)
        p.s_x[s].wait(v, ld_at[s])
        dst = mirror[:, kt * T:(kt + 1) * T]
        v.tensor_sub(t1[:], xin[s][:], w["mu_b"][:])
        if gcols is not None:
            v.tensor_mul(t1[:], t1[:], w["sc_b"][:])
            ve_at[s] = p.s_ve.inc(
                v.tensor_scalar(dst, t1[:], gcols[:, kt:kt + 1],
                                bcols[:, kt:kt + 1], op0=ALU.mult, op1=ALU.add))
        else:
            ve_at[s] = p.s_ve.inc(v.tensor_mul(dst, t1[:], w["sc_b"][:]))


_FFN_N = [0]


def emit_ffn(p, x_dram, out_dram, wf, w, last_sem_holder):
    v, se, te, gp, sy = p.ve, p.se, p.te, p.gp, p.sy
    nc = p.nc
    mm = te.matmul
    fctx = p.ctx
    sctx = ExitStack()
    _FFN_N[0] += 1
    sfx = str(_FFN_N[0])
    w["mirror"] = sctx.enter_context(nc.sbuf_tensor("ffn_mirror" + sfx, [128, NKT * T], BF16))
    w["ustage"] = sctx.enter_context(nc.sbuf_tensor("ffn_ustage" + sfx, [128, T], BF16))
    w["yg"] = sctx.enter_context(nc.sbuf_tensor("ffn_yg" + sfx, [128, T], F32))
    emit_stats(p, x_dram, w, double_ln=True)
    bcast_rows(p, [(w["rstd"], w["sc2_b"])], w)
    emit_mirror(p, x_dram, w, wf["gcols"], wf["bcols"])
    mirror = w["mirror"]
    u_dram = wf["u_dram"]
    ustage = w["ustage"]
    p.s_ve.wait(te)
    banks = [w["pb"][0], w["pb"][1], w["pb"][2], w["pb"][3]]
    ve_evac = [0, 0]
    gd_at = [p.s_ug[0].n, p.s_ug[1].n]
    te_at = [0, 0, 0, 0]
    for pi in range(64):
        kt, half = pi // 2, pi % 2
        slot = pi % 2
        for th in range(2):
            b = banks[2 * slot + th]
            p.s_ve.wait(te, ve_evac[slot])
            rhs = mirror[half * 64:half * 64 + 64,
                         kt * T + th * 512:kt * T + (th + 1) * 512]
            lhsA = wf["A_sb"][half * 64:half * 64 + 64, :]
            te_at[2 * slot + th] = p.s_te.inc(
                mm(b[0:64, :], lhsA, rhs, start=True, stop=True,
                   skip_group_check=True))
        p.s_te.wait(v, te_at[2 * slot + 1])
        p.s_ug[slot].wait(v, gd_at[slot])
        v.tensor_copy(ustage[slot * 64:slot * 64 + 64, 0:512], banks[2 * slot][0:64, :])
        ve_evac[slot] = p.s_ve.inc(
            v.tensor_copy(ustage[slot * 64:slot * 64 + 64, 512:1024],
                          banks[2 * slot + 1][0:64, :]))
        p.s_ve.wait(gp, ve_evac[slot])
        gd_at[slot] = p.s_ug[slot].inc(
            gp.dma_start(out=u_dram[pi * 64:(pi + 1) * 64, :],
                         in_=ustage[slot * 64:slot * 64 + 64, :]), 16)
    u2v = u_dram.rearrange("(pp bb) t -> bb pp t", bb=64)
    x2v = x_dram.rearrange("(q bb) t -> bb q t", bb=64)
    o2v = out_dram.rearrange("(q bb) t -> bb q t", bb=64)
    yg = w["yg"]
    xp2 = w["xin2"][0]
    p.s_ve.wait(te)
    sto_tot = (p.s_ug[0].n, p.s_ug[1].n)
    u2_gd = [p.s_gd.n]
    te2_at = [p.s_te.n]
    se2_at = [p.s_se.n] * 4
    ve2_at = [p.s_ve.n]
    st2_at = [p.s_sto[0].n]
    for pair in range(32):
        b0 = 2 * pair
        p.s_ug[0].wait(gp, sto_tot[0])
        p.s_ug[1].wait(gp, sto_tot[1])
        p.s_te.wait(gp, te2_at[0])
        p.s_gd.inc(gp.dma_start(out=ustage[0:64, :], in_=u2v[b0]), 16)
        u2_gd[0] = p.s_gd.inc(gp.dma_start(out=ustage[64:128, :], in_=u2v[b0 + 1]), 16)
        p.s_ve.wait(sy, ve2_at[0])
        p.s_sto[0].wait(sy, st2_at[0])
        p.s_x[0].inc(sy.dma_start(out=xp2[0:64, :], in_=x2v[b0]), 16)
        x_ld = p.s_x[0].inc(sy.dma_start(out=xp2[64:128, :], in_=x2v[b0 + 1]), 16)
        p.s_gd.wait(te, u2_gd[0])
        for sub in range(2):
            for th in range(2):
                b = banks[2 * sub + th]
                p.s_se.wait(te, se2_at[2 * sub + th])
                ins = mm(b[0:64, :], wf["B_sb"][sub * 64:sub * 64 + 64, :],
                         ustage[sub * 64:sub * 64 + 64, th * 512:(th + 1) * 512],
                         start=True, stop=True, skip_group_check=True)
            te2_at[0] = p.s_te.inc(ins)
            p.s_te.wait(se, te2_at[0])
            p.s_ve.wait(se, ve2_at[0])
            se.activation(yg[sub * 64:sub * 64 + 64, 0:512],
                          banks[2 * sub][0:64, :], AF.Gelu)
            se2_at[2 * sub] = se2_at[2 * sub + 1] = p.s_se.inc(
                se.activation(yg[sub * 64:sub * 64 + 64, 512:1024],
                              banks[2 * sub + 1][0:64, :], AF.Gelu))
        p.s_se.wait(v)
        p.s_x[0].wait(v, x_ld)
        t1 = w["t1"]
        v.tensor_sub(t1[:], xp2[:], w["mu_b"][:])
        v.tensor_mul(t1[:], t1[:], w["sc2_b"][:])
        v.tensor_add(t1[:], t1[:], xp2[:])
        ve2_at[0] = p.s_ve.inc(v.scalar_tensor_tensor(
            xp2[:], yg[:], wf["biasp_cols"][:, pair:pair + 1], t1[:],
            op0=ALU.add, op1=ALU.add))
        p.s_ve.wait(sy, ve2_at[0])
        p.s_sto[0].inc(sy.dma_start(out=o2v[b0], in_=xp2[0:64, :]), 16)
        st2_at[0] = p.s_sto[0].inc(sy.dma_start(out=o2v[b0 + 1], in_=xp2[64:128, :]), 16)
    last_sem_holder["sem"] = p.s_sto[0]
    last_sem_holder["val"] = p.s_sto[0].n
    barrier(p)
    sctx.close()


def sync_all(p, eng):
    for s in (p.s_ld, p.s_gd, p.s_te, p.s_ve, p.s_se, p.s_gp, p.s_cc,
              p.s_x[0], p.s_x[1], p.s_sto[0], p.s_sto[1], p.s_wt,
              p.s_const, p.s_z, p.s_zg, p.s_ug[0], p.s_ug[1],
              p.s_kv[0], p.s_kv[1], p.s_q[0], p.s_q[1], p.s_w2):
        s.wait(eng)


def barrier(p):
    for eng in p.engs:
        sync_all(p, eng)
        p.s_bar.inc(eng.nop())
    for eng in p.engs:
        p.s_bar.wait(eng)


def small_stats(p, w, src_sb, col0, gcols_pair, dst_sb, dcol0):
    v, se, te = p.ve, p.se, p.te
    pb = w["pb"]
    sync_all(p, te)
    sync_all(p, se)
    sqt = w["t1"]
    for ct in range(4):
        sl = src_sb[:, col0 + ct * T:col0 + (ct + 1) * T]
        p.s_ve.wait(se)
        p.s_te.wait(se)
        p.s_se.inc(se.activation(sqt[:], sl, AF.Square))
        p.s_se.wait(te)
        st, sp = ct == 0, ct == 3
        for th in range(2):
            te.matmul(pb[th][0:1, :], w["ones_f32"][:], sl[:, th * 512:(th + 1) * 512],
                      start=st, stop=sp, skip_group_check=True)
            ins = te.matmul(pb[2 + th][0:1, :], w["ones_f32"][:],
                            sqt[:, th * 512:(th + 1) * 512],
                            start=st, stop=sp, skip_group_check=True)
        p.s_te.inc(ins)
    mu, ex2, var, tmp = w["r0"], w["r1"], w["r2"], w["r3"]
    p.s_te.wait(v)
    inv_d = float(1.0 / DIM)
    v.tensor_scalar_mul(mu[0:1, 0:512], pb[0][0:1, :], inv_d)
    v.tensor_scalar_mul(mu[0:1, 512:1024], pb[1][0:1, :], inv_d)
    v.tensor_scalar_mul(ex2[0:1, 0:512], pb[2][0:1, :], inv_d)
    v.tensor_scalar_mul(ex2[0:1, 512:1024], pb[3][0:1, :], inv_d)
    v.tensor_mul(tmp[0:1, :], mu[0:1, :], mu[0:1, :])
    v.tensor_sub(var[0:1, :], ex2[0:1, :], tmp[0:1, :])
    v.tensor_scalar_add(tmp[0:1, :], var[0:1, :], float(EPS))
    p.s_ve.inc(v.reciprocal(tmp[0:1, :], tmp[0:1, :]))
    p.s_ve.wait(se)
    p.s_se.inc(se.activation(w["rstd"][0:1, :], tmp[0:1, :], AF.Sqrt))
    bcast_rows(p, [(mu, w["mu_b"]), (w["rstd"], w["sc_b"])], w)
    gc, bc = gcols_pair
    for ct in range(4):
        sl = src_sb[:, col0 + ct * T:col0 + (ct + 1) * T]
        dst = dst_sb[:, dcol0 + ct * T:dcol0 + (ct + 1) * T]
        v.tensor_sub(w["t1"][:], sl, w["mu_b"][:])
        v.tensor_mul(w["t1"][:], w["t1"][:], w["sc_b"][:])
        p.s_ve.inc(v.tensor_scalar(dst, w["t1"][:], gc[:, ct:ct + 1],
                                   bc[:, ct:ct + 1], op0=ALU.mult, op1=ALU.add))


def emit_s1(p, x_dram, out_dram, ws, w, dram, dbg):
    nc = p.nc
    v, se, te, gp, sy = p.ve, p.se, p.te, p.gp, p.sy
    mm = te.matmul

    octx = ExitStack()
    h_sb = octx.enter_context(nc.sbuf_tensor("h_sb", [128, 4 * T], F32))
    y2_sb = octx.enter_context(nc.sbuf_tensor("y2_sb", [128, 4 * T], BF16))
    wbar = octx.enter_context(nc.sbuf_tensor("wbar", [128, 4], F32))

    sctx = ExitStack()
    w["mirror"] = sctx.enter_context(nc.sbuf_tensor("s1_mirror", [128, NKT * T], BF16))
    win_sb = sctx.enter_context(nc.sbuf_tensor("win_sb", [128, NKT * DIM], BF16))
    emit_stats(p, x_dram, w, double_ln=False, keep_cast=True)
    mirror = w["mirror"]
    if True:
        for q in range(4):
            p.s_wt.inc(gp.dma_start(
                out=win_sb[:, q * 8 * DIM:(q + 1) * 8 * DIM],
                in_=dram["win_img"][:, q * 8 * DIM:(q + 1) * 8 * DIM]), 16)
        p.s_wt.wait(te)
        p.s_ve.wait(te)
        for ct in range(4):
            p.s_ve.wait(te)
            for kt in range(NKT):
                ins = mm(w["pb"][5][0:128, 0:1],
                         win_sb[:, kt * DIM + ct * 128:kt * DIM + (ct + 1) * 128],
                         w["ones_bf"][:], start=kt == 0, stop=kt == NKT - 1,
                         skip_group_check=True)
            p.s_te.inc(ins)
            p.s_te.wait(v)
            wb_at = p.s_ve.inc(v.tensor_copy(wbar[:, ct:ct + 1], w["pb"][5][0:128, 0:1]))
            v.wait_ge(p.s_ve.h, wb_at)
        p.s_ve.wait(v)
        p.s_ve.inc(v.tensor_mul(w["r1"][0:1, :], w["r0"][0:1, :], w["rstd"][0:1, :]))
        bcast_rows(p, [(w["r1"], w["mu_b"]), (w["rstd"], w["sc_b"])], w)

        embank = [w["pb"][0], w["pb"][1]]
        ve_emb = [p.s_ve.n] * 2
        for ct in range(4):
            for th in range(2):
                slot = (2 * ct + th) % 2
                bank = embank[slot]
                p.s_ve.wait(te, ve_emb[slot])
                for kt in range(NKT):
                    ins = mm(bank[:, :],
                             win_sb[:, kt * DIM + ct * 128:kt * DIM + (ct + 1) * 128],
                             mirror[:, kt * T + th * 512:kt * T + (th + 1) * 512],
                             start=kt == 0, stop=kt == NKT - 1, skip_group_check=True)
                p.s_te.inc(ins)
                p.s_te.wait(v)
                dst = h_sb[:, ct * T + th * 512:ct * T + (th + 1) * 512]
                mu_sl = w["mu_b"][:, th * 512:(th + 1) * 512]
                sc_sl = w["sc_b"][:, th * 512:(th + 1) * 512]
                v.scalar_tensor_tensor(w["t1"][:, 0:512], mu_sl, wbar[:, ct:ct + 1],
                                       bank[:, :], op0=ALU.mult, op1=ALU.subtract)
                v.scalar_tensor_tensor(w["t1"][:, 512:1024], w["t1"][:, 0:512], -1.0,
                                       sc_sl, op0=ALU.mult, op1=ALU.mult)
                ve_emb[slot] = p.s_ve.inc(
                    v.tensor_scalar(dst, w["t1"][:, 512:1024],
                                    ws["bin_cols"][:, ct:ct + 1], None, op0=ALU.add))
        if dbg is not None and "h_embed" in dbg:
            emit_dump(p, dbg["h_embed"], h_sb, 4)
    barrier(p)
    sctx.close()

    sctx = ExitStack()
    y_sb = sctx.enter_context(nc.sbuf_tensor("y_sb", [128, 4 * T], F32))
    u_sb = sctx.enter_context(nc.sbuf_tensor("u_sb", [128, 16 * 512], BF16))
    w1_sb = sctx.enter_context(nc.sbuf_tensor("w1_sb", [128, 4 * DFF], BF16))
    w2_sb = sctx.enter_context(nc.sbuf_tensor("w2_sb", [128, 16 * 512], BF16))
    vre = sctx.enter_context(nc.sbuf_tensor("vre", [128, T], F32))
    vim = sctx.enter_context(nc.sbuf_tensor("vim", [128, T], F32))
    cost = sctx.enter_context(nc.sbuf_tensor("cost", [128, 4 * T], F32))
    sint = sctx.enter_context(nc.sbuf_tensor("sint", [128, 4 * T], F32))
    xmre = sctx.enter_context(nc.sbuf_tensor("xmre", [128, T], F32))
    xmim = sctx.enter_context(nc.sbuf_tensor("xmim", [128, T], F32))
    rtile4 = sctx.enter_context(nc.sbuf_tensor("rtile4", [128, 4 * 512], F32))
    sc1 = sctx.enter_context(nc.sbuf_tensor("sc1", [128, 176], F32))
    zsb = sctx.enter_context(nc.sbuf_tensor("zsb", [128, 16], F32))
    zg_sb = sctx.enter_context(nc.sbuf_tensor("zg_sb", [16, 512], F32))
    zt_sb = sctx.enter_context(nc.sbuf_tensor("zt_sb", [128, 64], F32))
    if True:
        for lay in range(AD):
            sync_all(p, gp)
            for q in range(4):
                p.s_wt.inc(gp.dma_start(
                    out=w1_sb[:, q * DFF:(q + 1) * DFF],
                    in_=dram["w1_img"][lay, :, q * DFF:(q + 1) * DFF]), 16)
                p.s_wt.inc(gp.dma_start(
                    out=w2_sb[:, q * 2048:(q + 1) * 2048],
                    in_=dram["w2_img"][lay, :, q * 2048:(q + 1) * 2048]), 16)
            small_stats(p, w, h_sb, 0,
                        (ws["ln1_w"][lay], ws["ln1_b"][lay]), y_sb, 0)
            sync_all(p, se)
            for ct in range(4):
                p.s_se.inc(se.activation(y_sb[:, ct * T:(ct + 1) * T],
                                         y_sb[:, ct * T:(ct + 1) * T], AF.Silu))
            if dbg is not None and f"y{lay}" in dbg:
                emit_dump(p, dbg[f"y{lay}"], y_sb, 4)
            emit_sconv(p, w, ws, lay, y_sb, h_sb, vre, vim, cost, sint, xmre,
                       xmim, rtile4, sc1, zsb, zg_sb, zt_sb, dram, dbg)
            if dbg is not None and f"h_mid{lay}" in dbg:
                emit_dump(p, dbg[f"h_mid{lay}"], h_sb, 4)
            small_stats(p, w, h_sb, 0,
                        (ws["ln2_w"][lay], ws["ln2_b"][lay]), y2_sb, 0)
            for th in range(2):
                sync_all(p, te)
                for ft in range(16):
                    bank = w["pb"][ft % 2]
                    for ct in range(4):
                        ins = mm(bank[:, :],
                                 w1_sb[:, ct * DFF + ft * 128:ct * DFF + (ft + 1) * 128],
                                 y2_sb[:, ct * T + th * 512:ct * T + (th + 1) * 512],
                                 start=ct == 0, stop=ct == 3, skip_group_check=True)
                    p.s_te.inc(ins)
                    p.s_te.wait(se)
                    p.s_se.inc(se.activation(
                        u_sb[:, ft * 512:(ft + 1) * 512],
                        bank[:, :], AF.Silu, bias=ws["b1_cols"][lay][:, ft:ft + 1]))
                    if ft >= 1:
                        p.s_se.wait(te)
                sync_all(p, te)
                for ct in range(4):
                    bank = w["pb"][2 + ct % 2]
                    for ft in range(16):
                        ins = mm(bank[:, :],
                                 w2_sb[:, ft * 512 + ct * 128:ft * 512 + (ct + 1) * 128],
                                 u_sb[:, ft * 512:(ft + 1) * 512],
                                 start=ft == 0, stop=ft == 15, skip_group_check=True)
                    p.s_te.inc(ins)
                    p.s_te.wait(v)
                    dst = h_sb[:, ct * T + th * 512:ct * T + (th + 1) * 512]
                    p.s_ve.inc(v.scalar_tensor_tensor(
                        dst, bank[:, :], ws["b2_cols"][lay][:, ct:ct + 1], dst,
                        op0=ALU.add, op1=ALU.add))
                    p.s_ve.wait(te)
            if dbg is not None and f"h{lay}" in dbg:
                emit_dump(p, dbg[f"h{lay}"], h_sb, 4)

    small_stats(p, w, h_sb, 0, (ws["lnl_w"], ws["lnl_b"]), y2_sb, 0)
    barrier(p)
    sctx.close()

    sctx = ExitStack()
    wout_sb = sctx.enter_context(nc.sbuf_tensor("wout_sb", [128, 4 * D], BF16))
    if True:
        sync_all(p, gp)
        for q in range(4):
            p.s_wt.inc(gp.dma_start(
                out=wout_sb[:, q * 4096:(q + 1) * 4096],
                in_=dram["wout_img"][:, q * 4096:(q + 1) * 4096]), 16)
        p.s_wt.wait(te)
        xin = w["xin2"]
        obank = [w["pb"][0], w["pb"][1]]
        ld_at = [p.s_x[0].n, p.s_x[1].n]
        st_at = [p.s_sto[0].n, p.s_sto[1].n]
        ve_at = [p.s_ve.n] * 2
        last_stt = [p.s_ve.n]
        for ot in range(NKT):
            s = ot % 2
            p.s_ve.wait(sy, ve_at[s])
            p.s_sto[s].wait(sy, st_at[s])
            ld_at[s] = p.s_x[s].inc(
                sy.dma_start(out=xin[s][:], in_=x_dram[ot * 128:(ot + 1) * 128, :]), 16)
            for th in range(2):
                bank = obank[th]
                p.s_ve.wait(te, last_stt[0])
                for ct in range(4):
                    ins = mm(bank[:, :],
                             wout_sb[:, ct * 4096 + ot * 128:ct * 4096 + (ot + 1) * 128],
                             y2_sb[:, ct * T + th * 512:ct * T + (th + 1) * 512],
                             start=ct == 0, stop=ct == 3, skip_group_check=True)
                p.s_te.inc(ins)
            p.s_te.wait(v)
            p.s_x[s].wait(v, ld_at[s])
            v.scalar_tensor_tensor(xin[s][:, 0:512], obank[0][:, :],
                                   ws["bout_cols"][:, ot:ot + 1], xin[s][:, 0:512],
                                   op0=ALU.add, op1=ALU.add)
            ve_at[s] = p.s_ve.inc(v.scalar_tensor_tensor(
                xin[s][:, 512:1024], obank[1][:, :], ws["bout_cols"][:, ot:ot + 1],
                xin[s][:, 512:1024], op0=ALU.add, op1=ALU.add))
            last_stt[0] = ve_at[s]
            p.s_ve.wait(sy, ve_at[s])
            st_at[s] = p.s_sto[s].inc(
                sy.dma_start(out=out_dram[ot * 128:(ot + 1) * 128, :],
                             in_=xin[s][:]), 16)
    barrier(p)
    sctx.close()
    octx.close()


def _cmul_sq(v, c, s, ta, tb):
    v.tensor_mul(ta, c, c)
    v.tensor_mul(tb, s, s)
    v.tensor_mul(s, c, s)
    v.tensor_scalar_mul(s, s, 2.0)
    v.tensor_sub(c, ta, tb)


def emit_sconv(p, w, ws, lay, y_sb, h_sb, vre, vim, cost, sint, xmre, xmim,
               rtile4, sc1, zsb, zg_sb, zt_sb, dram, dbg):
    nc = p.nc
    v, se, te, gp, sy = p.ve, p.se, p.te, p.gp, p.sy
    sv = SV(p)
    X = mybir.AxisListType.X
    C = lambda i: sc1[:, i:i + 1]
    MAG, RR, CTH, STH = 12, 13, 14, 15
    EC, ES, TA, TB, NEG = 16, 17, 18, 19, 20
    CG0, SG0, CG1, SG1 = 21, 22, 23, 24
    APC, APS = 25, 33
    VEC, VES = 41, 50
    T8 = 59
    phz_re, phz_im = ws["phz_re"][lay], ws["phz_im"][lay]
    lc_re, lc_im = ws["lc_re"][lay], ws["lc_im"][lay]
    scl = ws["scl"][lay]

    MAG4, RR4, CTH4, STH4, TA4, TB4 = 68, 72, 76, 80, 84, 88
    B4 = lambda i: sc1[:, i:i + 4]
    sync_all(p, v)
    sync_all(p, se)
    sv.tensor_mul(B4(TA4), phz_re[:, 0:4], phz_re[:, 0:4])
    sv.tensor_mul(B4(TB4), phz_im[:, 0:4], phz_im[:, 0:4])
    sv.tensor_add(B4(MAG4), B4(TA4), B4(TB4))
    sv.tensor_scalar_add(B4(MAG4), B4(MAG4), 1e-30)
    p.s_ve.wait(se)
    sq_at = p.s_se.inc(se.activation(B4(MAG4), B4(MAG4), AF.Sqrt))
    p.s_se.wait(se, sq_at)
    p.s_se.inc(se.activation(B4(RR4), B4(MAG4), AF.Tanh))
    p.s_se.wait(v)
    sv.reciprocal(B4(TA4), B4(MAG4))
    sv.tensor_mul(B4(CTH4), phz_re[:, 0:4], B4(TA4))
    sv.tensor_mul(B4(STH4), phz_im[:, 0:4], B4(TA4))
    sv.tensor_mul(B4(TA4), B4(CTH4), lc_re[:, 0:4])
    sv.tensor_mul(B4(TB4), B4(STH4), lc_im[:, 0:4])
    sv.tensor_sub(sc1[:, 4:8], B4(TA4), B4(TB4))
    sv.tensor_mul(B4(TA4), B4(STH4), lc_re[:, 0:4])
    sv.tensor_mul(B4(TB4), B4(CTH4), lc_im[:, 0:4])
    sv.tensor_add(sc1[:, 8:12], B4(TA4), B4(TB4))
    sv.tensor_scalar_mul(sc1[:, 8:12], sc1[:, 8:12], -1.0)
    EC4, ES4 = 92, 96
    sv.tensor_copy(sc1[:, 0:4], B4(RR4))
    sv.tensor_copy(B4(EC4), B4(CTH4))
    sv.tensor_copy(B4(ES4), B4(STH4))
    for _ in range(9):
        sv.tensor_mul(sc1[:, 0:4], sc1[:, 0:4], sc1[:, 0:4])
        _cmul_sq(sv, B4(EC4), B4(ES4), B4(TA4), B4(TB4))
    for ct in range(4):
        sync_all(p, v)
        sync_all(p, se)
        ecol, scol = sc1[:, EC4 + ct:EC4 + ct + 1], sc1[:, ES4 + ct:ES4 + ct + 1]
        sv.memset(C(APC), 1.0)
        sv.memset(C(APS), 0.0)
        for k in range(7):
            sv.tensor_mul(C(TA), C(APC + k), ecol)
            sv.tensor_mul(C(TB), C(APS + k), scol)
            sv.tensor_sub(C(APC + k + 1), C(TA), C(TB))
            sv.tensor_mul(C(TA), C(APS + k), ecol)
            sv.tensor_mul(C(TB), C(APC + k), scol)
            sv.tensor_add(C(APS + k + 1), C(TA), C(TB))
        for (cg, sg, ohb) in ((CG0, SG0, w["oh0_b"]), (CG1, SG1, w["oh1_b"])):
            sv.tensor_mul(sc1[:, T8:T8 + 8], sc1[:, APC:APC + 8], ohb[:])
            sv.tensor_reduce(C(cg), sc1[:, T8:T8 + 8], X, ALU.add)
            sv.tensor_mul(sc1[:, T8:T8 + 8], sc1[:, APS:APS + 8], ohb[:])
            sv.tensor_reduce(C(sg), sc1[:, T8:T8 + 8], X, ALU.add)
        sv.memset(xmre[:, 0:1], 1.0)
        sv.memset(xmim[:, 0:1], 0.0)
        sv.tensor_copy(C(EC), sc1[:, CTH4 + ct:CTH4 + ct + 1])
        sv.tensor_copy(C(ES), sc1[:, STH4 + ct:STH4 + ct + 1])
        n = 1
        while n < 512:
            sv.tensor_scalar_mul(C(NEG), C(ES), -1.0)
            sv.tensor_scalar(vre[:, 0:n], xmre[:, 0:n], C(EC), None, op0=ALU.mult)
            sv.scalar_tensor_tensor(xmre[:, n:2 * n], xmim[:, 0:n], C(NEG),
                                    vre[:, 0:n], op0=ALU.mult, op1=ALU.add)
            sv.tensor_scalar(vre[:, 0:n], xmim[:, 0:n], C(EC), None, op0=ALU.mult)
            sv.scalar_tensor_tensor(xmim[:, n:2 * n], xmre[:, 0:n], C(ES),
                                    vre[:, 0:n], op0=ALU.mult, op1=ALU.add)
            _cmul_sq(sv, C(EC), C(ES), C(TA), C(TB))
            n *= 2
        for (sl, cg, sg) in ((0, CG0, SG0), (1, CG1, SG1)):
            cd = cost[:, ct * T + sl * 512:ct * T + (sl + 1) * 512]
            sd = sint[:, ct * T + sl * 512:ct * T + (sl + 1) * 512]
            sv.tensor_scalar_mul(C(NEG), C(sg), -1.0)
            sv.tensor_scalar(vre[:, 0:512], xmre[:, 0:512], C(cg), None, op0=ALU.mult)
            sv.scalar_tensor_tensor(cd, xmim[:, 0:512], C(NEG), vre[:, 0:512],
                                   op0=ALU.mult, op1=ALU.add)
            sv.tensor_scalar(vre[:, 0:512], xmim[:, 0:512], C(cg), None, op0=ALU.mult)
            sv.scalar_tensor_tensor(sd, xmre[:, 0:512], C(sg), vre[:, 0:512],
                                   op0=ALU.mult, op1=ALU.add)
        sv.memset(rtile4[:, ct * 512:(ct + 1) * 512], 1.0)
        sv.tensor_scalar(rtile4[:, ct * 512:(ct + 1) * 512],
                         rtile4[:, ct * 512:(ct + 1) * 512],
                         sc1[:, RR4 + ct:RR4 + ct + 1], None, op0=ALU.mult)
        ysl = y_sb[:, ct * T:(ct + 1) * T]
        sv.tensor_mul(xmre[:], ysl, cost[:, ct * T:(ct + 1) * T])
        sv.tensor_mul(xmim[:], ysl, sint[:, ct * T:(ct + 1) * T])
        for sl in range(2):
            for comp, (vv, xm) in enumerate(((vre, xmre), (vim, xmim))):
                sv.tensor_tensor_scan(
                    vv[:, sl * 512:(sl + 1) * 512],
                    rtile4[:, ct * 512:(ct + 1) * 512],
                    xm[:, sl * 512:(sl + 1) * 512], 0.0,
                    op0=ALU.mult, op1=ALU.add)
                sv.tensor_copy(
                    zsb[:, ct * 4 + sl * 2 + comp:ct * 4 + sl * 2 + comp + 1],
                    vv[:, sl * 512 + 511:sl * 512 + 512])
        if dbg is not None and "scdbg" in dbg and lay == 0 and ct == 0:
            dd = dbg["scdbg"]
            p.s_ve.wait(sy)
            for bi, (tile, wdt) in enumerate((
                    (cost[:, 0:T], T), (sint[:, 0:T], T), (xmre[:], T),
                    (xmim[:], T), (vre[:], T), (vim[:], T),
                    (rtile4[:, 0:512], 512), (sc1[:, 0:72], 72))):
                p.s_z.inc(sy.dma_start(out=dd[bi * 128:bi * 128 + 128, 0:wdt],
                                       in_=tile), 16)
            p.s_z.wait(v)
    sync_all(p, te)
    p.s_te.inc(te.transpose(w["pbt"][0:16, 0:128], zsb[:], w["ident"][:]))
    p.s_te.wait(v)
    sv.tensor_copy(zg_sb[0:16, 0:128], w["pbt"][0:16, 0:128])
    p.s_ve.wait(sy)
    for ct in range(4):
        p.s_z.inc(sy.dma_start(out=dram["z_src"][0:4, ct * 128:(ct + 1) * 128],
                               in_=zg_sb[ct * 4:(ct + 1) * 4, 0:128]), 16)
    p.s_z.wait(gp)
    p.s_cc.inc(gp.collective_compute(
        "AllGather", ALU.bypass,
        replica_groups=[[0, 1, 2, 3], [4, 5, 6, 7]],
        ins=[dram["z_src"][:, :].opt()],
        outs=[dram["z_gth"][:, :].opt()]), 1)
    p.s_cc.wait(gp)
    p.s_zg.inc(gp.dma_start(out=zg_sb[:], in_=dram["z_gth"][:, :]), 16)
    p.s_zg.wait(te)
    for ct in range(4):
        p.s_ve.wait(te)
        p.s_te.inc(te.transpose(w["pbt"][0:128, 0:16],
                                zg_sb[0:16, ct * 128:(ct + 1) * 128],
                                w["ident"][0:16, 0:16]))
        p.s_te.wait(v)
        sv.tensor_copy(zt_sb[:, ct * 16:(ct + 1) * 16], w["pbt"][0:128, 0:16])
    VEC4, VES4 = 100, 136
    sync_all(p, v)
    sv.tensor_copy(sc1[:, VEC4:VEC4 + 4], sc1[:, 4:8])
    sv.tensor_copy(sc1[:, VES4:VES4 + 4], sc1[:, 8:12])
    ztv = zt_sb[:].rearrange("p (ct c) -> p ct c", c=16)
    for k in range(8):
        rk = k if k < 4 else 7 - k
        slk = 0 if k < 4 else 1
        for comp, base in ((0, VEC4), (1, VES4)):
            zc = ztv[:, :, rk * 4 + slk * 2 + comp]
            a, b = sc1[:, base + k * 4:base + (k + 1) * 4], \
                sc1[:, base + (k + 1) * 4:base + (k + 2) * 4]
            sv.tensor_mul(b, a, sc1[:, 0:4])
            sv.tensor_add(b, b, zc)
    for ct in range(4):
        sync_all(p, v)
        for (sl, ohb, ic, isv) in ((0, w["oh0_b"], TA, TB), (1, w["oh1_b"], EC, ES)):
            vv_c = sc1[:].rearrange("p (k c) -> p k c", c=4)
            sv.tensor_mul(sc1[:, T8:T8 + 8],
                          vv_c[:, VEC4 // 4:VEC4 // 4 + 8, ct], ohb[:])
            sv.tensor_reduce(C(ic), sc1[:, T8:T8 + 8], X, ALU.add)
            sv.tensor_mul(sc1[:, T8:T8 + 8],
                          vv_c[:, VES4 // 4:VES4 // 4 + 8, ct], ohb[:])
            sv.tensor_reduce(C(isv), sc1[:, T8:T8 + 8], X, ALU.add)
        ysl = y_sb[:, ct * T:(ct + 1) * T]
        sv.tensor_mul(xmre[:], ysl, cost[:, ct * T:(ct + 1) * T])
        sv.tensor_mul(xmim[:], ysl, sint[:, ct * T:(ct + 1) * T])
        for sl, (icol, iscol) in ((0, (TA, TB)), (1, (EC, ES))):
            sv.tensor_tensor_scan(vre[:, sl * 512:(sl + 1) * 512],
                                 rtile4[:, ct * 512:(ct + 1) * 512],
                                 xmre[:, sl * 512:(sl + 1) * 512], C(icol),
                                 op0=ALU.mult, op1=ALU.add)
            sv.tensor_tensor_scan(vim[:, sl * 512:(sl + 1) * 512],
                                 rtile4[:, ct * 512:(ct + 1) * 512],
                                 xmim[:, sl * 512:(sl + 1) * 512], C(iscol),
                                 op0=ALU.mult, op1=ALU.add)
        if dbg is not None and "scdbg2" in dbg and lay == 0 and ct == 0:
            dd = dbg["scdbg2"]
            p.s_ve.wait(sy)
            for bi, (tile, wdt) in enumerate((
                    (zt_sb[:, 0:64], 64), (sc1[:, 0:72], 72), (vre[:], T),
                    (vim[:], T), (xmre[:], T), (xmim[:], T),
                    (rtile4[:, 0:512], 512), (zg_sb[0:16, 0:512], 512))):
                p.s_z.inc(sy.dma_start(out=dd[bi * 128:bi * 128 + 128, 0:wdt]
                                       if bi != 7 else dd[bi * 128:bi * 128 + 16, 0:wdt],
                                       in_=tile), 16)
            p.s_z.wait(v)
        sv.tensor_mul(xmre[:], cost[:, ct * T:(ct + 1) * T], vre[:])
        sv.tensor_mul(xmim[:], sint[:, ct * T:(ct + 1) * T], vim[:])
        sv.tensor_add(xmre[:], xmre[:], xmim[:])
        if dbg is not None and f"conv{lay}" in dbg:
            sv.tensor_copy(xmim[:], xmre[:])
            p.s_ve.wait(sy)
            p.s_z.inc(sy.dma_start(
                out=dbg[f"conv{lay}"][ct * 128:(ct + 1) * 128, :], in_=xmim[:]), 16)
        hs = h_sb[:, ct * T:(ct + 1) * T]
        sv.scalar_tensor_tensor(hs, xmre[:], scl[:, ct:ct + 1], hs,
                                op0=ALU.mult, op1=ALU.add)


NSB = 16
SBW = 256
INV_SQRT_HD = float(1.0 / np.sqrt(HD))
EXP_SHIFT = -12.0


def s3_positions():
    pos = []
    bias_idx = 0
    for sl in (0, 1):
        n_off = 4 if sl == 0 else 7
        for ch in range(n_off):
            rank = ch if ch < 4 else 7 - ch
            slot = 0 if ch < 4 else 1
            pos.append((sl, "g", rank, slot, bias_idx))
            bias_idx += 1
        pos.append((sl, "d", 0, sl, bias_idx))
        bias_idx += 1
    return pos


S3_POS = s3_positions()
S3_NBIAS = 13


def emit_qkvw_proj(p, w, wimg_dram, bias_cols, dst_writer, wbuf, mirror):
    te, v, gp = p.te, p.ve, p.gp
    mm = te.matmul
    wslot = [p.s_wt.n, p.s_w2.n]
    wsems = [p.s_wt, p.s_w2]
    te_sb = [p.s_te.n] * 2
    for sbi in range(NSB):
        s = sbi % 2
        p.s_te.wait(gp, te_sb[s])
        wslot[s] = wsems[s].inc(gp.dma_start(
            out=wbuf[s][:], in_=wimg_dram[sbi]), 16)
        wsems[s].wait(te, wslot[s])
        for ft in range(2):
            for th in range(2):
                bank = w["pb"][2 * ft + th]
                dst_writer("guard", bank, None, None, None)
                for kt in range(NKT):
                    ins = mm(bank[:, :],
                             wbuf[s][:, kt * SBW + ft * 128:kt * SBW + (ft + 1) * 128],
                             mirror[:, kt * T + th * 512:kt * T + (th + 1) * 512],
                             start=kt == 0, stop=kt == NKT - 1,
                             skip_group_check=True)
                p.s_te.inc(ins)
                te_sb[s] = p.s_te.n
                dst_writer("use", bank, sbi, ft, th)


def emit_s3(p, x_dram, out_dram, wa, w, dram, dbg):
    nc = p.nc
    v, se, te, gp, sy = p.ve, p.se, p.te, p.gp, p.sy
    sv = SV(p)
    mm = te.matmul

    octx = ExitStack()

    sctx = ExitStack()
    w["mirror"] = sctx.enter_context(nc.sbuf_tensor("s3_mirror", [128, NKT * T], BF16))
    wbuf = [sctx.enter_context(nc.sbuf_tensor(f"s3_wb{i}", [128, NKT * SBW], BF16))
            for i in range(2)]
    kvstage = sctx.enter_context(nc.sbuf_tensor("kvstage", [128, T], BF16))
    bvb = sctx.enter_context(nc.sbuf_tensor("bvb", [128, SBW], F32))
    bvrow = sctx.enter_context(nc.sbuf_tensor("bvrow", [1, SBW], F32))
    emit_stats(p, x_dram, w, double_ln=False)
    emit_mirror(p, x_dram, w, None, None)
    mirror = w["mirror"]

    q_sto = [0]

    def q_writer(mode, bank, sbi, ft, th):
        if mode == "guard":
            p.s_ve.wait(te)
            return
        p.s_te.wait(v)
        p.s_sto[1].wait(v, q_sto[0])
        fcol = sbi * SBW + ft * 128
        p.s_ve.inc(v.tensor_scalar(
            kvstage[:, th * 512:(th + 1) * 512], bank[:, :],
            wa["bq_cols"][:, fcol // 128:fcol // 128 + 1], None, op0=ALU.add))
        if th == 1:
            p.s_ve.wait(sy)
            q_sto[0] = p.s_sto[1].inc(sy.dma_start(
                out=dram["q_dram"][fcol:fcol + 128, :], in_=kvstage[:]), 16)
    emit_qkvw_proj(p, w, dram["wq_img"], None, q_writer, wbuf, mirror)

    k_sto = [0]

    def k_writer(mode, bank, sbi, ft, th):
        if mode == "guard":
            p.s_ve.wait(te)
            return
        p.s_te.wait(v)
        p.s_sto[0].wait(v, k_sto[0])
        fcol = sbi * SBW + ft * 128
        p.s_ve.inc(v.tensor_scalar(
            kvstage[:, th * 512:(th + 1) * 512], bank[:, :],
            wa["bk_cols"][:, fcol // 128:fcol // 128 + 1], None, op0=ALU.add))
        if th == 1:
            p.s_ve.wait(sy)
            k_sto[0] = p.s_sto[0].inc(sy.dma_start(
                out=dram["kv_src"][fcol:fcol + 128, :], in_=kvstage[:]), 16)
    emit_qkvw_proj(p, w, dram["wk_img"], None, k_writer, wbuf, mirror)

    vreg = dram["kv_src"].rearrange("(r c) t -> r c t", c=4)
    v_sto = [0]
    te_sbv = [p.s_te.n] * 2
    for sbi in range(NSB):
        s = sbi % 2
        p.s_te.wait(gp, te_sbv[s])
        wsem = p.s_wt if s == 0 else p.s_w2
        wld = wsem.inc(gp.dma_start(out=wbuf[s][:], in_=dram["wv_img"][sbi]), 16)
        p.s_te.wait(sy)
        bvld = p.s_x[0].inc(sy.dma_start(
            out=bvrow[:], in_=dram["bv_row"][0:1, sbi * SBW:(sbi + 1) * SBW]), 16)
        p.s_x[0].wait(te, bvld)
        p.s_ve.wait(te)
        p.s_te.inc(te.matmul(w["pb"][5][:, 0:SBW], w["ones_row"][:],
                             bvrow[0:1, :], start=True, stop=True,
                             skip_group_check=True))
        p.s_te.wait(v)
        p.s_ve.inc(v.tensor_copy(bvb[:], w["pb"][5][:, 0:SBW]))
        wsem.wait(te, wld)
        for tt in range(8):
            bank = w["pb"][tt % 2]
            p.s_ve.wait(te)
            for kt in range(NKT):
                ins = mm(bank[:, 0:SBW],
                         mirror[:, kt * T + tt * 128:kt * T + (tt + 1) * 128],
                         wbuf[s][:, kt * SBW:(kt + 1) * SBW],
                         start=kt == 0, stop=kt == NKT - 1, skip_group_check=True)
            p.s_te.inc(ins)
            te_sbv[s] = p.s_te.n
            p.s_te.wait(v)
            p.s_sto[1].wait(v, v_sto[0])
            p.s_ve.inc(v.tensor_add(kvstage[:, 0:SBW], bank[:, 0:SBW], bvb[:]))
            p.s_ve.wait(sy)
            f0 = sbi * SBW
            v_sto[0] = p.s_sto[1].inc(sy.dma_start(
                out=vreg[1024 + tt * 128:1024 + (tt + 1) * 128, f0 // 1024,
                         f0 % 1024:f0 % 1024 + SBW],
                in_=kvstage[0:128, 0:SBW]), 16)
    barrier(p)
    sctx.close()
    av_sb = octx.enter_context(nc.sbuf_tensor("av_sb", [128, NH * T], BF16))

    p.s_cc.inc(gp.collective_compute(
        "AllGather", ALU.bypass,
        replica_groups=[[0, 1, 2, 3], [4, 5, 6, 7]],
        ins=[dram["kv_src"][:, :].opt()],
        outs=[dram["kv_gth"].rearrange("r f t -> (r f) t").opt()]), 1)
    p.s_cc.wait(gp)
    barrier(p)

    sctx = ExitStack()
    qbuf = [sctx.enter_context(nc.sbuf_tensor(f"qb{i}", [128, 512], BF16))
            for i in range(2)]
    kst = [sctx.enter_context(nc.sbuf_tensor(f"kst{i}", [128, 512], BF16))
           for i in range(2)]
    vst = [sctx.enter_context(nc.sbuf_tensor(f"vst{i}", [128, 512], BF16))
           for i in range(2)]
    pt_sb = [sctx.enter_context(nc.sbuf_tensor(f"pt{i}", [128, 512], BF16))
             for i in range(4)]
    masks = sctx.enter_context(nc.sbuf_tensor("masks", [128, 4 * 512], BF16))
    diag_i2 = sctx.enter_context(nc.sbuf_tensor("diag_i2", [128, 512], I32))
    recipr = sctx.enter_context(nc.sbuf_tensor("recipr", [1, 512], F32))
    recipb = sctx.enter_context(nc.sbuf_tensor("recipb", [128, 512], F32))
    p.s_gp.inc(gp.iota(diag_i2[:], pattern=[[-1, 512]], base=0,
                       channel_multiplier=1))
    p.s_gp.wait(v)
    for kt in range(4):
        sv.tensor_single_scalar(masks[:, kt * 512:(kt + 1) * 512], diag_i2[:],
                                float(-kt * 128), ALU.is_le)
    kv_at = [p.s_kv[0].n, p.s_kv[1].n]
    q_at = [p.s_q[0].n, p.s_q[1].n]
    te_kv = [p.s_te.n] * 2
    te_q = [p.s_te.n] * 2
    te_pt = [p.s_te.n] * 4
    se_pt = [p.s_se.n] * 4
    ve_pt = [p.s_ve.n] * 4
    sc_at = [0] * 4
    npos = len(S3_POS)
    AVB, DENB, SCB0, BCB = 0, 1, 2, 6
    for h in range(NH):
        for pi, (sl, kind, rank, slot, bcol) in enumerate(S3_POS):
            s = pi % 2
            seg = (h * 2 + sl) % 2
            if pi == 0 or S3_POS[pi - 1][0] != sl:
                p.s_te.wait(sy, te_q[seg])
                q_at[seg] = p.s_q[seg].inc(sy.dma_start(
                    out=qbuf[seg][:],
                    in_=dram["q_dram"][h * 128:(h + 1) * 128,
                                       sl * 512:(sl + 1) * 512]), 16)
            p.s_te.wait(sy, te_kv[s])
            if kind == "g":
                ksrc = dram["kv_gth"][rank, h * 128:(h + 1) * 128,
                                      slot * 512:(slot + 1) * 512]
                vsrc = dram["kv_gth"].rearrange("r (q c) t -> r q c t", c=4)[
                    rank, 1024 + slot * 512:1024 + (slot + 1) * 512,
                    h // 8, (h % 8) * 128:(h % 8) * 128 + 128]
            else:
                ksrc = dram["kv_src"][h * 128:(h + 1) * 128,
                                      slot * 512:(slot + 1) * 512]
                vsrc = dram["kv_src"].rearrange("(q c) t -> q c t", c=4)[
                    1024 + slot * 512:1024 + (slot + 1) * 512,
                    h // 8, (h % 8) * 128:(h % 8) * 128 + 128]
            p.s_kv[s].inc(sy.dma_start(out=kst[s][:], in_=ksrc), 16)
            kv_at[s] = p.s_kv[s].inc(sy.dma_start(
                out=vst[s][:].rearrange("p (kt c) -> p kt c", kt=4),
                in_=vsrc.rearrange("(kt p) c -> p kt c", p=128)), 16)
            first = pi == 0 or S3_POS[pi - 1][0] != sl
            last = pi == npos - 1 or S3_POS[pi + 1][0] != sl
            p.s_kv[s].wait(te, kv_at[s])
            p.s_q[seg].wait(te, q_at[seg])
            for kt in range(4):
                scb = w["pb"][SCB0 + kt]
                p.s_se.wait(te, se_pt[kt])
                sc_at[kt] = p.s_te.inc(
                    mm(scb[:, :], kst[s][:, kt * 128:(kt + 1) * 128],
                       qbuf[seg][:], start=True, stop=True,
                       skip_group_check=True))
            for kt in range(4):
                p.s_te.wait(se, max(sc_at[kt], te_pt[kt]))
                se_pt[kt] = p.s_se.inc(se.activation(
                    pt_sb[kt][:], w["pb"][SCB0 + kt][:, :], AF.Exp,
                    bias=wa["kvb_cols"][:, bcol:bcol + 1],
                    scale=np.float32(INV_SQRT_HD)))
                if kind == "d":
                    p.s_se.wait(v, se_pt[kt])
                    ve_pt[kt] = p.s_ve.inc(v.tensor_mul(
                        pt_sb[kt][:], pt_sb[kt][:],
                        masks[:, kt * 512:(kt + 1) * 512]))
            for kt in range(4):
                p.s_se.wait(te, se_pt[kt])
                if kind == "d":
                    p.s_ve.wait(te, ve_pt[kt])
                st = first and kt == 0
                sp = last and kt == 3
                mm(w["pb"][AVB][:, :], vst[s][:, kt * 128:(kt + 1) * 128],
                   pt_sb[kt][:], start=st, stop=sp, skip_group_check=True)
                ins = mm(w["pb"][DENB][0:1, :], w["ones_bf"][:], pt_sb[kt][:],
                         start=st, stop=sp, skip_group_check=True)
                te_pt[kt] = p.s_te.inc(ins)
            te_kv[s] = p.s_te.n
            te_q[seg] = p.s_te.n
            if last:
                p.s_te.wait(v)
                sv.tensor_copy(recipr[0:1, :], w["pb"][DENB][0:1, :])
                sv.reciprocal(recipr[0:1, :], recipr[0:1, :])
                p.s_ve.wait(te)
                p.s_te.inc(te.matmul(w["pb"][BCB][:, :], w["ones_row"][:],
                                     recipr[0:1, :], start=True, stop=True,
                                     skip_group_check=True))
                p.s_te.wait(v)
                sv.tensor_copy(recipb[:], w["pb"][BCB][:, :])
                p.s_ve.inc(v.tensor_mul(
                    av_sb[:, h * T + sl * 512:h * T + (sl + 1) * 512],
                    w["pb"][AVB][:, :], recipb[:]))
                p.s_ve.wait(te)
    if dbg is not None and "xav" in dbg:
        emit_dump(p, dbg["xav"], av_sb, NKT)
    barrier(p)
    sctx.close()

    sctx = ExitStack()
    wobuf = [sctx.enter_context(nc.sbuf_tensor(f"s3_wo{i}", [128, NKT * SBW], BF16))
             for i in range(2)]
    xin = w["xin2"]
    wslot = [0, 0]
    ld_at = [p.s_x[0].n, p.s_x[1].n]
    st_at = [p.s_sto[0].n, p.s_sto[1].n]
    ve_at = [p.s_ve.n] * 2
    last_stt = [p.s_ve.n]
    te_sbo = [p.s_te.n] * 2
    for sbi in range(NSB):
        s2 = sbi % 2
        p.s_te.wait(gp, te_sbo[s2])
        wsem = p.s_wt if s2 == 0 else p.s_w2
        wslot[s2] = wsem.inc(gp.dma_start(out=wobuf[s2][:],
                                          in_=dram["wo_img"][sbi]), 16)
        wsem.wait(te, wslot[s2])
        for ft in range(2):
            ot = sbi * 2 + ft
            s = ot % 2
            p.s_ve.wait(sy, ve_at[s])
            p.s_sto[s].wait(sy, st_at[s])
            ld_at[s] = p.s_x[s].inc(sy.dma_start(
                out=xin[s][:], in_=x_dram[ot * 128:(ot + 1) * 128, :]), 16)
            for th in range(2):
                bank = w["pb"][th]
                p.s_ve.wait(te, last_stt[0])
                for kt in range(NKT):
                    ins = mm(bank[:, :],
                             wobuf[s2][:, kt * SBW + ft * 128:kt * SBW + (ft + 1) * 128],
                             av_sb[:, kt * T + th * 512:kt * T + (th + 1) * 512],
                             start=kt == 0, stop=kt == NKT - 1,
                             skip_group_check=True)
                p.s_te.inc(ins)
            te_sbo[s2] = p.s_te.n
            p.s_te.wait(v)
            p.s_x[s].wait(v, ld_at[s])
            v.scalar_tensor_tensor(xin[s][:, 0:512], w["pb"][0][:, :],
                                   wa["bo_cols"][:, ot:ot + 1], xin[s][:, 0:512],
                                   op0=ALU.add, op1=ALU.add)
            ve_at[s] = p.s_ve.inc(v.scalar_tensor_tensor(
                xin[s][:, 512:1024], w["pb"][1][:, :], wa["bo_cols"][:, ot:ot + 1],
                xin[s][:, 512:1024], op0=ALU.add, op1=ALU.add))
            last_stt[0] = ve_at[s]
            p.s_ve.wait(sy, ve_at[s])
            st_at[s] = p.s_sto[s].inc(sy.dma_start(
                out=out_dram[ot * 128:(ot + 1) * 128, :], in_=xin[s][:]), 16)
    barrier(p)
    sctx.close()
    octx.close()


def emit_dump(p, dst_dram, src_sb, nct):
    barrier(p)
    for ct in range(nct):
        eng = p.gp if src_sb.dtype != F32 else p.sy
    for ct in range(nct):
        if src_sb.dtype != F32:
            p.s_gd.inc(p.gp.dma_start(out=dst_dram[ct * 128:(ct + 1) * 128, :],
                                      in_=src_sb[:, ct * T:(ct + 1) * T]), 16)
        else:
            p.s_z.inc(p.sy.dma_start(out=dst_dram[ct * 128:(ct + 1) * 128, :],
                                     in_=src_sb[:, ct * T:(ct + 1) * T]), 16)
    barrier(p)


def _s1_cols_layout():
    lay = {}
    off = 0

    def add(name, n):
        nonlocal off
        lay[name] = (off, n)
        off += n
    add("bin", 4)
    add("lnl_w", 4)
    add("lnl_b", 4)
    add("bout", 32)
    for l in range(AD):
        for nm, n in (("ln1_w", 4), ("ln1_b", 4), ("ln2_w", 4), ("ln2_b", 4),
                      ("phz_re", 4), ("phz_im", 4), ("lc_re", 4), ("lc_im", 4),
                      ("scl", 4), ("b2", 4), ("b1", 16)):
            add(f"{nm}{l}", n)
    return lay, off


S1_COLS, S1C_N = _s1_cols_layout()


def build_program(stages=("s1", "s2", "s3", "s4"), dbg_keys=()):
    nc = bass.Bass("TRN2", target_bir_lowering=False, debug=False,
                   detect_race_conditions=False)
    dram = {}

    def din(name, shape, dt=F32):
        dram[name] = nc.dram_tensor(name, shape, dt, kind="ExternalInput").ap()
        return dram[name]

    x_in = din("x_in", [D, T])
    out_fm = nc.dram_tensor("out_fm", [D, T], F32, kind="ExternalOutput").ap()
    x_res = nc.dram_tensor("x_res", [D, T], F32).ap()
    u_bounce = nc.dram_tensor("u_bounce", [D, T], BF16).ap()

    din("ffn_A", [64, 64])
    din("ffn_B", [64, 64])
    din("ffn_g_cols", [128, NKT])
    din("ffn_b_cols", [128, NKT])
    din("ffn_biasp_cols", [128, NKT])

    if "s3" in stages:
        for nm in ("wq_img", "wk_img", "wv_img", "wo_img"):
            din(nm, [NSB, 128, NKT * SBW])
        din("bq_cols", [128, NKT])
        din("bk_cols", [128, NKT])
        din("bo_cols", [128, NKT])
        din("bv_row", [1, D])
        din("kvb_cols", [128, S3_NBIAS])
        dram["kv_src"] = nc.dram_tensor("kv_src", [2 * D, T], BF16).ap()
        dram["kv_gth"] = nc.dram_tensor("kv_gth", [4, 2 * D, T], BF16).ap()
        dram["q_dram"] = nc.dram_tensor("q_dram", [D, T], BF16).ap()

    if "s1" in stages:
        din("win_img", [128, NKT * DIM])
        din("wout_img", [128, 4 * D])
        din("w1_img", [AD, 128, 4 * DFF])
        din("w2_img", [AD, 128, 16 * 512])
        din("s1_cols", [128, S1C_N])
        din("oh0", [1, 8])
        din("oh1", [1, 8])
        dram["z_src"] = nc.dram_tensor("z_src", [4, DIM], F32).ap()
        dram["z_gth"] = nc.dram_tensor("z_gth", [16, DIM], F32).ap()

    dbg = {}
    for k in dbg_keys:
        if k in ("scdbg", "scdbg2"):
            dbg[k] = nc.dram_tensor("dbg_" + k, [8 * 128, T], F32,
                                    kind="ExternalOutput").ap()
            continue
        nct = 4 if not k.startswith("x") else NKT
        dbg[k] = nc.dram_tensor("dbg_" + k, [nct * 128, T], F32,
                                kind="ExternalOutput").ap()

    with ExitStack() as ctx:
        p = P(nc, ctx)
        with nc.Block():
            w = {
                "xin2": [p.sbuf("xin0", [128, T], F32), p.sbuf("xin1", [128, T], F32)],
                "xbs2": [p.sbuf("xbs0", [128, T], BF16), p.sbuf("xbs1", [128, T], BF16)],
                "sqs2": [p.sbuf("sqs0", [128, T], BF16), p.sbuf("sqs1", [128, T], BF16)],
                "t1": p.sbuf("t1", [128, T], F32),
                "ones_bf": p.sbuf("ones_bf", [128, 1], BF16),
                "ones_row": p.sbuf("ones_row", [1, 128], F32),
                "r0": p.sbuf("r0", [1, T], F32),
                "r1": p.sbuf("r1", [1, T], F32),
                "r2": p.sbuf("r2", [1, T], F32),
                "r3": p.sbuf("r3", [1, T], F32),
                "rstd": p.sbuf("rstd", [1, T], F32),
                "sc_row": p.sbuf("sc_row", [1, T], F32),
                "mu_b": p.sbuf("mu_b", [128, T], F32),
                "sc_b": p.sbuf("sc_b", [128, T], F32),
                "sc2_b": p.sbuf("sc2_b", [128, T], F32),
                "pb": [p.psum(f"pb{i}", [128, 512]) for i in range(7)],
            }
            w["ones_f32"] = p.sbuf("ones_f32", [128, 1], F32)
            w["pbt"] = w["pb"][5]
            p.s_ve.inc(p.ve.memset(w["ones_bf"][:], 1.0))
            p.s_ve.inc(p.ve.memset(w["ones_f32"][:], 1.0))
            p.s_ve.inc(p.ve.memset(w["ones_row"][:], 1.0))

            ws = None
            if "s1" in stages:
                diag_i = p.sbuf("diag_i", [128, 128], I32)
                w["ident"] = p.sbuf("ident", [128, 128], F32)
                p.s_gp.inc(p.gp.iota(diag_i[:], pattern=[[-1, 128]], base=0,
                                     channel_multiplier=1))
                oh0r = p.sbuf("oh0r", [1, 8], F32)
                oh1r = p.sbuf("oh1r", [1, 8], F32)
                w["oh0_b"] = p.sbuf("oh0_b", [128, 8], F32)
                w["oh1_b"] = p.sbuf("oh1_b", [128, 8], F32)
                p.s_const.inc(p.gp.dma_start(out=oh0r[:], in_=dram["oh0"]), 16)
                p.s_const.inc(p.gp.dma_start(out=oh1r[:], in_=dram["oh1"]), 16)
                s1c = p.sbuf("s1c", [128, S1C_N], F32)
                p.s_const.inc(p.gp.dma_start(out=s1c[:], in_=dram["s1_cols"]), 16)

                def wsl(name):
                    o, n = S1_COLS[name]
                    return s1c[:, o:o + n]
                ws = {
                    "bin_cols": wsl("bin"), "lnl_w": wsl("lnl_w"),
                    "lnl_b": wsl("lnl_b"), "bout_cols": wsl("bout"),
                }
                for nm in ("ln1_w", "ln1_b", "ln2_w", "ln2_b", "phz_re",
                           "phz_im", "lc_re", "lc_im", "scl", "b2", "b1"):
                    key = nm + "_cols" if nm in ("b1", "b2") else nm
                    ws[key] = [wsl(f"{nm}{l}") for l in range(AD)]

            wf = {"u_dram": u_bounce}
            for nm in ("A_sb", "B_sb"):
                t = p.sbuf(nm, [128, 64], BF16)
                p.s_const.inc(p.gp.dma_start(out=t[0:64, :], in_=dram["ffn_" + nm[0]]), 16)
                p.s_const.inc(p.gp.dma_start(out=t[64:128, :], in_=dram["ffn_" + nm[0]]), 16)
                wf[nm] = t
            for nm in ("gcols", "bcols", "biasp_cols"):
                t = p.sbuf(nm, [128, NKT], F32)
                key = {"gcols": "ffn_g_cols", "bcols": "ffn_b_cols",
                       "biasp_cols": "ffn_biasp_cols"}[nm]
                p.s_const.inc(p.gp.dma_start(out=t[:], in_=dram[key]), 16)
                wf[nm] = t
            barrier(p)

            wa = None
            if "s3" in stages:
                wa = {}
                for nm in ("bq_cols", "bk_cols", "bo_cols", "kvb_cols"):
                    t = p.sbuf("wa_" + nm, [128, dram[nm].shape[1]], F32)
                    p.s_const.inc(p.gp.dma_start(out=t[:], in_=dram[nm]), 16)
                    wa[nm] = t
            if "s1" in stages:
                p.s_ve.inc(p.ve.tensor_single_scalar(w["ident"][:], diag_i[:],
                                                     0.0, ALU.is_equal))
                for ohr, ohb in ((oh0r, w["oh0_b"]), (oh1r, w["oh1_b"])):
                    p.s_ve.wait(p.te)
                    p.s_te.inc(p.te.matmul(w["pb"][4][:, 0:8], w["ones_row"][:],
                                           ohr[:], start=True, stop=True,
                                           skip_group_check=True))
                    p.s_te.wait(p.ve)
                    p.s_ve.inc(p.ve.tensor_copy(ohb[:], w["pb"][4][:, 0:8]))

            last = {}
            cur_x = x_in
            for si, st in enumerate(stages):
                is_last = si == len(stages) - 1
                dst = out_fm if is_last else x_res
                if st in ("s2", "s4"):
                    emit_ffn(p, cur_x, dst, wf, w, last)
                elif st == "s1":
                    emit_s1(p, cur_x, dst, ws, w, dram, dbg if dbg else None)
                elif st == "s3":
                    emit_s3(p, cur_x, dst, wa, w, dram, dbg if dbg else None)
                else:
                    raise NotImplementedError(st)
                dk = f"xafter{si}"
                if dbg and dk in dbg:
                    barrier(p)
                    for q8 in range(8):
                        p.s_z.inc(p.sy.dma_start(
                            out=dbg[dk][q8 * 512:(q8 + 1) * 512, :],
                            in_=dst[q8 * 512:(q8 + 1) * 512, :]), 16)
                    barrier(p)
                cur_x = x_res

            sync_all(p, p.sy)
    return nc


def tok_ids(core):
    j = core % 4
    return np.r_[j * CH:(j + 1) * CH, (7 - j) * CH:(8 - j) * CH]


def host_inputs(inputs):
    x = inputs["L0_attn_a_x"]
    A = np.ascontiguousarray(np.asarray(inputs["L0_ffn_A"], f32))
    Bm = np.ascontiguousarray(np.asarray(inputs["L0_ffn_B"], f32))
    bias = np.asarray(inputs["L0_ffn_bias"], f32)
    g = np.asarray(inputs["L0_ffn_ln_gamma"], f32)
    be = np.asarray(inputs["L0_ffn_ln_beta"], f32)

    def cols(vec):
        return np.ascontiguousarray(vec.reshape(NKT, 128).T)

    biasp = np.ascontiguousarray(bias.reshape(64, 64).T).reshape(D)
    shared = {
        "ffn_A": A, "ffn_B": Bm,
        "ffn_g_cols": cols(g), "ffn_b_cols": cols(be),
        "ffn_biasp_cols": cols(biasp),
    }

    def colv(vec, n):
        return np.asarray(vec, f32).reshape(n, 128).T

    aa = {k[len("L0_attn_a_"):]: np.asarray(v, f32) for k, v in inputs.items()
          if k.startswith("L0_attn_a_") and k != "L0_attn_a_x"}
    shared["win_img"] = np.ascontiguousarray(
        aa["Win"].T.reshape(NKT, 128, DIM).transpose(1, 0, 2).reshape(128, NKT * DIM))
    shared["wout_img"] = np.ascontiguousarray(
        aa["Wout"].T.reshape(4, 128, D).transpose(1, 0, 2).reshape(128, 4 * D))
    shared["w1_img"] = np.ascontiguousarray(np.stack([
        aa["W1"][l].T.reshape(4, 128, DFF).transpose(1, 0, 2).reshape(128, 4 * DFF)
        for l in range(AD)]))
    shared["w2_img"] = np.ascontiguousarray(np.stack([
        aa["W2"][l].T.reshape(16, 128, 512).transpose(1, 0, 2).reshape(128, 16 * 512)
        for l in range(AD)]))
    ab = {k[len("L0_attn_b_"):]: np.asarray(v, f32) for k, v in inputs.items()
          if k.startswith("L0_attn_b_")}

    def wimg(m):
        return np.ascontiguousarray(
            m.T.reshape(NKT, 128, NSB, SBW).transpose(2, 1, 0, 3)
            .reshape(NSB, 128, NKT * SBW))
    shared["wq_img"] = wimg(ab["Wq"])
    shared["wk_img"] = wimg(ab["Wk"])
    shared["wv_img"] = wimg(ab["Wv"])
    shared["wo_img"] = wimg(ab["Wo"])
    shared["bq_cols"] = cols(ab["bq"])
    shared["bk_cols"] = cols(ab["bk"])
    shared["bo_cols"] = cols(ab["bo"])
    shared["bv_row"] = np.ascontiguousarray(ab["bv"].reshape(1, D))

    s1c = np.zeros((128, S1C_N), f32)

    def put(name, vec, n):
        o, k = S1_COLS[name]
        assert k == n
        s1c[:, o:o + n] = colv(vec, n)
    put("bin", aa["bin"], 4)
    put("lnl_w", aa["lnl_w"], 4)
    put("lnl_b", aa["lnl_b"], 4)
    put("bout", aa["bout"], 32)
    for l in range(AD):
        for nm, n in (("ln1_w", 4), ("ln1_b", 4), ("ln2_w", 4), ("ln2_b", 4),
                      ("phz_re", 4), ("phz_im", 4), ("lc_re", 4), ("lc_im", 4),
                      ("sc_lin", 4), ("b2", 4), ("b1", 16)):
            key = "scl" if nm == "sc_lin" else nm
            put(f"{key}{l}", aa[nm][l], n)
    shared["s1_cols"] = s1c

    maps = []
    for c in range(NCORES):
        b = c // 4
        j = c % 4
        xs = np.ascontiguousarray(np.asarray(x[b], dtype=f32)[tok_ids(c), :].T)
        m = dict(shared)
        m["x_in"] = xs
        oh0 = np.zeros((1, 8), f32)
        oh0[0, j] = 1.0
        oh1 = np.zeros((1, 8), f32)
        oh1[0, 7 - j] = 1.0
        m["oh0"] = oh0
        m["oh1"] = oh1
        kvb = np.full((128, S3_NBIAS), -1e30, f32)
        bi = 0
        for sl in (0, 1):
            n_off = 4 if sl == 0 else 7
            gq = j if sl == 0 else 7 - j
            for ch in range(n_off):
                kvb[:, bi] = -12.0 if ch < gq else -1e30
                bi += 1
            kvb[:, bi] = -12.0
            bi += 1
        m["kvb_cols"] = kvb
        maps.append(m)
    return maps


def kernel(**inputs):
    maps = host_inputs(inputs)
    nc = build_program()
    res = run_bass_kernel_spmd(nc, maps, core_ids=list(range(NCORES)))
    out = np.empty((NB, L, D), np.float32)
    for c in range(NCORES):
        out[c // 4, tok_ids(c), :] = res.results[c]["out_fm"].T
    return out
